# version 17
# baseline (speedup 1.0000x reference)
import numpy as np
from contextlib import ExitStack
import concourse.bass as bass
import concourse.mybir as mybir
from concourse.bass_utils import run_bass_kernel_spmd

F32 = mybir.dt.float32
BF16 = mybir.dt.bfloat16
AF = mybir.ActivationFunctionType
ALU = mybir.AluOpType
AX = mybir.AxisListType

T = 4096
D = 1024
KO = 8
NT = 32
NW = 8
TPW = 4
EPS_OUT = 1e-5
S_ATT = 1.0 / np.sqrt(128.0)


class Ctx:

    def __init__(self, nc, es):
        self.nc = nc
        self.es = es
        self.block = es.enter_context(nc.Block())
        self.sem = {}
        self.cnt = {}
        for n in ["dl", "dg", "ds", "da", "pe", "dv", "ac", "po"]:
            self.sem[n] = es.enter_context(nc.semaphore(n))
            self.cnt[n] = 0

    def inc(self, inst, name, k=1):
        inst.then_inc(self.sem[name], k)
        self.cnt[name] += k
        return self.cnt[name]

    def w(self, eng, name, v=None):
        vv = self.cnt[name] if v is None else v
        if vv > 0:
            eng.wait_ge(self.sem[name], vv)

    def S(self, f): self.block.sync(f)
    def G(self, f): self.block.gpsimd(f)
    def TE(self, f): self.block.tensor(f)
    def V(self, f): self.block.vector(f)
    def A(self, f): self.block.scalar(f)


class Consts:

    def __init__(self, ctx):
        nc, es = ctx.nc, ctx.es
        self.ident_bf = es.enter_context(nc.sbuf_tensor("ident_bf", [128, 128], BF16))
        self.ident_f = es.enter_context(nc.sbuf_tensor("ident_f", [128, 128], F32))
        self.triu_bf = es.enter_context(nc.sbuf_tensor("triu_bf", [128, 128], BF16))
        self.triu_f = es.enter_context(nc.sbuf_tensor("triu_f", [128, 128], F32))
        self.ones128_f = es.enter_context(nc.sbuf_tensor("ones128_f", [128, 128], F32))
        self.ones_col_bf = es.enter_context(nc.sbuf_tensor("ones_col_bf", [128, 1], BF16))
        self.ones_col_f = es.enter_context(nc.sbuf_tensor("ones_col_f", [128, 1], F32))
        self.ones_row_bf = es.enter_context(nc.sbuf_tensor("ones_row_bf", [1, 128], BF16))
        self.negones_row_f = es.enter_context(nc.sbuf_tensor("negones_row_f", [1, 128], F32))
        self.ones_row_f = es.enter_context(nc.sbuf_tensor("ones_row_f", [1, 128], F32))
        self.eps5_col = es.enter_context(nc.sbuf_tensor("eps5_col", [128, 1], F32))
        self.eps6_col = es.enter_context(nc.sbuf_tensor("eps6_col", [128, 1], F32))
        self.lnS_col = es.enter_context(nc.sbuf_tensor("lnS_col", [128, 1], F32))
        ident_bf, ident_f, triu_bf, triu_f = (
            self.ident_bf, self.ident_f, self.triu_bf, self.triu_f)
        ones128_f, ones_col_bf, ones_col_f, ones_row_bf, negones_row_f, ones_row_f = (
            self.ones128_f, self.ones_col_bf, self.ones_col_f, self.ones_row_bf,
            self.negones_row_f, self.ones_row_f)

        @ctx.G
        def _(g):
            for idt in (ident_bf, ident_f):
                g.memset(idt[:], 1.0)
                g.affine_select(out=idt[:], in_=idt[:], compare_op=ALU.is_equal,
                                fill=0.0, base=0, pattern=[[-1, 128]], channel_multiplier=1)
            for tri in (triu_bf, triu_f):
                g.memset(tri[:], 1.0)
                g.affine_select(out=tri[:], in_=tri[:], compare_op=ALU.is_ge,
                                fill=0.0, base=0, pattern=[[1, 128]], channel_multiplier=-1)
            g.memset(ones128_f[:], 1.0)
            g.memset(ones_col_bf[:], 1.0)
            g.memset(ones_col_f[:], 1.0)
            g.memset(ones_row_bf[:], 1.0)
            g.memset(negones_row_f[:], -1.0)
            g.memset(self.eps5_col[:], 1e-5)
            g.memset(self.eps6_col[:], 1e-6)
            g.memset(self.lnS_col[:], float(np.log(S_ATT)))
            last = g.memset(ones_row_f[:], 1.0)
            ctx.inc(last, "po")
        self.ready = ctx.cnt["po"]


def ln_stats(ctx, bn12, mr, src_a, src_b, eps_col):

    @ctx.V
    def _(v):
        v.bn_stats(bn12[:, 0:6], src_a)
        ctx.inc(v.bn_stats(bn12[:, 6:12], src_b), "dv")
        ctx.w(v, "dv")
        ctx.inc(v.bn_aggr(mr[:, 0:2], bn12[:]), "dv")
    dvv = ctx.cnt["dv"]

    @ctx.A
    def _(a):
        ctx.w(a, "dv", dvv)
        ctx.inc(a.activation(mr[:, 3:4], mr[:, 1:2], AF.Ln, bias=eps_col), "ac")
        ctx.w(a, "ac")
        ctx.inc(a.activation(mr[:, 2:3], mr[:, 3:4], AF.Exp, scale=-0.5), "ac")
    return ctx.cnt["dv"], ctx.cnt["ac"]


def emit_l0_attn(ctx, cst, xin, xout, W, between=None):
    nc, es = ctx.nc, ctx.es
    ps = ExitStack()
    wq_sb = ps.enter_context(nc.sbuf_tensor("a_wq", [128, KO, 1024], BF16))
    wkv_sb = ps.enter_context(nc.sbuf_tensor("a_wkv", [128, KO, 4, 32], BF16))
    upk_sb = ps.enter_context(nc.sbuf_tensor("a_upk", [32, 128], BF16))
    upv_sb = ps.enter_context(nc.sbuf_tensor("a_upv", [32, 128], BF16))
    wo_sb = ps.enter_context(nc.sbuf_tensor("a_wo", [128, KO, 1024], BF16))
    xt = ps.enter_context(nc.sbuf_tensor("a_xt", [128, 2, TPW, 1024], F32))
    bn12 = ps.enter_context(nc.sbuf_tensor("a_bn12", [128, TPW, 12], F32))
    mr = ps.enter_context(nc.sbuf_tensor("a_mr", [128, TPW, 4], F32))
    xn_tm = ps.enter_context(nc.sbuf_tensor("a_xntm", [128, TPW, 1024], BF16))
    xn_fm = ps.enter_context(nc.sbuf_tensor("a_xnfm", [128, KO, 512], BF16))
    q_sb = ps.enter_context(nc.sbuf_tensor("a_q", [128, 8, 512], BF16))
    klat_sb = [ps.enter_context(nc.sbuf_tensor(f"a_klat{h}", [32, T], BF16)) for h in range(2)]
    vlat_sb = [ps.enter_context(nc.sbuf_tensor(f"a_vlat{h}", [32, T], BF16)) for h in range(2)]
    k_sb = ps.enter_context(nc.sbuf_tensor("a_k", [128, 2, T], BF16))
    v_sb = ps.enter_context(nc.sbuf_tensor("a_v", [128, NT, 256], BF16))
    sq_sb = ps.enter_context(nc.sbuf_tensor("a_sq", [128, 512], BF16))
    nrm = ps.enter_context(nc.sbuf_tensor("a_nrm", [1, 32], F32))
    negM = ps.enter_context(nc.sbuf_tensor("a_negM", [128, 8], F32))
    PT = ps.enter_context(nc.sbuf_tensor("a_PT", [128, 2, 2, 512], BF16))
    l_sb = ps.enter_context(nc.sbuf_tensor("a_l", [1, 512], F32))
    rl_sb = ps.enter_context(nc.sbuf_tensor("a_rl", [1, 512], F32))
    rlb_sb = ps.enter_context(nc.sbuf_tensor("a_rlb", [128, 512], F32))
    y_sb = ps.enter_context(nc.sbuf_tensor("a_y", [128, 8, 512], BF16))
    u_sb = ps.enter_context(nc.sbuf_tensor("a_u", [128, 1024], F32))
    xnew = ps.enter_context(nc.sbuf_tensor("a_xnew", [128, 2, 1024], F32))
    A_ps = ps.enter_context(nc.psum_tensor("a_A", [128, 4, 512], F32))
    B_ps = ps.enter_context(nc.psum_tensor("a_B", [128, 8, 128], BF16))
    y_ps = ps.enter_context(nc.psum_tensor("a_yps", [128, 512], F32))
    l_ps = ps.enter_context(nc.psum_tensor("a_lps", [128, 512], F32))
    r_ps = ps.enter_context(nc.psum_tensor("a_rps", [128, 512], F32))

    @ctx.G
    def _(g):
        for ko in range(KO):
            g.dma_start(out=wq_sb[:, ko, :], in_=W["wq"][ko * 128:(ko + 1) * 128, :]).then_inc(ctx.sem["dg"], 16)
            ctx.cnt["dg"] += 16
            g.dma_start(out=wkv_sb[:, ko, 0, :],
                        in_=W["wk_down"][ko * 128:(ko + 1) * 128, 0:32]).then_inc(ctx.sem["dg"], 16)
            g.dma_start(out=wkv_sb[:, ko, 1, :],
                        in_=W["wv_down"][ko * 128:(ko + 1) * 128, 0:32]).then_inc(ctx.sem["dg"], 16)
            g.dma_start(out=wkv_sb[:, ko, 2, :],
                        in_=W["wk_down"][ko * 128:(ko + 1) * 128, 32:64]).then_inc(ctx.sem["dg"], 16)
            g.dma_start(out=wkv_sb[:, ko, 3, :],
                        in_=W["wv_down"][ko * 128:(ko + 1) * 128, 32:64]).then_inc(ctx.sem["dg"], 16)
            ctx.cnt["dg"] += 64
        g.dma_start(out=upk_sb[:], in_=W["wk_up"][:]).then_inc(ctx.sem["dg"], 16)
        g.dma_start(out=upv_sb[:], in_=W["wv_up"][:]).then_inc(ctx.sem["dg"], 16)
        ctx.cnt["dg"] += 32
        for ko in range(KO):
            g.dma_start(out=wo_sb[:, ko, :], in_=W["wo"][ko * 128:(ko + 1) * 128, :]).then_inc(ctx.sem["dg"], 16)
            ctx.cnt["dg"] += 16
    wts_loaded = ctx.cnt["dg"]

    prev_resid = [[0] * TPW, [0] * TPW]
    prev_store = [0, 0]
    prev_vevac = 0
    prev_yevac = 0
    wo_u2 = [0, 0]
    prev_expc = 0
    prev_avpe = 0
    prev_tr_evac = [0]
    xnfm_mark = {}

    def s1_tile(c, i):
        t = 4 * c + i
        par = c % 2

        @ctx.S
        def _(s, t=t, par=par, i=i, pr=prev_resid[par][i]):
            ctx.w(s, "dv", pr)
            ctx.inc(s.dma_start(out=xt[:, par, i, :], in_=xin[t * 128:(t + 1) * 128, :]), "dl", 16)
        dl_here = ctx.cnt["dl"]

        @ctx.V
        def _(v, dl_here=dl_here):
            ctx.w(v, "dl", dl_here)
        ln_stats(ctx, bn12[:, i, :], mr[:, i, :], xt[:, par, i, 0:512],
                 xt[:, par, i, 512:1024], cst.eps5_col[:])
        ac_rstd = ctx.cnt["ac"]

        @ctx.V
        def _(v, par=par, i=i, ac_rstd=ac_rstd):
            ctx.w(v, "ac", ac_rstd)
            ctx.inc(v.tensor_scalar(xn_tm[:, i, :], xt[:, par, i, :], mr[:, i, 0:1], mr[:, i, 2:3],
                                    ALU.subtract, ALU.mult), "dv")
        apply_dv = ctx.cnt["dv"]

        @ctx.TE
        def _(te, i=i, apply_dv=apply_dv, pt=prev_tr_evac[0]):
            ctx.w(te, "dv", apply_dv)
            ctx.w(te, "dv", pt)
            ctx.w(te, "po", cst.ready)
            for ko in range(KO):
                ins = te.transpose(B_ps[:, ko, :], xn_tm[:, i, ko * 128:(ko + 1) * 128], cst.ident_bf[:])
            ctx.inc(ins, "pe")
        tr_pe = ctx.cnt["pe"]

        @ctx.V
        def _(v, i=i, tr_pe=tr_pe):
            ctx.w(v, "pe", tr_pe)
            ctx.inc(v.tensor_copy(xn_fm[:, :, i * 128:(i + 1) * 128], B_ps[:]), "dv")
        prev_tr_evac[0] = ctx.cnt["dv"]
        xnfm_mark[c] = ctx.cnt["dv"]

    HOIST = True
    for i in range(TPW):
        s1_tile(0, i)

    for c in range(NW):
        if between is not None and c > 0:
            between(c)
        if not HOIST and c > 0:
            for i in range(TPW):
                s1_tile(c, i)
        xnfm_ready = xnfm_mark[c]

        for mo in range(8):
            bank = mo % 2

            @ctx.TE
            def _(te, mo=mo, bank=bank, dv_now=ctx.cnt["dv"]):
                ctx.w(te, "dg", wts_loaded)
                ctx.w(te, "dv", dv_now)
                ctx.w(te, "ac")
                for ko in range(KO):
                    ins = te.matmul(A_ps[:, bank, :], wq_sb[:, ko, mo * 128:(mo + 1) * 128],
                                    xn_fm[:, ko, :], start=(ko == 0), stop=(ko == KO - 1))
                ctx.inc(ins, "pe")
            q_pe = ctx.cnt["pe"]

            @ctx.A
            def _(a, mo=mo, bank=bank, q_pe=q_pe):
                ctx.w(a, "pe", q_pe)
                ctx.w(a, "dv")
                a.activation(q_sb[:, mo, :], A_ps[:, bank, :], AF.Copy)
                ctx.inc(a.activation(sq_sb[:], A_ps[:, bank, :], AF.Square), "ac")
            sq_ac = ctx.cnt["ac"]

            @ctx.TE
            def _(te, sq_ac=sq_ac):
                ctx.w(te, "ac", sq_ac)
                ctx.w(te, "dv")
                ctx.inc(te.matmul(l_ps[0:1, :], cst.ones_col_bf[:], sq_sb[:],
                                  start=True, stop=True), "pe")
            qn_pe = ctx.cnt["pe"]

            @ctx.V
            def _(v, mo=mo, qn_pe=qn_pe):
                ctx.w(v, "pe", qn_pe)
                ctx.inc(v.reduce_max(nrm[0:1, mo:mo + 1], l_ps[0:1, :], axis=AX.X), "dv")

        kvlat_out = [A_ps[0:32, 2, :], A_ps[32:64, 2, :], A_ps[64:96, 2, :], A_ps[0:32, 3, :]]
        kvlat_dst = [klat_sb[0], vlat_sb[0], klat_sb[1], vlat_sb[1]]

        @ctx.TE
        def _(te):
            ctx.w(te, "ac")
            ctx.w(te, "dv")
            for gg in range(4):
                for ko in range(KO):
                    ins = te.matmul(kvlat_out[gg], wkv_sb[:, ko, gg, :], xn_fm[:, ko, :],
                                    start=(ko == 0), stop=(ko == KO - 1))
            ctx.inc(ins, "pe")
        kv_pe = ctx.cnt["pe"]

        @ctx.A
        def _(a, c=c, kv_pe=kv_pe):
            ctx.w(a, "pe", kv_pe)
            for gg in range(4):
                ins = a.activation(kvlat_dst[gg][:, c * 512:(c + 1) * 512], kvlat_out[gg], AF.Copy)
            ctx.inc(ins, "ac")
        kvevac_ac = ctx.cnt["ac"]

        for h in range(2):
            @ctx.TE
            def _(te, h=h, c=c, kvevac_ac=kvevac_ac):
                ctx.w(te, "ac", kvevac_ac)
                ctx.w(te, "dv")
                ctx.inc(te.matmul(A_ps[:, h, :], upk_sb[:],
                                  klat_sb[h][:, c * 512:(c + 1) * 512],
                                  start=True, stop=True), "pe")
            kup_pe = ctx.cnt["pe"]

            @ctx.A
            def _(a, h=h, c=c, kup_pe=kup_pe):
                ctx.w(a, "pe", kup_pe)
                ctx.w(a, "dv")
                a.activation(k_sb[:, h, c * 512:(c + 1) * 512], A_ps[:, h, :], AF.Copy)
                ctx.inc(a.activation(sq_sb[:], A_ps[:, h, :], AF.Square), "ac")
            ks_ac = ctx.cnt["ac"]

            @ctx.TE
            def _(te, ks_ac=ks_ac):
                ctx.w(te, "ac", ks_ac)
                ctx.w(te, "dv")
                ctx.inc(te.matmul(l_ps[0:1, :], cst.ones_col_bf[:], sq_sb[:],
                                  start=True, stop=True), "pe")
            kn_pe = ctx.cnt["pe"]

            @ctx.V
            def _(v, h=h, c=c, kn_pe=kn_pe):
                ctx.w(v, "pe", kn_pe)
                ctx.inc(v.reduce_max(nrm[0:1, 12 + h:13 + h], l_ps[0:1, :], axis=AX.X), "dv")
                ctx.w(v, "dv")
                if c == 0:
                    ctx.inc(v.tensor_copy(nrm[0:1, 8 + h:9 + h], nrm[0:1, 12 + h:13 + h]), "dv")
                else:
                    ctx.inc(v.tensor_tensor(nrm[0:1, 8 + h:9 + h], nrm[0:1, 8 + h:9 + h],
                                            nrm[0:1, 12 + h:13 + h], ALU.max), "dv")

        for i in range(TPW):
            t = 4 * c + i
            for h in range(2):
                slot = (2 * i + h) % 3
                off = 128 + slot * 128

                @ctx.TE
                def _(te, t=t, h=h, off=off, pv=prev_vevac, kvevac_ac=kvevac_ac):
                    ctx.w(te, "dv", pv)
                    ctx.w(te, "ac", kvevac_ac)
                    ctx.inc(te.matmul(A_ps[:, 3, off:off + 128],
                                      vlat_sb[h][:, t * 128:(t + 1) * 128],
                                      upv_sb[:], start=True, stop=True), "pe")
                vt_pe = ctx.cnt["pe"]

                @ctx.V
                def _(v, t=t, h=h, off=off, vt_pe=vt_pe):
                    ctx.w(v, "pe", vt_pe)
                    ctx.inc(v.tensor_copy(v_sb[:, t, h * 128:(h + 1) * 128],
                                          A_ps[:, 3, off:off + 128]), "dv")
                prev_vevac = ctx.cnt["dv"]

        for h in range(8):
            kvh = h // 4

            @ctx.V
            def _(v, h=h, kvh=kvh):
                ctx.w(v, "dv")
                ctx.inc(v.tensor_tensor(nrm[0:1, 16 + h:17 + h], nrm[0:1, h:h + 1],
                                        nrm[0:1, 8 + kvh:9 + kvh], ALU.mult), "dv")
            pr_dv = ctx.cnt["dv"]

            @ctx.A
            def _(a, h=h, pr_dv=pr_dv):
                ctx.w(a, "dv", pr_dv)
                ctx.inc(a.activation(nrm[0:1, 16 + h:17 + h],
                                     nrm[0:1, 16 + h:17 + h], AF.Ln), "ac")
                ctx.w(a, "ac")
                ctx.inc(a.activation(nrm[0:1, 24 + h:25 + h], nrm[0:1, 16 + h:17 + h],
                                     AF.Exp, scale=0.5, bias=cst.lnS_col[0:1, :]), "ac")
            m_ac = ctx.cnt["ac"]

            @ctx.TE
            def _(te, h=h, m_ac=m_ac):
                ctx.w(te, "ac", m_ac)
                ctx.w(te, "dv")
                ctx.inc(te.matmul(l_ps[:, 0:1], cst.negones_row_f[:], nrm[0:1, 24 + h:25 + h],
                                  start=True, stop=True), "pe")
            nb_pe = ctx.cnt["pe"]

            @ctx.V
            def _(v, h=h, nb_pe=nb_pe):
                ctx.w(v, "pe", nb_pe)
                ctx.inc(v.tensor_copy(negM[:, h:h + 1], l_ps[:, 0:1]), "dv")
        negM_dv = ctx.cnt["dv"]

        for h in range(8):
            kvh = h // 4
            ng2 = 2 * (c + 1)
            ac0 = ctx.cnt["ac"]

            mask_marks = {}
            for g2 in range(ng2 - 2, ng2):
                @ctx.V
                def _(v, c=c, g2=g2, ac0=ac0):
                    v.wait_ge(ctx.sem["ac"], ac0 + g2 + 1)
                    for jj in range(2):
                        j = g2 * 2 + jj
                        r = j - 4 * c
                        if r > 0:
                            v.memset(PT[:, g2 % 2, jj, 0:r * 128], 0.0)
                        ins = v.tensor_tensor(PT[:, g2 % 2, jj, r * 128:(r + 1) * 128],
                                              PT[:, g2 % 2, jj, r * 128:(r + 1) * 128],
                                              cst.triu_bf[:], ALU.mult)
                    ctx.inc(ins, "dv")
                mask_marks[g2] = ctx.cnt["dv"]

            st_pe = {}
            avp = {}

            @ctx.TE
            def _(te, h=h, c=c, kvh=kvh, ng2=ng2, ac0=ac0, mask_marks=dict(mask_marks),
                  yfree=prev_yevac, expq=prev_expc):
                njt = 2 * ng2

                def avpair(g2):
                    te.wait_ge(ctx.sem["ac"], ac0 + g2 + 1)
                    if g2 in mask_marks:
                        te.wait_ge(ctx.sem["dv"], mask_marks[g2])
                    if g2 == 0:
                        ctx.w(te, "dv", yfree)
                    for jj in range(2):
                        j = g2 * 2 + jj
                        te.matmul(y_ps[:], v_sb[:, j, kvh * 128:(kvh + 1) * 128],
                                  PT[:, g2 % 2, jj, :], start=(j == 0), stop=(j == njt - 1))
                        ins = te.matmul(l_ps[0:1, :], cst.ones_col_bf[:],
                                        PT[:, g2 % 2, jj, :], start=(j == 0), stop=(j == njt - 1))
                    ctx.inc(ins, "pe")
                    avp[g2] = ctx.cnt["pe"]

                for g2 in range(ng2):
                    if g2 >= 2:
                        te.wait_ge(ctx.sem["ac"], ac0 + g2 - 1)
                    elif expq:
                        te.wait_ge(ctx.sem["ac"], expq)
                    for jj in range(2):
                        j = g2 * 2 + jj
                        ins = te.matmul(A_ps[:, (g2 % 2) * 2 + jj, :],
                                        k_sb[:, kvh, j * 128:(j + 1) * 128],
                                        q_sb[:, h, :], start=True, stop=True)
                    ctx.inc(ins, "pe")
                    st_pe[g2] = ctx.cnt["pe"]
                    if g2 >= 1:
                        avpair(g2 - 1)
                avpair(ng2 - 1)

            @ctx.A
            def _(a, h=h, ng2=ng2, ac0=ac0, st_pe=dict(st_pe), avp=dict(avp),
                  negM_dv=negM_dv, pav=prev_avpe):
                ctx.w(a, "dv", negM_dv)
                for g2 in range(ng2):
                    a.wait_ge(ctx.sem["pe"], st_pe[g2])
                    if g2 >= 2:
                        a.wait_ge(ctx.sem["pe"], avp[g2 - 2])
                    elif pav:
                        a.wait_ge(ctx.sem["pe"], pav)
                    sl = g2 % 2
                    ctx.inc(a.activation(PT[:, sl].rearrange("p a b -> p (a b)"),
                                         A_ps[:, sl * 2:(sl + 1) * 2].rearrange("p a b -> p (a b)"),
                                         AF.Exp, bias=negM[:, h:h + 1], scale=float(S_ATT)), "ac")
            assert ctx.cnt["ac"] == ac0 + ng2
            prev_expc = ctx.cnt["ac"]
            prev_avpe = avp[ng2 - 1]
            av_pe = prev_avpe

            @ctx.V
            def _(v, h=h, av_pe=av_pe):
                ctx.w(v, "pe", av_pe)
                ctx.inc(v.tensor_copy(l_sb[:], l_ps[0:1, :]), "dv")
                ctx.w(v, "dv")
                ctx.inc(v.reciprocal(rl_sb[:], l_sb[:]), "dv")
            rl_dv = ctx.cnt["dv"]

            @ctx.TE
            def _(te, rl_dv=rl_dv):
                ctx.w(te, "dv", rl_dv)
                ctx.inc(te.matmul(r_ps[:], cst.ones_row_f[:], rl_sb[:], start=True, stop=True), "pe")
            rb_pe = ctx.cnt["pe"]

            @ctx.V
            def _(v, h=h, rb_pe=rb_pe):
                ctx.w(v, "pe", rb_pe)
                ctx.inc(v.tensor_copy(rlb_sb[:], r_ps[:]), "dv")
                ctx.w(v, "dv")
                ctx.inc(v.tensor_tensor(y_sb[:, h, :], y_ps[:], rlb_sb[:], ALU.mult), "dv")
            prev_yevac = ctx.cnt["dv"]
            if HOIST and c + 1 < NW and 3 <= h <= 2 + TPW:
                s1_tile(c + 1, h - 3)
        y_ready = ctx.cnt["dv"]
        for i in range(TPW):
            t = 4 * c + i
            bp = (i % 2) * 2

            @ctx.TE
            def _(te, i=i, bp=bp, y_ready=y_ready, expc=prev_expc, wou=wo_u2[i % 2]):
                ctx.w(te, "dv", y_ready)
                ctx.w(te, "ac", expc)
                ctx.w(te, "dv", wou)
                for n in range(2):
                    for h in range(8):
                        ins = te.matmul(A_ps[:, bp + n, :], y_sb[:, h, i * 128:(i + 1) * 128],
                                        wo_sb[:, h, n * 512:(n + 1) * 512],
                                        start=(h == 0), stop=(h == 7))
                ctx.inc(ins, "pe")
            wo_pe = ctx.cnt["pe"]

            @ctx.V
            def _(v, wo_pe=wo_pe):
                ctx.w(v, "pe", wo_pe)
            _, _ = ln_stats(ctx, bn12[:, i, :], mr[:, i, :],
                            A_ps[:, bp, :], A_ps[:, bp + 1, :], cst.eps5_col[:])
            ac_rstd = ctx.cnt["ac"]

            @ctx.V
            def _(v, i=i, t=t, bp=bp, par=c % 2, ac_rstd=ac_rstd, slot=i % 2,
                  pstore=prev_store[i % 2]):
                ctx.w(v, "ac", ac_rstd)
                ctx.inc(v.tensor_scalar(u_sb[:], A_ps[:, bp:bp + 2, :].rearrange("p a b -> p (a b)"),
                                        mr[:, i, 0:1], mr[:, i, 2:3], ALU.subtract, ALU.mult), "dv")
                ctx.w(v, "dv")
                ctx.w(v, "ds", pstore)
                ctx.inc(v.tensor_add(xnew[:, slot, :], u_sb[:], xt[:, par, i, :]), "dv")
            wo_u2[i % 2] = ctx.cnt["dv"]
            prev_resid[c % 2][i] = ctx.cnt["dv"]
            res_dv = ctx.cnt["dv"]

            @ctx.S
            def _(s, t=t, slot=i % 2, res_dv=res_dv):
                ctx.w(s, "dv", res_dv)
                ctx.inc(s.dma_start(out=xout[t * 128:(t + 1) * 128, :], in_=xnew[:, slot, :]), "ds", 16)
            prev_store[i % 2] = ctx.cnt["ds"]

    return ps


def emit_precast(ctx, pairs):
    nc = ctx.nc
    ps = ExitStack()
    scr = ps.enter_context(nc.sbuf_tensor("pc_scr", [128, 4, 2048], BF16))
    slot_store = [0, 0, 0, 0]
    work = []
    for src, dst, mode in pairs:
        R, C = src.shape
        assert R % 128 == 0 and C <= 2048
        dstv = None
        if mode == "pairtile":
            assert R % 1024 == 0 and C == 2048
            dstv = dst.rearrange("(e p) c -> e (p c)", p=1024)
        for r in range(R // 128):
            work.append((src, dst, dstv, mode, r, C))
    pos = [0]

    def emit_some(k):
        end = min(pos[0] + k, len(work))
        while pos[0] < end:
            gi = pos[0]
            pos[0] += 1
            src, dst, dstv, mode, r, C = work[gi]
            slot = gi % 4

            @ctx.G
            def _(g, src=src, r=r, slot=slot, C=C, pst=slot_store[slot]):
                if pst:
                    g.wait_ge(ctx.sem["ds"], pst)
                g.dma_start(out=scr[:, slot, 0:C],
                            in_=src[r * 128:(r + 1) * 128, :]).then_inc(ctx.sem["dg"], 16)
                ctx.cnt["dg"] += 16
            ld = ctx.cnt["dg"]

            if mode == "flat":
                @ctx.S
                def _(s, dst=dst, r=r, slot=slot, C=C, ld=ld):
                    s.wait_ge(ctx.sem["dg"], ld)
                    ctx.inc(s.dma_start(out=dst[r * 128:(r + 1) * 128, :],
                                        in_=scr[:, slot, 0:C]), "ds", 16)
                slot_store[slot] = ctx.cnt["ds"]
            else:
                e, ko = r // 8, r % 8
                @ctx.S
                def _(s, e=e, ko=ko, slot=slot, ld=ld, dstv=dstv):
                    s.wait_ge(ctx.sem["dg"], ld)
                    out_ap = dstv[e:e + 1, :].rearrange(
                        "a (pair p k f) -> (a k) p pair f", pair=8, p=128, k=8, f=256)[ko]
                    in_ap = scr[:, slot, :].rearrange("p (pair f) -> p pair f", f=256)
                    ctx.inc(s.dma_start(out=out_ap, in_=in_ap), "ds", 16)
                slot_store[slot] = ctx.cnt["ds"]
    return ps, emit_some, len(work)


def emit_l0_ffn(ctx, cst, xin, xout, W):
    nc = ctx.nc
    NG, GT, TPG, NF, NE = 8, 512, 4, 16, 8
    ps = ExitStack()
    gate_sb = ps.enter_context(nc.sbuf_tensor("f_gate", [128, KO, 8], BF16))
    lnb_sb = ps.enter_context(nc.sbuf_tensor("f_lnb", [8, 1024], BF16))
    rows4 = ps.enter_context(nc.sbuf_tensor("f_rows4", [1, 4, 1024], BF16))
    b1t_sb = ps.enter_context(nc.sbuf_tensor("f_b1t", [16, 2, 128], F32))
    b1c_sb = ps.enter_context(nc.sbuf_tensor("f_b1c", [128, 2, 16], F32))

    w1_sb = ps.enter_context(nc.sbuf_tensor("f_w1", [128, 4, KO, 256], BF16))
    w2_sb = ps.enter_context(nc.sbuf_tensor("f_w2", [128, 2, NF, 1024], BF16))
    h_sb = ps.enter_context(nc.sbuf_tensor("f_h", [128, NF, GT], BF16))
    xt = ps.enter_context(nc.sbuf_tensor("f_xt", [128, 2, 1024], F32))
    xo = ps.enter_context(nc.sbuf_tensor("f_xo", [128, 1, 1024], F32))
    xnew = ps.enter_context(nc.sbuf_tensor("f_xnew", [128, 1, 1024], F32))
    xn_tm = ps.enter_context(nc.sbuf_tensor("f_xntm", [128, 2, TPG, 1024], BF16))
    xn_fm = ps.enter_context(nc.sbuf_tensor("f_xnfm", [128, 2, KO, GT], BF16))
    acc = ps.enter_context(nc.sbuf_tensor("f_acc", [128, 2, TPG, 1024], F32))
    bn12 = ps.enter_context(nc.sbuf_tensor("f_bn12", [128, 2, 12], F32))
    mr = ps.enter_context(nc.sbuf_tensor("f_mr", [128, 2, 4], F32))
    mrd = ps.enter_context(nc.sbuf_tensor("f_mrd", [128, 4, 4], F32))
    sc_sb = ps.enter_context(nc.sbuf_tensor("f_sc", [128, 8], F32))
    mx8 = ps.enter_context(nc.sbuf_tensor("f_mx8", [128, 8], F32))
    rt1 = ps.enter_context(nc.sbuf_tensor("f_rt1", [128, 8], F32))
    rt2 = ps.enter_context(nc.sbuf_tensor("f_rt2", [128, 8], F32))
    wcol = ps.enter_context(nc.sbuf_tensor("f_wcol", [128, 4], F32))
    wE_sb = ps.enter_context(nc.sbuf_tensor("f_wE", [128, 2, TPG, 8], F32))
    wE8_sb = ps.enter_context(nc.sbuf_tensor("f_wE8", [8, 2, GT], BF16))
    gb_sb = ps.enter_context(nc.sbuf_tensor("f_gb", [128, 2, 1024], BF16))
    u4_sb = ps.enter_context(nc.sbuf_tensor("f_u4", [128, 4, 1024], BF16))
    t3_sb = ps.enter_context(nc.sbuf_tensor("f_t3", [128, 1024], BF16))
    H0 = ps.enter_context(nc.psum_tensor("f_H0", [128, 2, 512], F32))
    Y = ps.enter_context(nc.psum_tensor("f_Y", [128, 2, 2, 512], F32))
    B_ps = ps.enter_context(nc.psum_tensor("f_B", [128, 8, 128], BF16))
    R_ps = ps.enter_context(nc.psum_tensor("f_R", [128, 512], F32))

    @ctx.G
    def _(g):
        for ko in range(KO):
            g.dma_start(out=gate_sb[:, ko, :], in_=W["gate_w"][ko * 128:(ko + 1) * 128, :]).then_inc(ctx.sem["dg"], 16)
            ctx.cnt["dg"] += 16
        blk0 = W["w1b"].rearrange("(e p) c -> e (p c)", p=1024)
        for pp in range(3):
            g.dma_start(out=w1_sb[:, pp, :, :],
                        in_=blk0[0:1, pp * 262144:(pp + 1) * 262144].rearrange(
                            "a (p k f) -> (a p) k f", p=128, k=8, f=256)).then_inc(ctx.sem["da"], 16)
            ctx.cnt["da"] += 16

    @ctx.G
    def _(s):
        ctx.inc(s.dma_start(out=lnb_sb[:], in_=W["ln_b"][:]), "dl", 16)
    base_loads = (ctx.cnt["dg"], ctx.cnt["dl"])

    st = {
        "xt": [0, 0], "xo": [0, 0], "store": [0, 0], "trB": 0,
        "w1free": [0, 0, 0, 0], "w1da": [16, 32, 48, 0], "w2free": [0, 0], "smfree": [0, 0],
        "hfree": [0, 0], "yfree": [0, 0],
        "xnfm": [0, 0], "wE": [0, 0], "accinit": [0, 0],
        "accfree": [0, 0], "xnfree": [0, 0], "rfree": 0, "rfree_ac": 0,
        "pend_d2b": None,
    }

    def emit_ABC(g):
        par = g % 2
        for tt in range(TPG):
            t = g * TPG + tt
            sl = tt % 2

            @ctx.S
            def _(s, t=t, sl=sl, pa=st["xt"][sl]):
                ctx.w(s, "dv", pa)
                ctx.inc(s.dma_start(out=xt[:, sl, :], in_=xin[t * 128:(t + 1) * 128, :]), "dl", 16)
            dl_here = ctx.cnt["dl"]

            @ctx.V
            def _(v, dl_here=dl_here):
                ctx.w(v, "dl", dl_here)
            ln_stats(ctx, bn12[:, sl, :], mr[:, sl, :], xt[:, sl, 0:512],
                     xt[:, sl, 512:1024], cst.eps5_col[:])
            ac_rstd = ctx.cnt["ac"]

            @ctx.V
            def _(v, par=par, tt=tt, sl=sl, ac_rstd=ac_rstd, xf=st["xnfree"][par]):
                ctx.w(v, "ac", ac_rstd)
                ctx.w(v, "dv", xf)
                ctx.inc(v.tensor_scalar(xn_tm[:, par, tt, :], xt[:, sl, :], mr[:, sl, 0:1],
                                        mr[:, sl, 2:3], ALU.subtract, ALU.mult), "dv")
            st["xt"][sl] = ctx.cnt["dv"]
            apply_dv = ctx.cnt["dv"]

            @ctx.TE
            def _(te, par=par, tt=tt, apply_dv=apply_dv, pt=st["trB"]):
                ctx.w(te, "dv", apply_dv)
                ctx.w(te, "dv", pt)
                ctx.w(te, "po", cst.ready)
                for ko in range(KO):
                    ins = te.transpose(B_ps[:, ko, :], xn_tm[:, par, tt, ko * 128:(ko + 1) * 128],
                                       cst.ident_bf[:])
                ctx.inc(ins, "pe")
            tr_pe = ctx.cnt["pe"]

            @ctx.V
            def _(v, par=par, tt=tt, tr_pe=tr_pe):
                ctx.w(v, "pe", tr_pe)
                ctx.inc(v.tensor_copy(xn_fm[:, par, :, tt * 128:(tt + 1) * 128], B_ps[:]), "dv")
            st["trB"] = ctx.cnt["dv"]
        st["xnfm"][par] = ctx.cnt["dv"]

        for tt in range(TPG):
            @ctx.TE
            def _(te, par=par, tt=tt, xnfm_dv=st["xnfm"][par], rfree=st["rfree"],
                  rfa=st["rfree_ac"]):
                ctx.w(te, "dv", xnfm_dv)
                ctx.w(te, "dg", base_loads[0])
                ctx.w(te, "dv", rfree)
                if rfa:
                    te.wait_ge(ctx.sem["ac"], rfa)
                for ko in range(KO):
                    ins = te.matmul(R_ps[:, 0:8], xn_fm[:, par, ko, tt * 128:(tt + 1) * 128],
                                    gate_sb[:, ko, :], start=(ko == 0), stop=(ko == KO - 1))
                ctx.inc(ins, "pe")
            sc_pe = ctx.cnt["pe"]

            @ctx.V
            def _(v, sc_pe=sc_pe):
                ctx.w(v, "pe", sc_pe)
                ctx.inc(v.tensor_copy(sc_sb[:], R_ps[:, 0:8]), "dv")
                st["rfree"] = max(st["rfree"], ctx.cnt["dv"])
                ctx.w(v, "dv")
                ctx.inc(v.max(mx8[:], sc_sb[:]), "dv")
                ctx.w(v, "dv")
                ctx.inc(v.tensor_tensor(wcol[:, 0:1], mx8[:, 1:2], mx8[:, 0:1], ALU.subtract), "dv")
            d_dv = ctx.cnt["dv"]

            @ctx.A
            def _(a, d_dv=d_dv):
                ctx.w(a, "dv", d_dv)
                ctx.inc(a.activation(wcol[:, 1:2], wcol[:, 0:1], AF.Exp), "ac")
            e_ac = ctx.cnt["ac"]

            @ctx.V
            def _(v, par=par, tt=tt, e_ac=e_ac):
                ctx.w(v, "ac", e_ac)
                ctx.inc(v.tensor_single_scalar(wcol[:, 2:3], wcol[:, 1:2], 1.0, ALU.add), "dv")
                ctx.w(v, "dv")
                ctx.inc(v.reciprocal(wcol[:, 3:4], wcol[:, 2:3]), "dv")
                ctx.w(v, "dv")
                v.tensor_single_scalar(rt1[:], sc_sb[:], mx8[:, 0:1], ALU.is_equal)
                ctx.inc(v.tensor_single_scalar(rt2[:], sc_sb[:], mx8[:, 1:2], ALU.is_equal), "dv")
                ctx.w(v, "dv")
                v.tensor_scalar_mul(rt1[:], rt1[:], wcol[:, 3:4])
                ctx.inc(v.tensor_scalar(rt2[:], rt2[:], wcol[:, 3:4], wcol[:, 1:2],
                                        ALU.mult, ALU.mult), "dv")
                ctx.w(v, "dv")
                ctx.inc(v.tensor_add(wE_sb[:, par, tt, :], rt1[:], rt2[:]), "dv")
            wE_dv = ctx.cnt["dv"]

            @ctx.TE
            def _(te, par=par, tt=tt, wE_dv=wE_dv, rfree=st["rfree"],
                  rfa=st["rfree_ac"]):
                ctx.w(te, "dv", wE_dv)
                ctx.w(te, "dv", rfree)
                if rfa:
                    te.wait_ge(ctx.sem["ac"], rfa)
                ctx.inc(te.transpose(R_ps[0:8, 0:128], wE_sb[:, par, tt, :], cst.ident_f[:]), "pe")
            wt_pe = ctx.cnt["pe"]

            @ctx.V
            def _(v, par=par, tt=tt, wt_pe=wt_pe):
                ctx.w(v, "pe", wt_pe)
                ctx.inc(v.tensor_copy(wE8_sb[:, par, tt * 128:(tt + 1) * 128],
                                      R_ps[0:8, 0:128]), "dv")
                st["rfree"] = max(st["rfree"], ctx.cnt["dv"])
        st["wE"][par] = ctx.cnt["dv"]

        for tt in range(TPG):
            for n in range(2):
                @ctx.TE
                def _(te, par=par, tt=tt, n=n, wEd=st["wE"][par], rfree=st["rfree"],
                      rfa=st["rfree_ac"]):
                    ctx.w(te, "dv", wEd)
                    ctx.w(te, "dv", rfree)
                    if rfa:
                        te.wait_ge(ctx.sem["ac"], rfa)
                    ctx.w(te, "dg", base_loads[0])
                    ctx.inc(te.matmul(R_ps[:], wE8_sb[:, par, tt * 128:(tt + 1) * 128],
                                      lnb_sb[:, n * 512:(n + 1) * 512], start=True, stop=True), "pe")
                bt_pe = ctx.cnt["pe"]

                @ctx.V
                def _(v, par=par, tt=tt, n=n, bt_pe=bt_pe, af=st["accfree"][par]):
                    ctx.w(v, "pe", bt_pe)
                    ctx.w(v, "dv", af)
                    ctx.inc(v.tensor_copy(acc[:, par, tt, n * 512:(n + 1) * 512], R_ps[:]), "dv")
                    st["rfree"] = max(st["rfree"], ctx.cnt["dv"])
        st["accinit"][par] = ctx.cnt["dv"]

    def emit_eload(e):
        eslot = e % 2

        @ctx.S
        def _(s, e=e, eslot=eslot, w2f=st["w2free"][eslot], smf=st["smfree"][eslot]):
            if w2f:
                s.wait_ge(ctx.sem["pe"], w2f)
            if smf:
                s.wait_ge(ctx.sem["dv"], smf)
            s.dma_start(out=b1t_sb[:, eslot, :],
                        in_=W["b1"][e:e + 1, :].rearrange("a (b c) -> (a b) c", c=128)).then_inc(ctx.sem["dl"], 16)
            ctx.cnt["dl"] += 16
            for fr in range(NF):
                s.dma_start(out=w2_sb[:, eslot, fr, :],
                            in_=W["w2b"][e * 2048 + fr * 128:e * 2048 + (fr + 1) * 128, :]).then_inc(ctx.sem["dl"], 16)
                ctx.cnt["dl"] += 16

        @ctx.G
        def _(gg, e=e, eslot=eslot, w2f=st["w2free"][eslot]):
            if w2f:
                gg.wait_ge(ctx.sem["pe"], w2f)
            gg.dma_start(out=rows4[:, eslot, :], in_=W["b2"][e:e + 1, :]).then_inc(ctx.sem["dg"], 16)
            gg.dma_start(out=rows4[:, 2 + eslot, :], in_=W["ln_g"][e:e + 1, :]).then_inc(ctx.sem["dg"], 16)
            ctx.cnt["dg"] += 32
        return (ctx.cnt["dl"], ctx.cnt["dg"])

    def emit_expert(g, e, ld_marks):
        par = g % 2
        eslot = e % 2
        ld_dl, ld_dg = ld_marks

        @ctx.TE
        def _(te, eslot=eslot, ld_dl=ld_dl, ld_dg=ld_dg, rfree=st["rfree"],
              yf=st["yfree"][0], rfa=st["rfree_ac"]):
            ctx.w(te, "dl", ld_dl)
            ctx.w(te, "dg", ld_dg)
            ctx.w(te, "dv", rfree)
            if rfa:
                te.wait_ge(ctx.sem["ac"], rfa)
            ctx.w(te, "dv", yf)
            te.matmul(Y[:, 0, 0, :], cst.ones_row_bf[:], rows4[0:1, 2 + eslot, 0:512],
                      start=True, stop=True)
            te.matmul(Y[:, 0, 1, :], cst.ones_row_bf[:], rows4[0:1, 2 + eslot, 512:1024],
                      start=True, stop=True)
            ctx.inc(te.transpose(R_ps[:, 0:16], b1t_sb[:, eslot, :],
                                 cst.ident_f[0:16, 0:16]), "pe")
        gb_pe = ctx.cnt["pe"]

        @ctx.V
        def _(v, eslot=eslot, gb_pe=gb_pe):
            ctx.w(v, "pe", gb_pe)
            v.tensor_copy(gb_sb[:, eslot, 0:512], Y[:, 0, 0, :])
            ctx.inc(v.tensor_copy(gb_sb[:, eslot, 512:1024], Y[:, 0, 1, :]), "dv")
        gb_dv = ctx.cnt["dv"]
        st["yfree"][0] = max(st["yfree"][0], gb_dv)
        b1_pe = gb_pe

        ac0 = ctx.cnt["ac"]
        gelu_marks = []
        NPAIR = NF // 2
        gpair = g * NE * NPAIR + e * NPAIR
        for f in range(NF):
            pair, sub = f // 2, f % 2
            slot = (gpair + pair) % 4
            if sub == 0:
                gp_next = gpair + pair + 3
                tot = NG * NE * NPAIR
                if gp_next < tot:
                    ne, np_ = (gp_next // NPAIR) % NE, gp_next % NPAIR
                    nslot = gp_next % 4

                    @ctx.G
                    def _(s, ne=ne, np_=np_, nslot=nslot, used=st["w1free"][nslot]):
                        if used:
                            s.wait_ge(ctx.sem["pe"], used)
                        blk = W["w1b"].rearrange("(e p) c -> e (p c)", p=1024)
                        s.dma_start(
                            out=w1_sb[:, nslot, :, :],
                            in_=blk[ne:ne + 1, np_ * 262144:(np_ + 1) * 262144].rearrange(
                                "a (p k f) -> (a p) k f", p=128, k=8, f=256)).then_inc(ctx.sem["da"], 16)
                        ctx.cnt["da"] += 16
                    st["w1da"][nslot] = ctx.cnt["da"]

            hb = f % 2
            hfree = (ac0 + f - 1) if f >= 2 else st["hfree"][hb]

            @ctx.TE
            def _(te, par=par, f=f, slot=slot, sub=sub, hb=hb, gb_dv=gb_dv,
                  hfree=hfree, w1f=st["w1da"][slot], xnfm_dv=st["xnfm"][par]):
                ctx.w(te, "da", w1f)
                ctx.w(te, "dv", xnfm_dv)
                if hfree:
                    te.wait_ge(ctx.sem["ac"], hfree)
                for ko in range(KO):
                    ins = te.matmul(H0[:, hb, :],
                                    w1_sb[:, slot, ko, sub * 128:(sub + 1) * 128],
                                    xn_fm[:, par, ko, :],
                                    start=(ko == 0), stop=(ko == KO - 1))
                ctx.inc(ins, "pe")
            h_pe = ctx.cnt["pe"]
            gelu_marks.append((f, hb, h_pe))
            if sub == 1:
                st["w1free"][slot] = h_pe

        @ctx.A
        def _(a, eslot=eslot, b1_pe=b1_pe, marks=tuple(gelu_marks)):
            a.wait_ge(ctx.sem["pe"], b1_pe)
            ctx.inc(a.activation(b1c_sb[:, eslot, :], R_ps[:, 0:16], AF.Copy), "po")
            ctx.w(a, "po")
            for f, hb, h_pe in marks:
                a.wait_ge(ctx.sem["pe"], h_pe)
                ctx.inc(a.activation(h_sb[:, f, :], H0[:, hb, :], AF.Gelu,
                                     bias=b1c_sb[:, eslot, f:f + 1]), "ac")
        assert ctx.cnt["ac"] == ac0 + NF
        st["rfree_ac"] = ac0 + 1
        flush_d2b()
        st["hfree"] = [ac0 + NF - 1, ac0 + NF]
        h_ac = ctx.cnt["ac"]

        aggr_marks = []
        for tt in range(TPG):
            slot = tt % 2

            @ctx.TE
            def _(te, par=par, tt=tt, slot=slot, eslot=eslot, h_ac=h_ac, ld_dl=ld_dl,
                  yfree=st["yfree"][slot]):
                ctx.w(te, "ac", h_ac)
                ctx.w(te, "dl", ld_dl)
                ctx.w(te, "dg")
                ctx.w(te, "dv", yfree)
                for n in range(2):
                    for f in range(NF):
                        te.matmul(Y[:, slot, n, :], h_sb[:, f, tt * 128:(tt + 1) * 128],
                                  w2_sb[:, eslot, f, n * 512:(n + 1) * 512],
                                  start=(f == 0), stop=False)
                    ins = te.matmul(Y[:, slot, n, :], cst.ones_row_bf[:],
                                    rows4[0:1, eslot, n * 512:(n + 1) * 512],
                                    start=False, stop=True)
                ctx.inc(ins, "pe")
            y_pe = ctx.cnt["pe"]

            @ctx.V
            def _(v, par=par, tt=tt, slot=slot, y_pe=y_pe):
                ctx.w(v, "pe", y_pe)
                ctx.inc(v.tensor_add(u4_sb[:, tt, :], Y[:, slot].rearrange("p a b -> p (a b)"),
                                     xn_tm[:, par, tt, :]), "dv")
                ctx.w(v, "dv")
                v.bn_stats(bn12[:, 0, 0:6], u4_sb[:, tt, 0:512])
                ctx.inc(v.bn_stats(bn12[:, 0, 6:12], u4_sb[:, tt, 512:1024]), "dv")
                ctx.w(v, "dv")
                ctx.inc(v.bn_aggr(mrd[:, tt, 0:2], bn12[:, 0, :]), "dv")
            st["yfree"][slot] = ctx.cnt["dv"]
            aggr_marks.append(ctx.cnt["dv"])
        st["w2free"][eslot] = ctx.cnt["pe"]
        st["pend_d2b"] = (g, e, tuple(aggr_marks))

    def flush_d2b():
        if st["pend_d2b"] is None:
            return
        pg, pe_, marks = st["pend_d2b"]
        st["pend_d2b"] = None
        ppar = pg % 2
        peslot = pe_ % 2

        @ctx.A
        def _(a, marks=marks):
            for tt in range(TPG):
                a.wait_ge(ctx.sem["dv"], marks[tt])
                ctx.inc(a.activation(mrd[:, tt, 3:4], mrd[:, tt, 1:2], AF.Ln,
                                     bias=cst.eps6_col[:]), "ac")
                ctx.w(a, "ac")
                ctx.inc(a.activation(mrd[:, tt, 2:3], mrd[:, tt, 3:4], AF.Exp,
                                     scale=-0.5), "ac")
        rstd_ac = ctx.cnt["ac"]

        @ctx.V
        def _(v, ppar=ppar, pe_=pe_, peslot=peslot, rstd_ac=rstd_ac,
              accinit=st["accinit"][ppar]):
            ctx.w(v, "ac", rstd_ac)
            ctx.w(v, "dv", accinit)
            for tt in range(TPG):
                ctx.inc(v.tensor_tensor(mrd[:, tt, 3:4], mrd[:, tt, 2:3],
                                        wE_sb[:, ppar, tt, pe_:pe_ + 1], ALU.mult), "dv")
            ctx.w(v, "dv")
            for tt in range(TPG):
                ctx.inc(v.scalar_tensor_tensor(t3_sb[:], u4_sb[:, tt, :], mrd[:, tt, 0:1],
                                               gb_sb[:, peslot, :],
                                               ALU.subtract, ALU.mult), "dv")
                ctx.w(v, "dv")
                ctx.inc(v.scalar_tensor_tensor(acc[:, ppar, tt, :], t3_sb[:], mrd[:, tt, 3:4],
                                               acc[:, ppar, tt, :], ALU.mult, ALU.add), "dv")
        st["smfree"][peslot] = ctx.cnt["dv"]

    def emit_E(g):
        flush_d2b()
        par = g % 2
        for tt in range(TPG):
            t = g * TPG + tt
            sl = tt % 2

            @ctx.S
            def _(s, t=t, po=st["xo"][0]):
                ctx.w(s, "dv", po)
                ctx.inc(s.dma_start(out=xo[:, 0, :], in_=xin[t * 128:(t + 1) * 128, :]), "dl", 16)
            xo_dl = ctx.cnt["dl"]

            @ctx.V
            def _(v, xo_dl=xo_dl):
                ctx.w(v, "dl", xo_dl)
                ctx.w(v, "dv")
            ln_stats(ctx, bn12[:, sl, :], mr[:, sl, :], acc[:, par, tt, 0:512],
                     acc[:, par, tt, 512:1024], cst.eps5_col[:])
            ac_rstd = ctx.cnt["ac"]

            @ctx.V
            def _(v, par=par, tt=tt, sl=sl, ac_rstd=ac_rstd, pstore=st["store"][0]):
                ctx.w(v, "ac", ac_rstd)
                ctx.inc(v.tensor_scalar(acc[:, par, tt, :], acc[:, par, tt, :], mr[:, sl, 0:1],
                                        mr[:, sl, 2:3], ALU.subtract, ALU.mult), "dv")
                ctx.w(v, "dv")
                ctx.w(v, "ds", pstore)
                ctx.inc(v.tensor_add(xnew[:, 0, :], acc[:, par, tt, :], xo[:, 0, :]), "dv")
            st["xo"][0] = ctx.cnt["dv"]
            res_dv = ctx.cnt["dv"]

            @ctx.S
            def _(s, t=t, res_dv=res_dv):
                ctx.w(s, "dv", res_dv)
                ctx.inc(s.dma_start(out=xout[t * 128:(t + 1) * 128, :], in_=xnew[:, 0, :]), "ds", 16)
            st["store"][0] = ctx.cnt["ds"]
        st["accfree"][par] = ctx.cnt["dv"]
        st["xnfree"][par] = ctx.cnt["dv"]

    emit_ABC(0)
    ld_marks = emit_eload(0)
    for g in range(NG):
        for e in range(NE):
            cur_marks = ld_marks
            emit_expert(g, e, cur_marks)
            if e == 0 and g > 0:
                emit_E(g - 1)
            if e + 1 < NE:
                ld_marks = emit_eload(e + 1)
            elif g + 1 < NG:
                ld_marks = emit_eload(0)
            if e == 3 and g + 1 < NG:
                emit_ABC(g + 1)
    emit_E(NG - 1)

    return ps


def emit_l1_attn(ctx, cst, xin, xout, W):
    nc = ctx.nc
    NH, DH, CH, NC_ = 16, 64, 256, 16
    EPS_R = 1e-6
    S_GLA = float(DH ** -0.5)
    ps = ExitStack()
    wsb = {}
    for nm in ("Wq", "Wk", "Wv", "Wg", "Wo"):
        wsb[nm] = ps.enter_context(nc.sbuf_tensor("g_" + nm, [128, KO, 1024], BF16))
    wsb["Wgt"] = ps.enter_context(nc.sbuf_tensor("g_Wgt", [128, KO, 16], BF16))
    xt = ps.enter_context(nc.sbuf_tensor("g_xt", [128, 2, 2, 1024], F32))
    xnew = ps.enter_context(nc.sbuf_tensor("g_xnew", [128, 2, 1024], F32))
    bn12 = ps.enter_context(nc.sbuf_tensor("g_bn12", [128, 12], F32))
    mr = ps.enter_context(nc.sbuf_tensor("g_mr", [128, 4], F32))
    xn_tm = ps.enter_context(nc.sbuf_tensor("g_xntm", [128, 2, 1024], BF16))
    xn_fm = ps.enter_context(nc.sbuf_tensor("g_xnfm", [128, KO, 256], BF16))
    q_tm = ps.enter_context(nc.sbuf_tensor("g_qtm", [128, 2, 1024], BF16))
    k_tm = ps.enter_context(nc.sbuf_tensor("g_ktm", [128, 2, 1024], BF16))
    v_tm = ps.enter_context(nc.sbuf_tensor("g_vtm", [128, 2, 1024], BF16))
    sg_tm = ps.enter_context(nc.sbuf_tensor("g_sgtm", [128, 2, 1024], BF16))
    kp_tm = ps.enter_context(nc.sbuf_tensor("g_kptm", [128, 2, 1024], BF16))
    vp_tm = ps.enter_context(nc.sbuf_tensor("g_vptm", [128, 2, 1024], BF16))
    o_tm = ps.enter_context(nc.sbuf_tensor("g_otm", [128, 2, 1024], BF16))
    on_tm = ps.enter_context(nc.sbuf_tensor("g_ontm", [128, 2, 1024], BF16))
    gated_tm = ps.enter_context(nc.sbuf_tensor("g_gated", [128, 2, 1024], BF16))
    gated_fm = ps.enter_context(nc.sbuf_tensor("g_gfm", [128, KO, 256], BF16))
    gt_sb = ps.enter_context(nc.sbuf_tensor("g_gt", [128, 2, 16], F32))
    lgA = ps.enter_context(nc.sbuf_tensor("g_lgA", [128, 2, 16], F32))
    lgB = ps.enter_context(nc.sbuf_tensor("g_lgB", [128, 2, 16], F32))
    EXN = ps.enter_context(nc.sbuf_tensor("g_EXN", [128, 2, 16], F32))
    EXPc = ps.enter_context(nc.sbuf_tensor("g_EXP", [128, 2, 16], F32))
    EK = ps.enter_context(nc.sbuf_tensor("g_EK", [128, 16], F32))
    al_sb = ps.enter_context(nc.sbuf_tensor("g_al", [128, 2, 16], F32))
    expB_sb = ps.enter_context(nc.sbuf_tensor("g_expB", [1, 16], F32))
    ebh_sb = ps.enter_context(nc.sbuf_tensor("g_ebh", [64, 2, 16], F32))
    cols = ps.enter_context(nc.sbuf_tensor("g_cols", [128, 2, 8], F32))
    sqs = ps.enter_context(nc.sbuf_tensor("g_sqs", [128, 1024], BF16))
    q_fm = ps.enter_context(nc.sbuf_tensor("g_qfm", [64, NH, 256], BF16))
    kp_fm = ps.enter_context(nc.sbuf_tensor("g_kpfm", [64, NH, 256], BF16))
    A_sb = ps.enter_context(nc.sbuf_tensor("g_A", [128, 2, 2, 256], BF16))
    oT_sb = ps.enter_context(nc.sbuf_tensor("g_oT", [64, 2, 256], BF16))
    kv_sb = ps.enter_context(nc.sbuf_tensor("g_kv", [64, 2, 64], F32))
    S_sb = ps.enter_context(nc.sbuf_tensor("g_S", [64, NH, 64], F32))
    S_bf = ps.enter_context(nc.sbuf_tensor("g_Sbf", [64, NH, 64], BF16))
    scr64 = ps.enter_context(nc.sbuf_tensor("g_scr64", [64, 64], F32))
    on2 = ps.enter_context(nc.sbuf_tensor("g_on2", [128, 16], F32))
    ost = ps.enter_context(nc.sbuf_tensor("g_ost", [128, 16], F32))
    u_sb = ps.enter_context(nc.sbuf_tensor("g_u", [128, 1024], F32))
    P0 = ps.enter_context(nc.psum_tensor("g_P0", [128, 2, 512], F32))
    AT_ps = ps.enter_context(nc.psum_tensor("g_AT", [128, 2, 2, 256], F32))
    OT_ps = ps.enter_context(nc.psum_tensor("g_OT", [64, 2, 512], F32))
    KV_ps = ps.enter_context(nc.psum_tensor("g_KV", [64, 64], F32))
    B_ps = ps.enter_context(nc.psum_tensor("g_B", [128, 8, 128], BF16))

    @ctx.G
    def _(g):
        for nm in ("Wq", "Wk", "Wv", "Wg", "Wo"):
            for ko in range(KO):
                g.dma_start(out=wsb[nm][:, ko, :],
                            in_=W[nm][ko * 128:(ko + 1) * 128, :]).then_inc(ctx.sem["dg"], 16)
                ctx.cnt["dg"] += 16
        for ko in range(KO):
            g.dma_start(out=wsb["Wgt"][:, ko, :],
                        in_=W["Wgt"][ko * 128:(ko + 1) * 128, :]).then_inc(ctx.sem["dg"], 16)
            ctx.cnt["dg"] += 16
    wts_dg = ctx.cnt["dg"]

    @ctx.V
    def _(v):
        v.memset(S_sb[:], 0.0)
        ctx.inc(v.memset(S_bf[:], 0.0), "dv")

    prev_resid = [[0, 0], [0, 0]]
    prev_store = [0, 0]
    trB = [0]
    prev_projfree = 0
    at_u_guard = [0]
    marks = {}
    xnfm_marks = {}
    gt_marks = {}
    front_marks = {}

    def s1_tile(n, i):
        t = 2 * n + i
        par = n % 2

        @ctx.S
        def _(s, t=t, par=par, i=i, pr=prev_resid[par][i]):
            ctx.w(s, "dv", pr)
            ctx.inc(s.dma_start(out=xt[:, par, i, :], in_=xin[t * 128:(t + 1) * 128, :]), "dl", 16)
        dl_here = ctx.cnt["dl"]

        @ctx.V
        def _(v, dl_here=dl_here):
            ctx.w(v, "dl", dl_here)
        ln_stats(ctx, bn12[:], mr[:], xt[:, par, i, 0:512], xt[:, par, i, 512:1024],
                 cst.eps5_col[:])
        ac_rstd = ctx.cnt["ac"]

        @ctx.V
        def _(v, par=par, i=i, ac_rstd=ac_rstd):
            ctx.w(v, "ac", ac_rstd)
            ctx.inc(v.tensor_scalar(xn_tm[:, i, :], xt[:, par, i, :], mr[:, 0:1], mr[:, 2:3],
                                    ALU.subtract, ALU.mult), "dv")
        apply_dv = ctx.cnt["dv"]

        @ctx.TE
        def _(te, i=i, apply_dv=apply_dv, pt=trB[0]):
            ctx.w(te, "dv", apply_dv)
            ctx.w(te, "dv", pt)
            ctx.w(te, "po", cst.ready)
            for ko in range(KO):
                ins = te.transpose(B_ps[:, ko, :], xn_tm[:, i, ko * 128:(ko + 1) * 128],
                                   cst.ident_bf[:])
            ctx.inc(ins, "pe")
        tr_pe = ctx.cnt["pe"]

        @ctx.V
        def _(v, i=i, tr_pe=tr_pe):
            ctx.w(v, "pe", tr_pe)
            ctx.inc(v.tensor_copy(xn_fm[:, :, i * 128:(i + 1) * 128], B_ps[:]), "dv")
        trB[0] = ctx.cnt["dv"]
        xnfm_marks[n] = ctx.cnt["dv"]

    def zt_tile(n, i):
        @ctx.TE
        def _(te, i=i, xnfm_dv=xnfm_marks[n]):
            ctx.w(te, "dv", xnfm_dv)
            ctx.w(te, "dg", wts_dg)
            ctx.w(te, "ac")
            for ko in range(KO):
                ins = te.matmul(P0[:, 0, i * 16:(i + 1) * 16], xn_fm[:, ko, i * 128:(i + 1) * 128],
                                wsb["Wgt"][:, ko, :], start=(ko == 0), stop=(ko == KO - 1))
            ctx.inc(ins, "pe")
        zt_pe = ctx.cnt["pe"]

        @ctx.V
        def _(v, i=i, zt_pe=zt_pe):
            ctx.w(v, "pe", zt_pe)
            ctx.inc(v.tensor_single_scalar(lgA[:, i, :], P0[:, 0, i * 16:(i + 1) * 16],
                                           0.0, ALU.min), "dv")
        mn_dv = ctx.cnt["dv"]

        @ctx.A
        def _(a, i=i, zt_pe=zt_pe, mn_dv=mn_dv):
            ctx.w(a, "pe", zt_pe)
            ctx.w(a, "dv", mn_dv)
            ctx.inc(a.activation(lgB[:, i, :], P0[:, 0, i * 16:(i + 1) * 16], AF.Abs), "ac")
            ctx.w(a, "ac")
            ctx.inc(a.activation(lgB[:, i, :], lgB[:, i, :], AF.Exp, scale=-1.0), "ac")
            ctx.w(a, "ac")
            ctx.inc(a.activation(lgB[:, i, :], lgB[:, i, :], AF.Ln, bias=1.0), "ac")
        lg_ac = ctx.cnt["ac"]

        @ctx.V
        def _(v, i=i, lg_ac=lg_ac):
            ctx.w(v, "ac", lg_ac)
            ctx.inc(v.tensor_tensor(gt_sb[:, i, :], lgA[:, i, :], lgB[:, i, :],
                                    ALU.subtract), "dv")
            ctx.w(v, "dv")
            ctx.inc(v.tensor_scalar_mul(gt_sb[:, i, :], gt_sb[:, i, :], 1.0 / 16.0), "dv")
        gt_marks[n] = ctx.cnt["dv"]

    def bc_chunk(n):
        par = n % 2

        @ctx.TE
        def _(te, gt_dv=gt_marks[n]):
            ctx.w(te, "dv", gt_dv)
            ctx.w(te, "ac")
            te.matmul(P0[:, 0, 32:48], cst.triu_f[:], gt_sb[:, 0, :], start=True, stop=True)
            te.matmul(P0[:, 0, 48:64], cst.ones128_f[:], gt_sb[:, 0, :], start=True, stop=False)
            te.matmul(P0[:, 0, 48:64], cst.triu_f[:], gt_sb[:, 1, :], start=False, stop=True)
            te.matmul(P0[0:1, 0, 64:80], cst.ones_col_f[:], gt_sb[:, 0, :], start=True, stop=False)
            ctx.inc(te.matmul(P0[0:1, 0, 64:80], cst.ones_col_f[:], gt_sb[:, 1, :],
                              start=False, stop=True), "pe")
        bc_pe = ctx.cnt["pe"]

        @ctx.A
        def _(a, bc_pe=bc_pe):
            ctx.w(a, "pe", bc_pe)
            for i in range(2):
                a.activation(EXN[:, i, :], P0[:, 0, (32 + 16 * i):(48 + 16 * i)], AF.Exp, scale=-1.0)
                a.activation(EXPc[:, i, :], P0[:, 0, (32 + 16 * i):(48 + 16 * i)], AF.Exp)
            ctx.inc(a.activation(expB_sb[:], P0[0:1, 0, 64:80], AF.Exp), "ac")
        exf_ac = ctx.cnt["ac"]

        @ctx.TE
        def _(te, exf_ac=exf_ac):
            ctx.w(te, "ac", exf_ac)
            ctx.inc(te.matmul(P0[0:64, 0, 80:96], cst.ones_row_f[0:1, 0:64], expB_sb[:],
                              start=True, stop=True), "pe")
        eb_pe = ctx.cnt["pe"]

        @ctx.V
        def _(v, par=par, eb_pe=eb_pe):
            ctx.w(v, "pe", eb_pe)
            ctx.inc(v.tensor_copy(ebh_sb[:, par, :], P0[0:64, 0, 80:96]), "dv")
        front_marks[n] = (exf_ac, ctx.cnt["dv"])

    def front_all(n):
        for i in range(2):
            s1_tile(n, i)
        for i in range(2):
            zt_tile(n, i)
        bc_chunk(n)

    front_all(0)

    for n in range(NC_):
        par = n % 2
        xnfm_dv = xnfm_marks[n]
        exf_ac, ebh_dv = front_marks[n]

        projfree = [prev_projfree, 0]
        for i in range(2):
            for pj, (nm, dst) in enumerate((("Wq", q_tm), ("Wk", k_tm), ("Wv", v_tm),
                                            ("Wg", sg_tm))):
                g8 = i * 4 + pj
                reg = g8 % 2

                @ctx.TE
                def _(te, i=i, pj=pj, nm=nm, g8=g8, reg=reg, pfree=projfree[reg],
                      xnfm_dv=xnfm_dv, exf=exf_ac, ebh=ebh_dv, ug=at_u_guard[0]):
                    ctx.w(te, "dv", xnfm_dv)
                    ctx.w(te, "ac", pfree)
                    if g8 == 0:
                        ctx.w(te, "ac", exf)
                        ctx.w(te, "dv", ebh)
                    if g8 == 1:
                        ctx.w(te, "dv", ug)
                    for ko in range(KO):
                        for nn in range(2):
                            out = P0[:, nn, :] if reg == 0 else \
                                AT_ps[:, nn].rearrange("p a b -> p (a b)")
                            ins = te.matmul(out, xn_fm[:, ko, i * 128:(i + 1) * 128],
                                            wsb[nm][:, ko, nn * 512:(nn + 1) * 512],
                                            start=(ko == 0), stop=(ko == KO - 1))
                    ctx.inc(ins, "pe")
                pj_pe = ctx.cnt["pe"]

                @ctx.A
                def _(a, i=i, pj=pj, reg=reg, dst=dst, pj_pe=pj_pe):
                    ctx.w(a, "pe", pj_pe)
                    ctx.w(a, "dv")
                    src = P0[:].rearrange("p a b -> p (a b)") if reg == 0 else \
                        AT_ps[:].rearrange("p a b c -> p (a b c)")
                    if pj == 3:
                        ins = a.activation(dst[:, i, :], src, AF.Silu)
                    else:
                        a.activation(dst[:, i, :], src, AF.Copy)
                        ins = a.activation(sqs[:], src, AF.Square,
                                           accum_out=cols[:, i, pj:pj + 1])
                    ctx.inc(ins, "ac")
                projfree[reg] = ctx.cnt["ac"]
                prev_projfree = ctx.cnt["ac"]
                ev_ac = ctx.cnt["ac"]

                if pj < 3:
                    @ctx.V
                    def _(v, i=i, pj=pj, ev_ac=ev_ac):
                        ctx.w(v, "ac", ev_ac)
                        ctx.inc(v.tensor_scalar(cols[:, i, 4 + pj:5 + pj], cols[:, i, pj:pj + 1],
                                                1.0 / D, float(EPS_R), ALU.mult, ALU.add), "dv")
                    rc_dv = ctx.cnt["dv"]

                    @ctx.A
                    def _(a, i=i, pj=pj, rc_dv=rc_dv):
                        ctx.w(a, "dv", rc_dv)
                        ctx.inc(a.activation(cols[:, i, 4 + pj:5 + pj],
                                             cols[:, i, 4 + pj:5 + pj], AF.Ln), "ac")
                        ctx.w(a, "ac")
                        ctx.inc(a.activation(cols[:, i, 4 + pj:5 + pj],
                                             cols[:, i, 4 + pj:5 + pj], AF.Exp, scale=-0.5), "ac")
        rqkv_ac = ctx.cnt["ac"]

        for i in range(2):
            @ctx.V
            def _(v, i=i, rqkv_ac=rqkv_ac, ebh_dv=ebh_dv):
                ctx.w(v, "ac", rqkv_ac)
                ctx.w(v, "dv", ebh_dv)
                ctx.inc(v.tensor_scalar_mul(EK[:], EXN[:, i, :], cols[:, i, 5:6]), "dv")
                ctx.w(v, "dv")
                ctx.inc(v.tensor_tensor(
                    kp_tm[:, i, :].rearrange("p (h e) -> p h e", e=64),
                    k_tm[:, i, :].rearrange("p (h e) -> p h e", e=64),
                    EK[:].unsqueeze(2).broadcast_to([128, 16, 64]), ALU.mult), "dv")
                ctx.inc(v.tensor_scalar_mul(vp_tm[:, i, :], v_tm[:, i, :], cols[:, i, 6:7]), "dv")
                ctx.inc(v.tensor_scalar(al_sb[:, i, :], EXPc[:, i, :], cols[:, i, 4:5],
                                        S_GLA, ALU.mult, ALU.mult), "dv")
        kpv_dv = ctx.cnt["dv"]

        for i in range(2):
            for grp in range(2):
                for src_tm, dst_fm in ((q_tm, q_fm), (kp_tm, kp_fm)):
                    @ctx.TE
                    def _(te, i=i, grp=grp, src_tm=src_tm, kpv_dv=kpv_dv, pt=trB[0]):
                        ctx.w(te, "dv", kpv_dv)
                        ctx.w(te, "dv", pt)
                        for hh in range(8):
                            h = grp * 8 + hh
                            ins = te.transpose(B_ps[0:64, hh, :],
                                               src_tm[:, i, h * 64:(h + 1) * 64],
                                               cst.ident_bf[:])
                        ctx.inc(ins, "pe")
                    tq_pe = ctx.cnt["pe"]

                    @ctx.V
                    def _(v, i=i, grp=grp, dst_fm=dst_fm, tq_pe=tq_pe):
                        ctx.w(v, "pe", tq_pe)
                        ctx.inc(v.tensor_copy(
                            dst_fm[:, grp * 8:(grp + 1) * 8, i * 128:(i + 1) * 128],
                            B_ps[0:64, :, :]), "dv")
                    trB[0] = ctx.cnt["dv"]
        qk_fm_dv = ctx.cnt["dv"]

        hg0 = n * NH

        def stage_at(h):
            sl = h % 2

            @ctx.TE
            def _(te, h=h, sl=sl, qk_fm_dv=qk_fm_dv,
                  atfree=marks.get(("aev", hg0 + h - 2), 0), uguard=at_u_guard[0]):
                ctx.w(te, "dv", qk_fm_dv)
                if h < 2:
                    ctx.w(te, "dv", uguard)
                if atfree:
                    te.wait_ge(ctx.sem["dv"], atfree)
                for jt in range(2):
                    ins = te.matmul(AT_ps[:, sl, jt, :], kp_fm[:, h, jt * 128:(jt + 1) * 128],
                                    q_fm[:, h, :], start=True, stop=True)
                ctx.inc(ins, "pe")
            marks[("at", hg0 + h)] = ctx.cnt["pe"]

        def stage_aev(h):
            sl = h % 2

            @ctx.V
            def _(v, h=h, sl=sl, at_pe=marks[("at", hg0 + h)],
                  afree=marks.get(("ot", hg0 + h - 2), 0)):
                ctx.w(v, "pe", at_pe)
                if afree:
                    v.wait_ge(ctx.sem["pe"], afree)
                v.tensor_tensor(A_sb[:, sl, 0, 0:128], AT_ps[:, sl, 0, 0:128],
                                cst.triu_bf[:], ALU.mult)
                v.tensor_copy(A_sb[:, sl, 0, 128:256], AT_ps[:, sl, 0, 128:256])
                v.memset(A_sb[:, sl, 1, 0:128], 0.0)
                ctx.inc(v.tensor_tensor(A_sb[:, sl, 1, 128:256], AT_ps[:, sl, 1, 128:256],
                                        cst.triu_bf[:], ALU.mult), "dv")
            marks[("aev", hg0 + h)] = ctx.cnt["dv"]

        def stage_ot(h):
            sl = h % 2

            @ctx.TE
            def _(te, h=h, sl=sl, n=n, a_dv=marks[("aev", hg0 + h)],
                  otfree=marks.get(("otev", hg0 + h - 2), 0),
                  kvfree=marks.get(("kvev", hg0 + h - 1), 0)):
                ctx.w(te, "dv", a_dv)
                if otfree:
                    te.wait_ge(ctx.sem["dv"], otfree)
                for jt in range(2):
                    te.matmul(OT_ps[:, sl, 0:256], vp_tm[:, jt, h * 64:(h + 1) * 64],
                              A_sb[:, sl, jt, :], start=(jt == 0), stop=(jt == 1 and n == 0))
                if n > 0:
                    te.matmul(OT_ps[:, sl, 0:256], S_bf[:, h, :], q_fm[:, h, :],
                              start=False, stop=True)
                if kvfree:
                    te.wait_ge(ctx.sem["dv"], kvfree)
                for jt in range(2):
                    ins = te.matmul(KV_ps[:], kp_tm[:, jt, h * 64:(h + 1) * 64],
                                    vp_tm[:, jt, h * 64:(h + 1) * 64],
                                    start=(jt == 0), stop=(jt == 1))
                ctx.inc(ins, "pe")
            marks[("ot", hg0 + h)] = ctx.cnt["pe"]

        def stage_ev(h):
            sl = h % 2

            @ctx.V
            def _(a, h=h, sl=sl, ot_pe=marks[("ot", hg0 + h)],
                  otr=marks.get(("otr", hg0 + h - 2), 0), sup=marks.get(("sup", hg0 + h - 2), 0)):
                ctx.w(a, "pe", ot_pe)
                if sup:
                    a.wait_ge(ctx.sem["dv"], sup)
                ctx.inc(a.tensor_copy(kv_sb[:, sl, :], KV_ps[:]), "dv")
                if otr:
                    a.wait_ge(ctx.sem["pe"], otr)
                ctx.inc(a.tensor_copy(oT_sb[:, sl, :], OT_ps[:, sl, 0:256]), "dv")
            marks[("kvev", hg0 + h)] = ctx.cnt["dv"] - 1
            marks[("otev", hg0 + h)] = ctx.cnt["dv"]

        def stage_otr(h):
            sl = h % 2

            @ctx.TE
            def _(te, h=h, sl=sl, otev=marks[("otev", hg0 + h)], pt=trB[0]):
                te.wait_ge(ctx.sem["dv"], otev)
                ctx.w(te, "dv", pt)
                te.transpose(B_ps[0:128, 0, 0:64], oT_sb[:, sl, 0:128],
                             cst.ident_bf[0:64, 0:64])
                ctx.inc(te.transpose(B_ps[0:128, 1, 0:64], oT_sb[:, sl, 128:256],
                                     cst.ident_bf[0:64, 0:64]), "pe")
            marks[("otr", hg0 + h)] = ctx.cnt["pe"]

            @ctx.V
            def _(v, h=h, otr_pe=marks[("otr", hg0 + h)]):
                ctx.w(v, "pe", otr_pe)
                for i in range(2):
                    ins = v.tensor_scalar_mul(o_tm[:, i, h * 64:(h + 1) * 64],
                                              B_ps[:, i, 0:64], al_sb[:, i, h:h + 1])
                ctx.inc(ins, "dv")
            trB[0] = ctx.cnt["dv"]

            @ctx.V
            def _(v, h=h, sl=sl, par=par, kvev=marks[("kvev", hg0 + h)]):
                v.wait_ge(ctx.sem["dv"], kvev)
                ctx.inc(v.tensor_add(scr64[:], S_sb[:, h, :], kv_sb[:, sl, :]), "dv")
                ctx.w(v, "dv")
                ctx.inc(v.tensor_scalar_mul(S_sb[:, h, :], scr64[:],
                                            ebh_sb[:, par, h:h + 1]), "dv")
                ctx.w(v, "dv")
                ctx.inc(v.tensor_copy(S_bf[:, h, :], S_sb[:, h, :]), "dv")
            marks[("sup", hg0 + h)] = ctx.cnt["dv"]

        stage_at(0)
        for h in range(NH):
            stage_aev(h)
            if h + 1 < NH:
                stage_at(h + 1)
            stage_ot(h)
            stage_ev(h)
            stage_otr(h)
            if n + 1 < NC_:
                if h == 3:
                    s1_tile(n + 1, 0)
                elif h == 5:
                    s1_tile(n + 1, 1)
                elif h == 7:
                    zt_tile(n + 1, 0)
                elif h == 9:
                    zt_tile(n + 1, 1)
                elif h == 11:
                    bc_chunk(n + 1)
        o_dv = ctx.cnt["dv"]

        for i in range(2):
            t = 2 * n + i

            @ctx.A
            def _(a, i=i, o_dv=o_dv):
                ctx.w(a, "dv", o_dv)
                ctx.w(a, "dv")
                ctx.inc(a.activation(sqs[:], o_tm[:, i, :], AF.Square), "ac")
            sq_ac = ctx.cnt["ac"]

            @ctx.V
            def _(v, i=i, sq_ac=sq_ac):
                ctx.w(v, "ac", sq_ac)
                ctx.inc(v.tensor_reduce(on2[:], sqs[:].rearrange("p (h e) -> p h e", e=64),
                                        AX.X, ALU.add), "dv")
                ctx.w(v, "dv")
                ctx.inc(v.tensor_scalar(ost[:], on2[:], 1.0 / 64.0, float(EPS_R),
                                        ALU.mult, ALU.add), "dv")
            rc_dv = ctx.cnt["dv"]

            @ctx.A
            def _(a, rc_dv=rc_dv):
                ctx.w(a, "dv", rc_dv)
                ctx.inc(a.activation(ost[:], ost[:], AF.Ln), "ac")
                ctx.w(a, "ac")
                ctx.inc(a.activation(ost[:], ost[:], AF.Exp, scale=-0.5), "ac")
            rs_ac = ctx.cnt["ac"]

            @ctx.V
            def _(v, i=i, rs_ac=rs_ac):
                ctx.w(v, "ac", rs_ac)
                ctx.inc(v.tensor_tensor(
                    on_tm[:, i, :].rearrange("p (h e) -> p h e", e=64),
                    o_tm[:, i, :].rearrange("p (h e) -> p h e", e=64),
                    ost[:].unsqueeze(2).broadcast_to([128, 16, 64]), ALU.mult), "dv")
                ctx.w(v, "dv")
                ctx.inc(v.tensor_tensor(gated_tm[:, i, :], on_tm[:, i, :], sg_tm[:, i, :],
                                        ALU.mult), "dv")
            gated_dv = ctx.cnt["dv"]

            @ctx.TE
            def _(te, i=i, gated_dv=gated_dv, pt=trB[0]):
                ctx.w(te, "dv", gated_dv)
                ctx.w(te, "dv", pt)
                for ko in range(KO):
                    ins = te.transpose(B_ps[:, ko, :], gated_tm[:, i, ko * 128:(ko + 1) * 128],
                                       cst.ident_bf[:])
                ctx.inc(ins, "pe")
            gtr_pe = ctx.cnt["pe"]

            @ctx.V
            def _(v, i=i, gtr_pe=gtr_pe):
                ctx.w(v, "pe", gtr_pe)
                ctx.inc(v.tensor_copy(gated_fm[:, :, i * 128:(i + 1) * 128], B_ps[:]), "dv")
            trB[0] = ctx.cnt["dv"]
            gfm_dv = ctx.cnt["dv"]

            @ctx.TE
            def _(te, i=i, gfm_dv=gfm_dv,
                  aev_last=marks[("aev", hg0 + NH - 1)], ug=at_u_guard[0]):
                ctx.w(te, "dv", gfm_dv)
                if i == 0:
                    ctx.w(te, "dv", aev_last)
                else:
                    ctx.w(te, "dv", ug)
                for nn in range(2):
                    for ko in range(KO):
                        ins = te.matmul(AT_ps[:, nn].rearrange("p a b -> p (a b)"),
                                        gated_fm[:, ko, i * 128:(i + 1) * 128],
                                        wsb["Wo"][:, ko, nn * 512:(nn + 1) * 512],
                                        start=(ko == 0), stop=(ko == KO - 1))
                ctx.inc(ins, "pe")
            wo_pe = ctx.cnt["pe"]

            @ctx.V
            def _(v, wo_pe=wo_pe):
                ctx.w(v, "pe", wo_pe)
            ln_stats(ctx, bn12[:], mr[:],
                     AT_ps[:, 0].rearrange("p a b -> p (a b)"),
                     AT_ps[:, 1].rearrange("p a b -> p (a b)"), cst.eps5_col[:])
            ac_rstd = ctx.cnt["ac"]

            @ctx.V
            def _(v, i=i, par=par, ac_rstd=ac_rstd, pstore=prev_store[i]):
                ctx.w(v, "ac", ac_rstd)
                ctx.inc(v.tensor_scalar(u_sb[:], AT_ps[:].rearrange("p a b c -> p (a b c)"),
                                        mr[:, 0:1], mr[:, 2:3], ALU.subtract, ALU.mult), "dv")
                ctx.w(v, "dv")
                ctx.w(v, "ds", pstore)
                ctx.inc(v.tensor_add(xnew[:, i, :], u_sb[:], xt[:, par, i, :]), "dv")
            at_u_guard[0] = ctx.cnt["dv"] - 1
            prev_resid[par][i] = ctx.cnt["dv"]
            res_dv = ctx.cnt["dv"]

            @ctx.S
            def _(s, t=t, i=i, res_dv=res_dv):
                ctx.w(s, "dv", res_dv)
                ctx.inc(s.dma_start(out=xout[t * 128:(t + 1) * 128, :], in_=xnew[:, i, :]),
                        "ds", 16)
            prev_store[i] = ctx.cnt["ds"]

    return ps


def emit_l1_ffn(ctx, cst, xin, xout, W):
    nc = ctx.nc
    NG, GT, TPG, NF, NE = 8, 512, 4, 16, 8
    ps = ExitStack()
    rt_w = ps.enter_context(nc.sbuf_tensor("m_rt", [128, KO, 8], BF16))
    w13_sb = ps.enter_context(nc.sbuf_tensor("m_w13", [128, 4, KO, 512], BF16))
    w2_sb = ps.enter_context(nc.sbuf_tensor("m_w2", [128, 2, NF, 1024], BF16))
    h_sb = ps.enter_context(nc.sbuf_tensor("m_h", [128, NF, GT], BF16))
    s1_sb = ps.enter_context(nc.sbuf_tensor("m_s1", [128, GT], BF16))
    xt = ps.enter_context(nc.sbuf_tensor("m_xt", [128, 2, 1024], F32))
    xo = ps.enter_context(nc.sbuf_tensor("m_xo", [128, 1, 1024], F32))
    xnew = ps.enter_context(nc.sbuf_tensor("m_xnew", [128, 1, 1024], F32))
    xn_tm = ps.enter_context(nc.sbuf_tensor("m_xntm", [128, 2, TPG, 1024], BF16))
    xn_fm = ps.enter_context(nc.sbuf_tensor("m_xnfm", [128, 2, KO, GT], BF16))
    acc = ps.enter_context(nc.sbuf_tensor("m_acc", [128, 2, TPG, 1024], F32))
    bn12 = ps.enter_context(nc.sbuf_tensor("m_bn12", [128, 2, 12], F32))
    mr = ps.enter_context(nc.sbuf_tensor("m_mr", [128, 2, 4], F32))
    sc_sb = ps.enter_context(nc.sbuf_tensor("m_sc", [128, 8], F32))
    mx8 = ps.enter_context(nc.sbuf_tensor("m_mx8", [128, 8], F32))
    ex8 = ps.enter_context(nc.sbuf_tensor("m_ex8", [128, 8], F32))
    rt1 = ps.enter_context(nc.sbuf_tensor("m_rt1", [128, 8], F32))
    rt2 = ps.enter_context(nc.sbuf_tensor("m_rt2", [128, 8], F32))
    wcol = ps.enter_context(nc.sbuf_tensor("m_wcol", [128, 4], F32))
    wE_sb = ps.enter_context(nc.sbuf_tensor("m_wE", [128, 2, TPG, 8], F32))
    Z = ps.enter_context(nc.psum_tensor("m_Z", [128, 2, 512], F32))
    Y = ps.enter_context(nc.psum_tensor("m_Y", [128, 2, 2, 512], F32))
    B_ps = ps.enter_context(nc.psum_tensor("m_B", [128, 8, 128], BF16))
    R_ps = ps.enter_context(nc.psum_tensor("m_R", [128, 512], F32))

    @ctx.G
    def _(g):
        for ko in range(KO):
            g.dma_start(out=rt_w[:, ko, :],
                        in_=W["w_router"][ko * 128:(ko + 1) * 128, :]).then_inc(ctx.sem["dg"], 16)
            ctx.cnt["dg"] += 16
    rt_dg = ctx.cnt["dg"]

    st = {
        "xt": [0, 0], "xo": [0], "store": [0], "trB": 0,
        "w13free": [0, 0, 0, 0], "w13da": [0, 0, 0, 0],
        "w2free": [0, 0], "zfree": (0, 0), "yfree": [0, 0],
        "xnfm": [0, 0], "wE": [0, 0], "accinit": [0, 0],
        "accfree": [0, 0], "xnfree": [0, 0], "rfree": 0,
    }

    NPAIR = NF // 2
    NEX = NE + 1

    def w13_block(e):
        if e < 0:
            return (W["ws1b"].rearrange("(e p) c -> e (p c)", p=1024),
                    W["ws3b"].rearrange("(e p) c -> e (p c)", p=1024), 0)
        return (W["w1b"].rearrange("(e p) c -> e (p c)", p=1024),
                W["w3b"].rearrange("(e p) c -> e (p c)", p=1024), e)

    def emit_w13_load(gp):
        ex = (gp // NPAIR) % NEX - 1
        np_ = gp % NPAIR
        nslot = gp % 4
        b1a, b3a, eidx = w13_block(ex)

        @ctx.G
        def _(s, b1a=b1a, b3a=b3a, eidx=eidx, np_=np_, nslot=nslot,
              used=st["w13free"][nslot]):
            if used:
                s.wait_ge(ctx.sem["pe"], used)
            s.dma_start(
                out=w13_sb[:, nslot, :, 0:256],
                in_=b1a[eidx:eidx + 1, np_ * 262144:(np_ + 1) * 262144].rearrange(
                    "a (p k f) -> (a p) k f", p=128, k=8, f=256)).then_inc(ctx.sem["da"], 16)
            s.dma_start(
                out=w13_sb[:, nslot, :, 256:512],
                in_=b3a[eidx:eidx + 1, np_ * 262144:(np_ + 1) * 262144].rearrange(
                    "a (p k f) -> (a p) k f", p=128, k=8, f=256)).then_inc(ctx.sem["da"], 16)
            ctx.cnt["da"] += 32
        st["w13da"][nslot] = ctx.cnt["da"]

    @ctx.G
    def _(g):
        pass
    for gp0 in range(3):
        emit_w13_load(gp0)

    def emit_AB(g):
        par = g % 2
        for tt in range(TPG):
            t = g * TPG + tt
            sl = tt % 2

            @ctx.S
            def _(s, t=t, sl=sl, pa=st["xt"][sl]):
                ctx.w(s, "dv", pa)
                ctx.inc(s.dma_start(out=xt[:, sl, :], in_=xin[t * 128:(t + 1) * 128, :]), "dl", 16)
            dl_here = ctx.cnt["dl"]

            @ctx.V
            def _(v, dl_here=dl_here):
                ctx.w(v, "dl", dl_here)
            ln_stats(ctx, bn12[:, sl, :], mr[:, sl, :], xt[:, sl, 0:512],
                     xt[:, sl, 512:1024], cst.eps5_col[:])
            ac_rstd = ctx.cnt["ac"]

            @ctx.V
            def _(v, par=par, tt=tt, sl=sl, ac_rstd=ac_rstd, xf=st["xnfree"][par]):
                ctx.w(v, "ac", ac_rstd)
                ctx.w(v, "dv", xf)
                ctx.inc(v.tensor_scalar(xn_tm[:, par, tt, :], xt[:, sl, :], mr[:, sl, 0:1],
                                        mr[:, sl, 2:3], ALU.subtract, ALU.mult), "dv")
            st["xt"][sl] = ctx.cnt["dv"]
            apply_dv = ctx.cnt["dv"]

            @ctx.TE
            def _(te, par=par, tt=tt, apply_dv=apply_dv, pt=st["trB"]):
                ctx.w(te, "dv", apply_dv)
                ctx.w(te, "dv", pt)
                ctx.w(te, "po", cst.ready)
                for ko in range(KO):
                    ins = te.transpose(B_ps[:, ko, :], xn_tm[:, par, tt, ko * 128:(ko + 1) * 128],
                                       cst.ident_bf[:])
                ctx.inc(ins, "pe")
            tr_pe = ctx.cnt["pe"]

            @ctx.V
            def _(v, par=par, tt=tt, tr_pe=tr_pe):
                ctx.w(v, "pe", tr_pe)
                ctx.inc(v.tensor_copy(xn_fm[:, par, :, tt * 128:(tt + 1) * 128], B_ps[:]), "dv")
            st["trB"] = ctx.cnt["dv"]
        st["xnfm"][par] = ctx.cnt["dv"]

        for tt in range(TPG):
            @ctx.TE
            def _(te, par=par, tt=tt, xnfm_dv=st["xnfm"][par], rfree=st["rfree"]):
                ctx.w(te, "dv", xnfm_dv)
                ctx.w(te, "dg", rt_dg)
                ctx.w(te, "dv", rfree)
                for ko in range(KO):
                    ins = te.matmul(R_ps[:, 0:8], xn_fm[:, par, ko, tt * 128:(tt + 1) * 128],
                                    rt_w[:, ko, :], start=(ko == 0), stop=(ko == KO - 1))
                ctx.inc(ins, "pe")
            sc_pe = ctx.cnt["pe"]

            @ctx.V
            def _(v, sc_pe=sc_pe):
                ctx.w(v, "pe", sc_pe)
                ctx.inc(v.tensor_copy(sc_sb[:], R_ps[:, 0:8]), "dv")
                st["rfree"] = max(st["rfree"], ctx.cnt["dv"])
                ctx.w(v, "dv")
                ctx.inc(v.max(mx8[:], sc_sb[:]), "dv")
                ctx.w(v, "dv")
                ctx.inc(v.tensor_scalar_mul(wcol[:, 0:1], mx8[:, 0:1], -1.0), "dv")
            nm_dv = ctx.cnt["dv"]

            @ctx.A
            def _(a, nm_dv=nm_dv):
                ctx.w(a, "dv", nm_dv)
                a.activation(ex8[:], sc_sb[:], AF.Exp, bias=wcol[:, 0:1])
                ctx.inc(a.activation(wcol[:, 3:4], mx8[:, 1:2], AF.Exp, bias=wcol[:, 0:1]), "ac")
            ex_ac = ctx.cnt["ac"]

            @ctx.V
            def _(v, par=par, tt=tt, ex_ac=ex_ac):
                ctx.w(v, "ac", ex_ac)
                ctx.inc(v.reduce_sum(wcol[:, 1:2], ex8[:], axis=AX.X), "dv")
                ctx.w(v, "dv")
                ctx.inc(v.reciprocal(wcol[:, 1:2], wcol[:, 1:2]), "dv")
                ctx.w(v, "dv")
                ctx.inc(v.tensor_scalar_mul(wcol[:, 2:3], wcol[:, 1:2], 2.0 / 3.0), "dv")
                ctx.w(v, "dv")
                v.tensor_single_scalar(rt1[:], sc_sb[:], mx8[:, 0:1], ALU.is_equal)
                ctx.inc(v.tensor_single_scalar(rt2[:], sc_sb[:], mx8[:, 1:2], ALU.is_equal), "dv")
                ctx.w(v, "dv")
                v.tensor_scalar_mul(rt1[:], rt1[:], wcol[:, 2:3])
                ctx.inc(v.tensor_scalar(rt2[:], rt2[:], wcol[:, 2:3], wcol[:, 3:4],
                                        ALU.mult, ALU.mult), "dv")
                ctx.w(v, "dv")
                ctx.inc(v.tensor_add(wE_sb[:, par, tt, :], rt1[:], rt2[:]), "dv")
        st["wE"][par] = ctx.cnt["dv"]

    def emit_expert(g, e, ld_marks):
        par = g % 2
        eslot = (e + 1) % 2
        ld_dl = ld_marks
        gpair = (g * NEX + (e + 1)) * NPAIR

        ac0 = ctx.cnt["ac"]
        silu_marks = []
        zfree = st["zfree"]
        for f in range(NF):
            pair, sub = f // 2, f % 2
            slot = (gpair + pair) % 4
            if sub == 0:
                gp_next = gpair + pair + 3
                if gp_next < NG * NEX * NPAIR:
                    emit_w13_load(gp_next)

            z1free = (ac0 + f) if f >= 1 else zfree[0]
            z3free = st["zfree"][1] if f == 0 else mult_dv

            @ctx.TE
            def _(te, par=par, f=f, slot=slot, sub=sub, z1free=z1free, z3free=z3free,
                  w13f=st["w13da"][slot], xnfm_dv=st["xnfm"][par]):
                ctx.w(te, "da", w13f)
                ctx.w(te, "dv", xnfm_dv)
                if z1free:
                    te.wait_ge(ctx.sem["ac"], z1free)
                for ko in range(KO):
                    ins = te.matmul(Z[:, 0, :],
                                    w13_sb[:, slot, ko, sub * 128:(sub + 1) * 128],
                                    xn_fm[:, par, ko, :],
                                    start=(ko == 0), stop=(ko == KO - 1))
                ctx.inc(ins, "pe")
                if z3free:
                    ctx.w(te, "dv", z3free)
                for ko in range(KO):
                    ins = te.matmul(Z[:, 1, :],
                                    w13_sb[:, slot, ko, 256 + sub * 128:256 + (sub + 1) * 128],
                                    xn_fm[:, par, ko, :],
                                    start=(ko == 0), stop=(ko == KO - 1))
                ctx.inc(ins, "pe")
            z_pe = ctx.cnt["pe"]
            silu_marks.append((f, z_pe - 1, 0 if f == 0 else mult_dv))
            if sub == 1:
                st["w13free"][slot] = z_pe

            @ctx.V
            def _(v, f=f, z_pe=z_pe, sac=ac0 + f + 1):
                v.wait_ge(ctx.sem["ac"], sac)
                ctx.w(v, "pe", z_pe)
                ctx.inc(v.tensor_tensor(h_sb[:, f, :], Z[:, 1, :], s1_sb[:], ALU.mult), "dv")
            mult_dv = ctx.cnt["dv"]

        @ctx.A
        def _(a, marks=tuple(silu_marks), s1f=st["zfree"][1]):
            for f, z1pe, mprev in marks:
                a.wait_ge(ctx.sem["pe"], z1pe)
                mwait = mprev if f > 0 else s1f
                if mwait:
                    a.wait_ge(ctx.sem["dv"], mwait)
                ctx.inc(a.activation(s1_sb[:], Z[:, 0, :], AF.Silu), "ac")
        assert ctx.cnt["ac"] == ac0 + NF
        st["zfree"] = (ac0 + NF, mult_dv)
        h_dv = mult_dv

        for tt in range(TPG):
            slot = tt % 2

            @ctx.TE
            def _(te, par=par, tt=tt, slot=slot, eslot=eslot, h_dv=h_dv, ld_dl=ld_dl,
                  yfree=st["yfree"][slot], accinit=st["accinit"][par]):
                ctx.w(te, "dv", h_dv)
                ctx.w(te, "dl", ld_dl)
                ctx.w(te, "dv", yfree)
                for n in range(2):
                    for f in range(NF):
                        ins = te.matmul(Y[:, slot, n, :], h_sb[:, f, tt * 128:(tt + 1) * 128],
                                        w2_sb[:, eslot, f, n * 512:(n + 1) * 512],
                                        start=(f == 0), stop=(f == NF - 1))
                ctx.inc(ins, "pe")
            y_pe = ctx.cnt["pe"]

            @ctx.V
            def _(v, par=par, tt=tt, e=e, slot=slot, y_pe=y_pe,
                  accinit=st["accinit"][par], accfree=st["accfree"][par]):
                ctx.w(v, "pe", y_pe)
                yflat = Y[:, slot].rearrange("p a b -> p (a b)")
                if e < 0:
                    ctx.w(v, "dv", accfree)
                    ctx.inc(v.tensor_scalar_mul(acc[:, par, tt, :], yflat, 1.0 / 3.0), "dv")
                else:
                    ctx.w(v, "dv", accinit)
                    ctx.inc(v.scalar_tensor_tensor(acc[:, par, tt, :], yflat,
                                                   wE_sb[:, par, tt, e:e + 1], acc[:, par, tt, :],
                                                   ALU.mult, ALU.add), "dv")
            st["yfree"][slot] = ctx.cnt["dv"]
        if e < 0:
            st["accinit"][par] = ctx.cnt["dv"]
        st["w2free"][eslot] = ctx.cnt["pe"]

    def emit_eload(e):
        eslot = (e + 1) % 2

        @ctx.S
        def _(s, e=e, eslot=eslot, w2f=st["w2free"][eslot]):
            if w2f:
                s.wait_ge(ctx.sem["pe"], w2f)
            for fr in range(NF):
                src = W["ws2b"][fr * 128:(fr + 1) * 128, :] if e < 0 else \
                    W["w2b"][e * 2048 + fr * 128:e * 2048 + (fr + 1) * 128, :]
                s.dma_start(out=w2_sb[:, eslot, fr, :], in_=src).then_inc(ctx.sem["dl"], 16)
                ctx.cnt["dl"] += 16
        return ctx.cnt["dl"]

    def emit_E(g):
        par = g % 2
        for tt in range(TPG):
            t = g * TPG + tt
            sl = tt % 2

            @ctx.S
            def _(s, t=t, po=st["xo"][0]):
                ctx.w(s, "dv", po)
                ctx.inc(s.dma_start(out=xo[:, 0, :], in_=xin[t * 128:(t + 1) * 128, :]), "dl", 16)
            xo_dl = ctx.cnt["dl"]

            @ctx.V
            def _(v, xo_dl=xo_dl):
                ctx.w(v, "dl", xo_dl)
                ctx.w(v, "dv")
            ln_stats(ctx, bn12[:, sl, :], mr[:, sl, :], acc[:, par, tt, 0:512],
                     acc[:, par, tt, 512:1024], cst.eps5_col[:])
            ac_rstd = ctx.cnt["ac"]

            @ctx.V
            def _(v, par=par, tt=tt, sl=sl, ac_rstd=ac_rstd, pstore=st["store"][0]):
                ctx.w(v, "ac", ac_rstd)
                ctx.inc(v.tensor_scalar(acc[:, par, tt, :], acc[:, par, tt, :], mr[:, sl, 0:1],
                                        mr[:, sl, 2:3], ALU.subtract, ALU.mult), "dv")
                ctx.w(v, "dv")
                ctx.w(v, "ds", pstore)
                ctx.inc(v.tensor_add(xnew[:, 0, :], acc[:, par, tt, :], xo[:, 0, :]), "dv")
            st["xo"][0] = ctx.cnt["dv"]
            res_dv = ctx.cnt["dv"]

            @ctx.S
            def _(s, t=t, res_dv=res_dv):
                ctx.w(s, "dv", res_dv)
                ctx.inc(s.dma_start(out=xout[t * 128:(t + 1) * 128, :], in_=xnew[:, 0, :]), "ds", 16)
            st["store"][0] = ctx.cnt["ds"]
        st["accfree"][par] = ctx.cnt["dv"]
        st["xnfree"][par] = ctx.cnt["dv"]

    emit_AB(0)
    ld_marks = emit_eload(-1)
    for g in range(NG):
        for e in range(-1, NE):
            cur_marks = ld_marks
            emit_expert(g, e, cur_marks)
            if e == -1 and g > 0:
                emit_E(g - 1)
            if e + 1 < NE:
                ld_marks = emit_eload(e + 1)
            elif g + 1 < NG:
                ld_marks = emit_eload(-1)
            if e == 3 and g + 1 < NG:
                emit_AB(g + 1)
    emit_E(NG - 1)

    return ps


def finish(ctx):
    @ctx.S
    def _(s):
        ctx.w(s, "ds")


def build_l0_attn_program():
    nc = bass.Bass("TRN2", target_bir_lowering=False, debug=False)
    xin = nc.dram_tensor("x_in", [T, D], F32, kind="ExternalInput").ap()
    xout = nc.dram_tensor("x_out", [T, D], F32, kind="ExternalOutput").ap()
    W = {
        "wq": nc.dram_tensor("wq", [1024, 1024], F32, kind="ExternalInput").ap(),
        "wk_down": nc.dram_tensor("wk_down", [1024, 64], F32, kind="ExternalInput").ap(),
        "wv_down": nc.dram_tensor("wv_down", [1024, 64], F32, kind="ExternalInput").ap(),
        "wk_up": nc.dram_tensor("wk_up", [32, 128], F32, kind="ExternalInput").ap(),
        "wv_up": nc.dram_tensor("wv_up", [32, 128], F32, kind="ExternalInput").ap(),
        "wo": nc.dram_tensor("wo", [1024, 1024], F32, kind="ExternalInput").ap(),
    }
    with ExitStack() as es:
        ctx = Ctx(nc, es)
        cst = Consts(ctx)
        ph = emit_l0_attn(ctx, cst, xin, xout, W)
        finish(ctx)
        ph.close()
    return nc


def build_l0_ffn_program():
    nc = bass.Bass("TRN2", target_bir_lowering=False, debug=False)
    xin = nc.dram_tensor("x_in", [T, D], F32, kind="ExternalInput").ap()
    xout = nc.dram_tensor("x_out", [T, D], F32, kind="ExternalOutput").ap()
    W = {
        "gate_w": nc.dram_tensor("gate_w", [1024, 8], F32, kind="ExternalInput").ap(),
        "w1": nc.dram_tensor("w1", [8, 1024, 2048], F32, kind="ExternalInput").ap(),
        "b1": nc.dram_tensor("b1", [8, 2048], F32, kind="ExternalInput").ap(),
        "w2": nc.dram_tensor("w2", [8, 2048, 1024], F32, kind="ExternalInput").ap(),
        "b2": nc.dram_tensor("b2", [8, 1024], F32, kind="ExternalInput").ap(),
        "ln_g": nc.dram_tensor("ln_g", [8, 1024], F32, kind="ExternalInput").ap(),
        "ln_b": nc.dram_tensor("ln_b", [8, 1024], F32, kind="ExternalInput").ap(),
    }
    W["w1b"] = nc.dram_tensor("w1b", [8 * 1024, 2048], BF16, kind="Internal").ap()
    W["w2b"] = nc.dram_tensor("w2b", [8 * 2048, 1024], BF16, kind="Internal").ap()
    with ExitStack() as es:
        ctx = Ctx(nc, es)
        cst = Consts(ctx)
        pc, _pce, _pct = emit_precast(ctx, [(W["w1"].rearrange("e k f -> (e k) f"), W["w1b"], "pairtile"),
                                (W["w2"].rearrange("e k f -> (e k) f"), W["w2b"], "flat")])
        _pce(_pct)
        emit_barrier(ctx)
        pc.close()
        ph = emit_l0_ffn(ctx, cst, xin, xout, W)
        finish(ctx)
        ph.close()
    return nc


def build_l1_attn_program():
    nc = bass.Bass("TRN2", target_bir_lowering=False, debug=False)
    xin = nc.dram_tensor("x_in", [T, D], F32, kind="ExternalInput").ap()
    xout = nc.dram_tensor("x_out", [T, D], F32, kind="ExternalOutput").ap()
    W = {}
    for nm in ("Wq", "Wk", "Wv", "Wg", "Wo"):
        W[nm] = nc.dram_tensor(nm, [1024, 1024], F32, kind="ExternalInput").ap()
    W["Wgt"] = nc.dram_tensor("Wgt", [1024, 16], F32, kind="ExternalInput").ap()
    with ExitStack() as es:
        ctx = Ctx(nc, es)
        cst = Consts(ctx)
        ph = emit_l1_attn(ctx, cst, xin, xout, W)
        finish(ctx)
        ph.close()
    return nc


def build_l1_ffn_program():
    nc = bass.Bass("TRN2", target_bir_lowering=False, debug=False)
    xin = nc.dram_tensor("x_in", [T, D], F32, kind="ExternalInput").ap()
    xout = nc.dram_tensor("x_out", [T, D], F32, kind="ExternalOutput").ap()
    W = {
        "w_router": nc.dram_tensor("w_router", [1024, 8], F32, kind="ExternalInput").ap(),
        "w1": nc.dram_tensor("w1", [8, 1024, 2048], F32, kind="ExternalInput").ap(),
        "w3": nc.dram_tensor("w3", [8, 1024, 2048], F32, kind="ExternalInput").ap(),
        "w2": nc.dram_tensor("w2", [8, 2048, 1024], F32, kind="ExternalInput").ap(),
        "ws1": nc.dram_tensor("ws1", [1024, 2048], F32, kind="ExternalInput").ap(),
        "ws3": nc.dram_tensor("ws3", [1024, 2048], F32, kind="ExternalInput").ap(),
        "ws2": nc.dram_tensor("ws2", [2048, 1024], F32, kind="ExternalInput").ap(),
    }
    W["w1b"] = nc.dram_tensor("m_w1b", [8 * 1024, 2048], BF16, kind="Internal").ap()
    W["w3b"] = nc.dram_tensor("m_w3b", [8 * 1024, 2048], BF16, kind="Internal").ap()
    W["w2b"] = nc.dram_tensor("m_w2b", [8 * 2048, 1024], BF16, kind="Internal").ap()
    W["ws1b"] = nc.dram_tensor("m_ws1b", [1024, 2048], BF16, kind="Internal").ap()
    W["ws3b"] = nc.dram_tensor("m_ws3b", [1024, 2048], BF16, kind="Internal").ap()
    W["ws2b"] = nc.dram_tensor("m_ws2b", [2048, 1024], BF16, kind="Internal").ap()
    with ExitStack() as es:
        ctx = Ctx(nc, es)
        cst = Consts(ctx)
        pc, _pce, _pct = emit_precast(ctx, [
            (W["w1"].rearrange("e k f -> (e k) f"), W["w1b"], "pairtile"),
            (W["w3"].rearrange("e k f -> (e k) f"), W["w3b"], "pairtile"),
            (W["w2"].rearrange("e k f -> (e k) f"), W["w2b"], "flat"),
            (W["ws1"], W["ws1b"], "pairtile"),
            (W["ws3"], W["ws3b"], "pairtile"),
            (W["ws2"], W["ws2b"], "flat")])
        _pce(_pct)
        emit_barrier(ctx)
        pc.close()
        ph = emit_l1_ffn(ctx, cst, xin, xout, W)
        finish(ctx)
        ph.close()
    return nc


def emit_barrier(ctx):
    marks = [(n, ctx.cnt[n]) for n in ("dl", "dg", "ds", "da", "pe", "dv", "ac", "po")
             if ctx.cnt[n] > 0]
    for sec in (ctx.S, ctx.G, ctx.TE, ctx.V, ctx.A):
        @sec
        def _(e, marks=marks):
            for n, v in marks:
                e.wait_ge(ctx.sem[n], v)


def build_full_program():
    nc = bass.Bass("TRN2", target_bir_lowering=False, debug=False)
    x_in = nc.dram_tensor("L0_attn_x", [T, D], F32, kind="ExternalInput").ap()
    x_out = nc.dram_tensor("x_out", [T, D], F32, kind="ExternalOutput").ap()
    xs1 = nc.dram_tensor("xs1", [T, D], F32, kind="Internal").ap()
    xs2 = nc.dram_tensor("xs2", [T, D], F32, kind="Internal").ap()

    def din(name, shape):
        return nc.dram_tensor(name, shape, F32, kind="ExternalInput").ap()

    W0a = {"wq": din("L0_attn_wq", [1024, 1024]),
           "wk_down": din("L0_attn_wk_down", [1024, 64]),
           "wv_down": din("L0_attn_wv_down", [1024, 64]),
           "wk_up": din("L0_attn_wk_up", [32, 128]),
           "wv_up": din("L0_attn_wv_up", [32, 128]),
           "wo": din("L0_attn_wo", [1024, 1024])}
    W0f = {"gate_w": din("L0_ffn_gate_w", [1024, 8]),
           "w1": din("L0_ffn_w1", [8, 1024, 2048]),
           "b1": din("L0_ffn_b1", [8, 2048]),
           "w2": din("L0_ffn_w2", [8, 2048, 1024]),
           "b2": din("L0_ffn_b2", [8, 1024]),
           "ln_g": din("L0_ffn_ln_g", [8, 1024]),
           "ln_b": din("L0_ffn_ln_b", [8, 1024])}
    W1a = {nm: din("L1_attn_" + nm, [1024, 1024]) for nm in ("Wq", "Wk", "Wv", "Wg", "Wo")}
    W1a["Wgt"] = din("L1_attn_Wgt", [1024, 16])
    W1f = {"w_router": din("L1_ffn_w_router", [1024, 8]),
           "w1": din("L1_ffn_w1", [8, 1024, 2048]),
           "w3": din("L1_ffn_w3", [8, 1024, 2048]),
           "w2": din("L1_ffn_w2", [8, 2048, 1024]),
           "ws1": din("L1_ffn_ws1", [1024, 2048]),
           "ws3": din("L1_ffn_ws3", [1024, 2048]),
           "ws2": din("L1_ffn_ws2", [2048, 1024])}

    W0f["w1b"] = nc.dram_tensor("f0_w1b", [8 * 1024, 2048], BF16, kind="Internal").ap()
    W0f["w2b"] = nc.dram_tensor("f0_w2b", [8 * 2048, 1024], BF16, kind="Internal").ap()
    W1f["w1b"] = nc.dram_tensor("m_w1b", [8 * 1024, 2048], BF16, kind="Internal").ap()
    W1f["w3b"] = nc.dram_tensor("m_w3b", [8 * 1024, 2048], BF16, kind="Internal").ap()
    W1f["w2b"] = nc.dram_tensor("m_w2b", [8 * 2048, 1024], BF16, kind="Internal").ap()
    W1f["ws1b"] = nc.dram_tensor("m_ws1b", [1024, 2048], BF16, kind="Internal").ap()
    W1f["ws3b"] = nc.dram_tensor("m_ws3b", [1024, 2048], BF16, kind="Internal").ap()
    W1f["ws2b"] = nc.dram_tensor("m_ws2b", [2048, 1024], BF16, kind="Internal").ap()
    with ExitStack() as es:
        ctx = Ctx(nc, es)
        cst = Consts(ctx)
        pc, pc_emit, pc_total = emit_precast(ctx, [
            (W0f["w1"].rearrange("e k f -> (e k) f"), W0f["w1b"], "pairtile"),
            (W0f["w2"].rearrange("e k f -> (e k) f"), W0f["w2b"], "flat"),
            (W1f["w1"].rearrange("e k f -> (e k) f"), W1f["w1b"], "pairtile"),
            (W1f["w3"].rearrange("e k f -> (e k) f"), W1f["w3b"], "pairtile"),
            (W1f["w2"].rearrange("e k f -> (e k) f"), W1f["w2b"], "flat"),
            (W1f["ws1"], W1f["ws1b"], "pairtile"),
            (W1f["ws3"], W1f["ws3b"], "pairtile"),
            (W1f["ws2"], W1f["ws2b"], "flat")])
        per_win = -(-pc_total // (NW - 1))
        ph = emit_l0_attn(ctx, cst, x_in, xs1, W0a,
                          between=lambda c: pc_emit(per_win))
        pc_emit(pc_total)
        pc_ds = ctx.cnt["ds"]
        emit_barrier(ctx)
        ph.close()
        pc.close()
        for sec in (ctx.S, ctx.G, ctx.A):
            @sec
            def _(e, pc_ds=pc_ds):
                e.wait_ge(ctx.sem["ds"], pc_ds)
        ph = emit_l0_ffn(ctx, cst, xs1, xs2, W0f)
        emit_barrier(ctx)
        ph.close()
        ph = emit_l1_attn(ctx, cst, xs2, xs1, W1a)
        emit_barrier(ctx)
        ph.close()
        ph = emit_l1_ffn(ctx, cst, xs1, x_out, W1f)
        finish(ctx)
        ph.close()
    return nc


_PHASE_SPECS = [
    ("l0_attn", build_l0_attn_program,
     {"wq": "L0_attn_wq", "wk_down": "L0_attn_wk_down", "wv_down": "L0_attn_wv_down",
      "wk_up": "L0_attn_wk_up", "wv_up": "L0_attn_wv_up", "wo": "L0_attn_wo"}),
    ("l0_ffn", build_l0_ffn_program,
     {"gate_w": "L0_ffn_gate_w", "w1": "L0_ffn_w1", "b1": "L0_ffn_b1",
      "w2": "L0_ffn_w2", "b2": "L0_ffn_b2", "ln_g": "L0_ffn_ln_g", "ln_b": "L0_ffn_ln_b"}),
    ("l1_attn", build_l1_attn_program,
     {"Wq": "L1_attn_Wq", "Wk": "L1_attn_Wk", "Wv": "L1_attn_Wv",
      "Wg": "L1_attn_Wg", "Wgt": "L1_attn_Wgt", "Wo": "L1_attn_Wo"}),
    ("l1_ffn", build_l1_ffn_program,
     {"w_router": "L1_ffn_w_router", "w1": "L1_ffn_w1", "w3": "L1_ffn_w3",
      "w2": "L1_ffn_w2", "ws1": "L1_ffn_ws1", "ws3": "L1_ffn_ws3", "ws2": "L1_ffn_ws2"}),
]


def kernel(**inputs):
    x = np.ascontiguousarray(np.asarray(inputs["L0_attn_x"], dtype=np.float32))
    B = x.shape[0]
    nc = build_full_program()
    wvals = {k: np.ascontiguousarray(np.asarray(v, dtype=np.float32))
             for k, v in inputs.items() if k != "L0_attn_x"}
    maps = []
    for c in range(B):
        m = {"L0_attn_x": np.ascontiguousarray(x[c])}
        m.update(wvals)
        maps.append(m)
    res = run_bass_kernel_spmd(nc, maps, core_ids=list(range(B)))
    return np.stack([r["x_out"] for r in res.results])


# version 18 (final)
# speedup vs baseline: 1.1407x; 1.1407x over previous
import numpy as np
from contextlib import ExitStack
import concourse.bass as bass
import concourse.mybir as mybir
from concourse.bass_utils import run_bass_kernel_spmd

F32 = mybir.dt.float32
BF16 = mybir.dt.bfloat16
AF = mybir.ActivationFunctionType
ALU = mybir.AluOpType
AX = mybir.AxisListType

T = 4096
D = 1024
KO = 8
NT = 32
NW = 8
TPW = 4
EPS_OUT = 1e-5
S_ATT = 1.0 / np.sqrt(128.0)


class Ctx:

    def __init__(self, nc, es):
        self.nc = nc
        self.es = es
        self.block = es.enter_context(nc.Block())
        self.sem = {}
        self.cnt = {}
        for n in ["dl", "dg", "ds", "da", "pe", "dv", "ac", "po"]:
            self.sem[n] = es.enter_context(nc.semaphore(n))
            self.cnt[n] = 0

    def inc(self, inst, name, k=1):
        inst.then_inc(self.sem[name], k)
        self.cnt[name] += k
        return self.cnt[name]

    def w(self, eng, name, v=None):
        vv = self.cnt[name] if v is None else v
        if vv > 0:
            eng.wait_ge(self.sem[name], vv)

    def S(self, f): self.block.sync(f)
    def G(self, f): self.block.gpsimd(f)
    def TE(self, f): self.block.tensor(f)
    def V(self, f): self.block.vector(f)
    def A(self, f): self.block.scalar(f)


class Consts:

    def __init__(self, ctx):
        nc, es = ctx.nc, ctx.es
        self.ident_bf = es.enter_context(nc.sbuf_tensor("ident_bf", [128, 128], BF16))
        self.ident_f = es.enter_context(nc.sbuf_tensor("ident_f", [128, 128], F32))
        self.triu_bf = es.enter_context(nc.sbuf_tensor("triu_bf", [128, 128], BF16))
        self.triu_f = es.enter_context(nc.sbuf_tensor("triu_f", [128, 128], F32))
        self.ones128_f = es.enter_context(nc.sbuf_tensor("ones128_f", [128, 128], F32))
        self.ones_col_bf = es.enter_context(nc.sbuf_tensor("ones_col_bf", [128, 1], BF16))
        self.ones_col_f = es.enter_context(nc.sbuf_tensor("ones_col_f", [128, 1], F32))
        self.ones_row_bf = es.enter_context(nc.sbuf_tensor("ones_row_bf", [1, 128], BF16))
        self.negones_row_f = es.enter_context(nc.sbuf_tensor("negones_row_f", [1, 128], F32))
        self.ones_row_f = es.enter_context(nc.sbuf_tensor("ones_row_f", [1, 128], F32))
        self.eps5_col = es.enter_context(nc.sbuf_tensor("eps5_col", [128, 1], F32))
        self.eps6_col = es.enter_context(nc.sbuf_tensor("eps6_col", [128, 1], F32))
        self.lnS_col = es.enter_context(nc.sbuf_tensor("lnS_col", [128, 1], F32))
        ident_bf, ident_f, triu_bf, triu_f = (
            self.ident_bf, self.ident_f, self.triu_bf, self.triu_f)
        ones128_f, ones_col_bf, ones_col_f, ones_row_bf, negones_row_f, ones_row_f = (
            self.ones128_f, self.ones_col_bf, self.ones_col_f, self.ones_row_bf,
            self.negones_row_f, self.ones_row_f)

        @ctx.G
        def _(g):
            for idt in (ident_bf, ident_f):
                g.memset(idt[:], 1.0)
                g.affine_select(out=idt[:], in_=idt[:], compare_op=ALU.is_equal,
                                fill=0.0, base=0, pattern=[[-1, 128]], channel_multiplier=1)
            for tri in (triu_bf, triu_f):
                g.memset(tri[:], 1.0)
                g.affine_select(out=tri[:], in_=tri[:], compare_op=ALU.is_ge,
                                fill=0.0, base=0, pattern=[[1, 128]], channel_multiplier=-1)
            g.memset(ones128_f[:], 1.0)
            g.memset(ones_col_bf[:], 1.0)
            g.memset(ones_col_f[:], 1.0)
            g.memset(ones_row_bf[:], 1.0)
            g.memset(negones_row_f[:], -1.0)
            g.memset(self.eps5_col[:], 1e-5)
            g.memset(self.eps6_col[:], 1e-6)
            g.memset(self.lnS_col[:], float(np.log(S_ATT)))
            last = g.memset(ones_row_f[:], 1.0)
            ctx.inc(last, "po")
        self.ready = ctx.cnt["po"]


def ln_stats(ctx, bn12, mr, src_a, src_b, eps_col):

    @ctx.V
    def _(v):
        v.bn_stats(bn12[:, 0:6], src_a)
        ctx.inc(v.bn_stats(bn12[:, 6:12], src_b), "dv")
        ctx.w(v, "dv")
        ctx.inc(v.bn_aggr(mr[:, 0:2], bn12[:]), "dv")
    dvv = ctx.cnt["dv"]

    @ctx.A
    def _(a):
        ctx.w(a, "dv", dvv)
        ctx.inc(a.activation(mr[:, 3:4], mr[:, 1:2], AF.Ln, bias=eps_col), "ac")
        ctx.w(a, "ac")
        ctx.inc(a.activation(mr[:, 2:3], mr[:, 3:4], AF.Exp, scale=-0.5), "ac")
    return ctx.cnt["dv"], ctx.cnt["ac"]


def emit_l0_attn(ctx, cst, xin, xout, W, between=None):
    nc, es = ctx.nc, ctx.es
    ps = ExitStack()
    wq_sb = ps.enter_context(nc.sbuf_tensor("a_wq", [128, KO, 1024], BF16))
    wkv_sb = ps.enter_context(nc.sbuf_tensor("a_wkv", [128, KO, 4, 32], BF16))
    upk_sb = ps.enter_context(nc.sbuf_tensor("a_upk", [32, 128], BF16))
    upv_sb = ps.enter_context(nc.sbuf_tensor("a_upv", [32, 128], BF16))
    wo_sb = ps.enter_context(nc.sbuf_tensor("a_wo", [128, KO, 1024], BF16))
    xt = ps.enter_context(nc.sbuf_tensor("a_xt", [128, 2, TPW, 1024], F32))
    bn12 = ps.enter_context(nc.sbuf_tensor("a_bn12", [128, TPW, 12], F32))
    mr = ps.enter_context(nc.sbuf_tensor("a_mr", [128, TPW, 4], F32))
    xn_tm = ps.enter_context(nc.sbuf_tensor("a_xntm", [128, TPW, 1024], BF16))
    xn_fm = ps.enter_context(nc.sbuf_tensor("a_xnfm", [128, KO, 512], BF16))
    q_sb = ps.enter_context(nc.sbuf_tensor("a_q", [128, 8, 512], BF16))
    klat_sb = [ps.enter_context(nc.sbuf_tensor(f"a_klat{h}", [32, T], BF16)) for h in range(2)]
    vlat_sb = [ps.enter_context(nc.sbuf_tensor(f"a_vlat{h}", [32, T], BF16)) for h in range(2)]
    k_sb = ps.enter_context(nc.sbuf_tensor("a_k", [128, 2, T], BF16))
    v_sb = ps.enter_context(nc.sbuf_tensor("a_v", [128, NT, 256], BF16))
    sq_sb = ps.enter_context(nc.sbuf_tensor("a_sq", [128, 512], BF16))
    nrm = ps.enter_context(nc.sbuf_tensor("a_nrm", [1, 32], F32))
    negM = ps.enter_context(nc.sbuf_tensor("a_negM", [128, 8], F32))
    PT = ps.enter_context(nc.sbuf_tensor("a_PT", [128, 2, 2, 512], BF16))
    l_sb = ps.enter_context(nc.sbuf_tensor("a_l", [1, 512], F32))
    rl_sb = ps.enter_context(nc.sbuf_tensor("a_rl", [1, 512], F32))
    rlb_sb = ps.enter_context(nc.sbuf_tensor("a_rlb", [128, 512], F32))
    y_sb = ps.enter_context(nc.sbuf_tensor("a_y", [128, 8, 512], BF16))
    u_sb = ps.enter_context(nc.sbuf_tensor("a_u", [128, 1024], F32))
    xnew = ps.enter_context(nc.sbuf_tensor("a_xnew", [128, 2, 1024], F32))
    A_ps = ps.enter_context(nc.psum_tensor("a_A", [128, 4, 512], F32))
    B_ps = ps.enter_context(nc.psum_tensor("a_B", [128, 8, 128], BF16))
    y_ps = ps.enter_context(nc.psum_tensor("a_yps", [128, 512], F32))
    l_ps = ps.enter_context(nc.psum_tensor("a_lps", [128, 512], F32))
    r_ps = ps.enter_context(nc.psum_tensor("a_rps", [128, 512], F32))

    @ctx.G
    def _(g):
        for ko in range(KO):
            g.dma_start(out=wq_sb[:, ko, :], in_=W["wq"][ko * 128:(ko + 1) * 128, :]).then_inc(ctx.sem["dg"], 16)
            ctx.cnt["dg"] += 16
            g.dma_start(out=wkv_sb[:, ko, 0, :],
                        in_=W["wk_down"][ko * 128:(ko + 1) * 128, 0:32]).then_inc(ctx.sem["dg"], 16)
            g.dma_start(out=wkv_sb[:, ko, 1, :],
                        in_=W["wv_down"][ko * 128:(ko + 1) * 128, 0:32]).then_inc(ctx.sem["dg"], 16)
            g.dma_start(out=wkv_sb[:, ko, 2, :],
                        in_=W["wk_down"][ko * 128:(ko + 1) * 128, 32:64]).then_inc(ctx.sem["dg"], 16)
            g.dma_start(out=wkv_sb[:, ko, 3, :],
                        in_=W["wv_down"][ko * 128:(ko + 1) * 128, 32:64]).then_inc(ctx.sem["dg"], 16)
            ctx.cnt["dg"] += 64
        g.dma_start(out=upk_sb[:], in_=W["wk_up"][:]).then_inc(ctx.sem["dg"], 16)
        g.dma_start(out=upv_sb[:], in_=W["wv_up"][:]).then_inc(ctx.sem["dg"], 16)
        ctx.cnt["dg"] += 32
        for ko in range(KO):
            g.dma_start(out=wo_sb[:, ko, :], in_=W["wo"][ko * 128:(ko + 1) * 128, :]).then_inc(ctx.sem["dg"], 16)
            ctx.cnt["dg"] += 16
    wts_loaded = ctx.cnt["dg"]

    prev_resid = [[0] * TPW, [0] * TPW]
    prev_store = [0, 0]
    prev_vevac = 0
    prev_yevac = 0
    wo_u2 = [0, 0]
    prev_expc = 0
    prev_avpe = 0
    prev_tr_evac = [0]
    xnfm_mark = {}

    def s1_tile(c, i):
        t = 4 * c + i
        par = c % 2

        @ctx.S
        def _(s, t=t, par=par, i=i, pr=prev_resid[par][i]):
            ctx.w(s, "dv", pr)
            ctx.inc(s.dma_start(out=xt[:, par, i, :], in_=xin[t * 128:(t + 1) * 128, :]), "dl", 16)
        dl_here = ctx.cnt["dl"]

        @ctx.V
        def _(v, dl_here=dl_here):
            ctx.w(v, "dl", dl_here)
        ln_stats(ctx, bn12[:, i, :], mr[:, i, :], xt[:, par, i, 0:512],
                 xt[:, par, i, 512:1024], cst.eps5_col[:])
        ac_rstd = ctx.cnt["ac"]

        @ctx.V
        def _(v, par=par, i=i, ac_rstd=ac_rstd):
            ctx.w(v, "ac", ac_rstd)
            ctx.inc(v.tensor_scalar(xn_tm[:, i, :], xt[:, par, i, :], mr[:, i, 0:1], mr[:, i, 2:3],
                                    ALU.subtract, ALU.mult), "dv")
        apply_dv = ctx.cnt["dv"]

        @ctx.TE
        def _(te, i=i, apply_dv=apply_dv, pt=prev_tr_evac[0]):
            ctx.w(te, "dv", apply_dv)
            ctx.w(te, "dv", pt)
            ctx.w(te, "po", cst.ready)
            for ko in range(KO):
                ins = te.transpose(B_ps[:, ko, :], xn_tm[:, i, ko * 128:(ko + 1) * 128], cst.ident_bf[:])
            ctx.inc(ins, "pe")
        tr_pe = ctx.cnt["pe"]

        @ctx.V
        def _(v, i=i, tr_pe=tr_pe):
            ctx.w(v, "pe", tr_pe)
            ctx.inc(v.tensor_copy(xn_fm[:, :, i * 128:(i + 1) * 128], B_ps[:]), "dv")
        prev_tr_evac[0] = ctx.cnt["dv"]
        xnfm_mark[c] = ctx.cnt["dv"]

    HOIST = True
    for i in range(TPW):
        s1_tile(0, i)

    for c in range(NW):
        if between is not None and c > 0:
            between(c)
        if not HOIST and c > 0:
            for i in range(TPW):
                s1_tile(c, i)
        xnfm_ready = xnfm_mark[c]

        for mo in range(8):
            bank = mo % 2

            @ctx.TE
            def _(te, mo=mo, bank=bank, dv_now=ctx.cnt["dv"]):
                ctx.w(te, "dg", wts_loaded)
                ctx.w(te, "dv", dv_now)
                ctx.w(te, "ac")
                for ko in range(KO):
                    ins = te.matmul(A_ps[:, bank, :], wq_sb[:, ko, mo * 128:(mo + 1) * 128],
                                    xn_fm[:, ko, :], start=(ko == 0), stop=(ko == KO - 1))
                ctx.inc(ins, "pe")
            q_pe = ctx.cnt["pe"]

            @ctx.A
            def _(a, mo=mo, bank=bank, q_pe=q_pe):
                ctx.w(a, "pe", q_pe)
                ctx.w(a, "dv")
                a.activation(q_sb[:, mo, :], A_ps[:, bank, :], AF.Copy)
                ctx.inc(a.activation(sq_sb[:], A_ps[:, bank, :], AF.Square), "ac")
            sq_ac = ctx.cnt["ac"]

            @ctx.TE
            def _(te, sq_ac=sq_ac):
                ctx.w(te, "ac", sq_ac)
                ctx.w(te, "dv")
                ctx.inc(te.matmul(l_ps[0:1, :], cst.ones_col_bf[:], sq_sb[:],
                                  start=True, stop=True), "pe")
            qn_pe = ctx.cnt["pe"]

            @ctx.V
            def _(v, mo=mo, qn_pe=qn_pe):
                ctx.w(v, "pe", qn_pe)
                ctx.inc(v.reduce_max(nrm[0:1, mo:mo + 1], l_ps[0:1, :], axis=AX.X), "dv")

        kvlat_out = [A_ps[0:32, 2, :], A_ps[32:64, 2, :], A_ps[64:96, 2, :], A_ps[0:32, 3, :]]
        kvlat_dst = [klat_sb[0], vlat_sb[0], klat_sb[1], vlat_sb[1]]

        @ctx.TE
        def _(te):
            ctx.w(te, "ac")
            ctx.w(te, "dv")
            for gg in range(4):
                for ko in range(KO):
                    ins = te.matmul(kvlat_out[gg], wkv_sb[:, ko, gg, :], xn_fm[:, ko, :],
                                    start=(ko == 0), stop=(ko == KO - 1))
            ctx.inc(ins, "pe")
        kv_pe = ctx.cnt["pe"]

        @ctx.A
        def _(a, c=c, kv_pe=kv_pe):
            ctx.w(a, "pe", kv_pe)
            for gg in range(4):
                ins = a.activation(kvlat_dst[gg][:, c * 512:(c + 1) * 512], kvlat_out[gg], AF.Copy)
            ctx.inc(ins, "ac")
        kvevac_ac = ctx.cnt["ac"]

        for h in range(2):
            @ctx.TE
            def _(te, h=h, c=c, kvevac_ac=kvevac_ac):
                ctx.w(te, "ac", kvevac_ac)
                ctx.w(te, "dv")
                ctx.inc(te.matmul(A_ps[:, h, :], upk_sb[:],
                                  klat_sb[h][:, c * 512:(c + 1) * 512],
                                  start=True, stop=True), "pe")
            kup_pe = ctx.cnt["pe"]

            @ctx.A
            def _(a, h=h, c=c, kup_pe=kup_pe):
                ctx.w(a, "pe", kup_pe)
                ctx.w(a, "dv")
                a.activation(k_sb[:, h, c * 512:(c + 1) * 512], A_ps[:, h, :], AF.Copy)
                ctx.inc(a.activation(sq_sb[:], A_ps[:, h, :], AF.Square), "ac")
            ks_ac = ctx.cnt["ac"]

            @ctx.TE
            def _(te, ks_ac=ks_ac):
                ctx.w(te, "ac", ks_ac)
                ctx.w(te, "dv")
                ctx.inc(te.matmul(l_ps[0:1, :], cst.ones_col_bf[:], sq_sb[:],
                                  start=True, stop=True), "pe")
            kn_pe = ctx.cnt["pe"]

            @ctx.V
            def _(v, h=h, c=c, kn_pe=kn_pe):
                ctx.w(v, "pe", kn_pe)
                ctx.inc(v.reduce_max(nrm[0:1, 12 + h:13 + h], l_ps[0:1, :], axis=AX.X), "dv")
                ctx.w(v, "dv")
                if c == 0:
                    ctx.inc(v.tensor_copy(nrm[0:1, 8 + h:9 + h], nrm[0:1, 12 + h:13 + h]), "dv")
                else:
                    ctx.inc(v.tensor_tensor(nrm[0:1, 8 + h:9 + h], nrm[0:1, 8 + h:9 + h],
                                            nrm[0:1, 12 + h:13 + h], ALU.max), "dv")

        for i in range(TPW):
            t = 4 * c + i
            for h in range(2):
                slot = (2 * i + h) % 3
                off = 128 + slot * 128

                @ctx.TE
                def _(te, t=t, h=h, off=off, pv=prev_vevac, kvevac_ac=kvevac_ac):
                    ctx.w(te, "dv", pv)
                    ctx.w(te, "ac", kvevac_ac)
                    ctx.inc(te.matmul(A_ps[:, 3, off:off + 128],
                                      vlat_sb[h][:, t * 128:(t + 1) * 128],
                                      upv_sb[:], start=True, stop=True), "pe")
                vt_pe = ctx.cnt["pe"]

                @ctx.V
                def _(v, t=t, h=h, off=off, vt_pe=vt_pe):
                    ctx.w(v, "pe", vt_pe)
                    ctx.inc(v.tensor_copy(v_sb[:, t, h * 128:(h + 1) * 128],
                                          A_ps[:, 3, off:off + 128]), "dv")
                prev_vevac = ctx.cnt["dv"]

        for h in range(8):
            kvh = h // 4

            @ctx.V
            def _(v, h=h, kvh=kvh):
                ctx.w(v, "dv")
                ctx.inc(v.tensor_tensor(nrm[0:1, 16 + h:17 + h], nrm[0:1, h:h + 1],
                                        nrm[0:1, 8 + kvh:9 + kvh], ALU.mult), "dv")
            pr_dv = ctx.cnt["dv"]

            @ctx.A
            def _(a, h=h, pr_dv=pr_dv):
                ctx.w(a, "dv", pr_dv)
                ctx.inc(a.activation(nrm[0:1, 16 + h:17 + h],
                                     nrm[0:1, 16 + h:17 + h], AF.Ln), "ac")
                ctx.w(a, "ac")
                ctx.inc(a.activation(nrm[0:1, 24 + h:25 + h], nrm[0:1, 16 + h:17 + h],
                                     AF.Exp, scale=0.5, bias=cst.lnS_col[0:1, :]), "ac")
            m_ac = ctx.cnt["ac"]

            @ctx.TE
            def _(te, h=h, m_ac=m_ac):
                ctx.w(te, "ac", m_ac)
                ctx.w(te, "dv")
                ctx.inc(te.matmul(l_ps[:, 0:1], cst.negones_row_f[:], nrm[0:1, 24 + h:25 + h],
                                  start=True, stop=True), "pe")
            nb_pe = ctx.cnt["pe"]

            @ctx.V
            def _(v, h=h, nb_pe=nb_pe):
                ctx.w(v, "pe", nb_pe)
                ctx.inc(v.tensor_copy(negM[:, h:h + 1], l_ps[:, 0:1]), "dv")
        negM_dv = ctx.cnt["dv"]

        for h in range(8):
            kvh = h // 4
            ng2 = 2 * (c + 1)
            ac0 = ctx.cnt["ac"]

            mask_marks = {}
            for g2 in range(ng2 - 2, ng2):
                @ctx.V
                def _(v, c=c, g2=g2, ac0=ac0):
                    v.wait_ge(ctx.sem["ac"], ac0 + g2 + 1)
                    for jj in range(2):
                        j = g2 * 2 + jj
                        r = j - 4 * c
                        if r > 0:
                            v.memset(PT[:, g2 % 2, jj, 0:r * 128], 0.0)
                        ins = v.tensor_tensor(PT[:, g2 % 2, jj, r * 128:(r + 1) * 128],
                                              PT[:, g2 % 2, jj, r * 128:(r + 1) * 128],
                                              cst.triu_bf[:], ALU.mult)
                    ctx.inc(ins, "dv")
                mask_marks[g2] = ctx.cnt["dv"]

            st_pe = {}
            avp = {}

            @ctx.TE
            def _(te, h=h, c=c, kvh=kvh, ng2=ng2, ac0=ac0, mask_marks=dict(mask_marks),
                  yfree=prev_yevac, expq=prev_expc):
                njt = 2 * ng2

                def avpair(g2):
                    te.wait_ge(ctx.sem["ac"], ac0 + g2 + 1)
                    if g2 in mask_marks:
                        te.wait_ge(ctx.sem["dv"], mask_marks[g2])
                    if g2 == 0:
                        ctx.w(te, "dv", yfree)
                    for jj in range(2):
                        j = g2 * 2 + jj
                        te.matmul(y_ps[:], v_sb[:, j, kvh * 128:(kvh + 1) * 128],
                                  PT[:, g2 % 2, jj, :], start=(j == 0), stop=(j == njt - 1))
                        ins = te.matmul(l_ps[0:1, :], cst.ones_col_bf[:],
                                        PT[:, g2 % 2, jj, :], start=(j == 0), stop=(j == njt - 1))
                    ctx.inc(ins, "pe")
                    avp[g2] = ctx.cnt["pe"]

                for g2 in range(ng2):
                    if g2 >= 2:
                        te.wait_ge(ctx.sem["ac"], ac0 + g2 - 1)
                    elif expq:
                        te.wait_ge(ctx.sem["ac"], expq)
                    for jj in range(2):
                        j = g2 * 2 + jj
                        ins = te.matmul(A_ps[:, (g2 % 2) * 2 + jj, :],
                                        k_sb[:, kvh, j * 128:(j + 1) * 128],
                                        q_sb[:, h, :], start=True, stop=True)
                    ctx.inc(ins, "pe")
                    st_pe[g2] = ctx.cnt["pe"]
                    if g2 >= 1:
                        avpair(g2 - 1)
                avpair(ng2 - 1)

            @ctx.A
            def _(a, h=h, ng2=ng2, ac0=ac0, st_pe=dict(st_pe), avp=dict(avp),
                  negM_dv=negM_dv, pav=prev_avpe):
                ctx.w(a, "dv", negM_dv)
                for g2 in range(ng2):
                    a.wait_ge(ctx.sem["pe"], st_pe[g2])
                    if g2 >= 2:
                        a.wait_ge(ctx.sem["pe"], avp[g2 - 2])
                    elif pav:
                        a.wait_ge(ctx.sem["pe"], pav)
                    sl = g2 % 2
                    ctx.inc(a.activation(PT[:, sl].rearrange("p a b -> p (a b)"),
                                         A_ps[:, sl * 2:(sl + 1) * 2].rearrange("p a b -> p (a b)"),
                                         AF.Exp, bias=negM[:, h:h + 1], scale=float(S_ATT)), "ac")
            assert ctx.cnt["ac"] == ac0 + ng2
            prev_expc = ctx.cnt["ac"]
            prev_avpe = avp[ng2 - 1]
            av_pe = prev_avpe

            @ctx.V
            def _(v, h=h, av_pe=av_pe):
                ctx.w(v, "pe", av_pe)
                ctx.inc(v.tensor_copy(l_sb[:], l_ps[0:1, :]), "dv")
                ctx.w(v, "dv")
                ctx.inc(v.reciprocal(rl_sb[:], l_sb[:]), "dv")
            rl_dv = ctx.cnt["dv"]

            @ctx.TE
            def _(te, rl_dv=rl_dv):
                ctx.w(te, "dv", rl_dv)
                ctx.inc(te.matmul(r_ps[:], cst.ones_row_f[:], rl_sb[:], start=True, stop=True), "pe")
            rb_pe = ctx.cnt["pe"]

            @ctx.V
            def _(v, h=h, rb_pe=rb_pe):
                ctx.w(v, "pe", rb_pe)
                ctx.inc(v.tensor_copy(rlb_sb[:], r_ps[:]), "dv")
                ctx.w(v, "dv")
                ctx.inc(v.tensor_tensor(y_sb[:, h, :], y_ps[:], rlb_sb[:], ALU.mult), "dv")
            prev_yevac = ctx.cnt["dv"]
            if HOIST and c + 1 < NW and 3 <= h <= 2 + TPW:
                s1_tile(c + 1, h - 3)
        y_ready = ctx.cnt["dv"]
        for i in range(TPW):
            t = 4 * c + i
            bp = (i % 2) * 2

            @ctx.TE
            def _(te, i=i, bp=bp, y_ready=y_ready, expc=prev_expc, wou=wo_u2[i % 2]):
                ctx.w(te, "dv", y_ready)
                ctx.w(te, "ac", expc)
                ctx.w(te, "dv", wou)
                for n in range(2):
                    for h in range(8):
                        ins = te.matmul(A_ps[:, bp + n, :], y_sb[:, h, i * 128:(i + 1) * 128],
                                        wo_sb[:, h, n * 512:(n + 1) * 512],
                                        start=(h == 0), stop=(h == 7))
                ctx.inc(ins, "pe")
            wo_pe = ctx.cnt["pe"]

            @ctx.V
            def _(v, wo_pe=wo_pe):
                ctx.w(v, "pe", wo_pe)
            _, _ = ln_stats(ctx, bn12[:, i, :], mr[:, i, :],
                            A_ps[:, bp, :], A_ps[:, bp + 1, :], cst.eps5_col[:])
            ac_rstd = ctx.cnt["ac"]

            @ctx.V
            def _(v, i=i, t=t, bp=bp, par=c % 2, ac_rstd=ac_rstd, slot=i % 2,
                  pstore=prev_store[i % 2]):
                ctx.w(v, "ac", ac_rstd)
                ctx.inc(v.tensor_scalar(u_sb[:], A_ps[:, bp:bp + 2, :].rearrange("p a b -> p (a b)"),
                                        mr[:, i, 0:1], mr[:, i, 2:3], ALU.subtract, ALU.mult), "dv")
                ctx.w(v, "dv")
                ctx.w(v, "ds", pstore)
                ctx.inc(v.tensor_add(xnew[:, slot, :], u_sb[:], xt[:, par, i, :]), "dv")
            wo_u2[i % 2] = ctx.cnt["dv"]
            prev_resid[c % 2][i] = ctx.cnt["dv"]
            res_dv = ctx.cnt["dv"]

            @ctx.S
            def _(s, t=t, slot=i % 2, res_dv=res_dv):
                ctx.w(s, "dv", res_dv)
                ctx.inc(s.dma_start(out=xout[t * 128:(t + 1) * 128, :], in_=xnew[:, slot, :]), "ds", 16)
            prev_store[i % 2] = ctx.cnt["ds"]

    return ps


def emit_precast(ctx, pairs):
    nc = ctx.nc
    ps = ExitStack()
    scr = ps.enter_context(nc.sbuf_tensor("pc_scr", [128, 4, 2048], BF16))
    slot_store = [0, 0, 0, 0]
    work = []
    for src, dst, mode in pairs:
        R, C = src.shape
        assert R % 128 == 0 and C <= 2048
        dstv = None
        if mode == "pairtile":
            assert R % 1024 == 0 and C == 2048
            dstv = dst.rearrange("(e p) c -> e (p c)", p=1024)
        for r in range(R // 128):
            work.append((src, dst, dstv, mode, r, C))
    pos = [0]

    def emit_some(k):
        end = min(pos[0] + k, len(work))
        while pos[0] < end:
            gi = pos[0]
            pos[0] += 1
            src, dst, dstv, mode, r, C = work[gi]
            slot = gi % 4

            @ctx.G
            def _(g, src=src, r=r, slot=slot, C=C, pst=slot_store[slot]):
                if pst:
                    g.wait_ge(ctx.sem["ds"], pst)
                g.dma_start(out=scr[:, slot, 0:C],
                            in_=src[r * 128:(r + 1) * 128, :]).then_inc(ctx.sem["dg"], 16)
                ctx.cnt["dg"] += 16
            ld = ctx.cnt["dg"]

            if mode == "flat":
                @ctx.S
                def _(s, dst=dst, r=r, slot=slot, C=C, ld=ld):
                    s.wait_ge(ctx.sem["dg"], ld)
                    ctx.inc(s.dma_start(out=dst[r * 128:(r + 1) * 128, :],
                                        in_=scr[:, slot, 0:C]), "ds", 16)
                slot_store[slot] = ctx.cnt["ds"]
            else:
                e, ko = r // 8, r % 8
                @ctx.S
                def _(s, e=e, ko=ko, slot=slot, ld=ld, dstv=dstv):
                    s.wait_ge(ctx.sem["dg"], ld)
                    out_ap = dstv[e:e + 1, :].rearrange(
                        "a (pair p k f) -> (a k) p pair f", pair=8, p=128, k=8, f=256)[ko]
                    in_ap = scr[:, slot, :].rearrange("p (pair f) -> p pair f", f=256)
                    ctx.inc(s.dma_start(out=out_ap, in_=in_ap), "ds", 16)
                slot_store[slot] = ctx.cnt["ds"]
    return ps, emit_some, len(work)


def emit_l0_ffn(ctx, cst, xin, xout, W):
    nc = ctx.nc
    NG, GT, TPG, NF, NE = 8, 512, 4, 16, 8
    ps = ExitStack()
    gate_sb = ps.enter_context(nc.sbuf_tensor("f_gate", [128, KO, 8], BF16))
    lnb_sb = ps.enter_context(nc.sbuf_tensor("f_lnb", [8, 1024], BF16))
    rows4 = ps.enter_context(nc.sbuf_tensor("f_rows4", [1, 4, 1024], BF16))
    b1t_sb = ps.enter_context(nc.sbuf_tensor("f_b1t", [16, 2, 128], F32))
    b1c_sb = ps.enter_context(nc.sbuf_tensor("f_b1c", [128, 2, 16], F32))

    w1_sb = ps.enter_context(nc.sbuf_tensor("f_w1", [128, 4, KO, 256], BF16))
    w2_sb = ps.enter_context(nc.sbuf_tensor("f_w2", [128, 2, NF, 1024], BF16))
    h_sb = ps.enter_context(nc.sbuf_tensor("f_h", [128, NF, GT], BF16))
    xt = ps.enter_context(nc.sbuf_tensor("f_xt", [128, 2, 1024], F32))
    xo = ps.enter_context(nc.sbuf_tensor("f_xo", [128, 1, 1024], F32))
    xnew = ps.enter_context(nc.sbuf_tensor("f_xnew", [128, 1, 1024], F32))
    xn_tm = ps.enter_context(nc.sbuf_tensor("f_xntm", [128, 2, TPG, 1024], BF16))
    xn_fm = ps.enter_context(nc.sbuf_tensor("f_xnfm", [128, 2, KO, GT], BF16))
    acc = ps.enter_context(nc.sbuf_tensor("f_acc", [128, 2, TPG, 1024], F32))
    bn12 = ps.enter_context(nc.sbuf_tensor("f_bn12", [128, 2, 12], F32))
    mr = ps.enter_context(nc.sbuf_tensor("f_mr", [128, 2, 4], F32))
    mrd = ps.enter_context(nc.sbuf_tensor("f_mrd", [128, 4, 4], F32))
    sc_sb = ps.enter_context(nc.sbuf_tensor("f_sc", [128, 8], F32))
    mx8 = ps.enter_context(nc.sbuf_tensor("f_mx8", [128, 8], F32))
    rt1 = ps.enter_context(nc.sbuf_tensor("f_rt1", [128, 8], F32))
    rt2 = ps.enter_context(nc.sbuf_tensor("f_rt2", [128, 8], F32))
    wcol = ps.enter_context(nc.sbuf_tensor("f_wcol", [128, 4], F32))
    wE_sb = ps.enter_context(nc.sbuf_tensor("f_wE", [128, 2, TPG, 8], F32))
    wE8_sb = ps.enter_context(nc.sbuf_tensor("f_wE8", [8, 2, GT], BF16))
    gb_sb = ps.enter_context(nc.sbuf_tensor("f_gb", [128, 2, 1024], BF16))
    u4_sb = ps.enter_context(nc.sbuf_tensor("f_u4", [128, 4, 1024], BF16))
    t3_sb = ps.enter_context(nc.sbuf_tensor("f_t3", [128, 1024], BF16))
    H0 = ps.enter_context(nc.psum_tensor("f_H0", [128, 2, 512], F32))
    Y = ps.enter_context(nc.psum_tensor("f_Y", [128, 2, 2, 512], F32))
    B_ps = ps.enter_context(nc.psum_tensor("f_B", [128, 8, 128], BF16))
    R_ps = ps.enter_context(nc.psum_tensor("f_R", [128, 512], F32))

    @ctx.G
    def _(g):
        for ko in range(KO):
            g.dma_start(out=gate_sb[:, ko, :], in_=W["gate_w"][ko * 128:(ko + 1) * 128, :]).then_inc(ctx.sem["dg"], 16)
            ctx.cnt["dg"] += 16
        blk0 = W["w1b"].rearrange("(e p) c -> e (p c)", p=1024)
        for pp in range(3):
            g.dma_start(out=w1_sb[:, pp, :, :],
                        in_=blk0[0:1, pp * 262144:(pp + 1) * 262144].rearrange(
                            "a (p k f) -> (a p) k f", p=128, k=8, f=256)).then_inc(ctx.sem["da"], 16)
            ctx.cnt["da"] += 16

    @ctx.G
    def _(s):
        ctx.inc(s.dma_start(out=lnb_sb[:], in_=W["ln_b"][:]), "dl", 16)
    base_loads = (ctx.cnt["dg"], ctx.cnt["dl"])

    st = {
        "xt": [0, 0], "xo": [0, 0], "store": [0, 0], "trB": 0,
        "w1free": [0, 0, 0, 0], "w1da": [16, 32, 48, 0], "w2free": [0, 0], "smfree": [0, 0],
        "hfree": [0, 0], "yfree": [0, 0],
        "xnfm": [0, 0], "wE": [0, 0], "accinit": [0, 0],
        "accfree": [0, 0], "xnfree": [0, 0], "rfree": 0, "rfree_ac": 0,
        "pend_d2b": None,
    }

    def emit_ABC(g):
        par = g % 2
        for tt in range(TPG):
            t = g * TPG + tt
            sl = tt % 2

            @ctx.S
            def _(s, t=t, sl=sl, pa=st["xt"][sl]):
                ctx.w(s, "dv", pa)
                ctx.inc(s.dma_start(out=xt[:, sl, :], in_=xin[t * 128:(t + 1) * 128, :]), "dl", 16)
            dl_here = ctx.cnt["dl"]

            @ctx.V
            def _(v, dl_here=dl_here):
                ctx.w(v, "dl", dl_here)
            ln_stats(ctx, bn12[:, sl, :], mr[:, sl, :], xt[:, sl, 0:512],
                     xt[:, sl, 512:1024], cst.eps5_col[:])
            ac_rstd = ctx.cnt["ac"]

            @ctx.V
            def _(v, par=par, tt=tt, sl=sl, ac_rstd=ac_rstd, xf=st["xnfree"][par]):
                ctx.w(v, "ac", ac_rstd)
                ctx.w(v, "dv", xf)
                ctx.inc(v.tensor_scalar(xn_tm[:, par, tt, :], xt[:, sl, :], mr[:, sl, 0:1],
                                        mr[:, sl, 2:3], ALU.subtract, ALU.mult), "dv")
            st["xt"][sl] = ctx.cnt["dv"]
            apply_dv = ctx.cnt["dv"]

            @ctx.TE
            def _(te, par=par, tt=tt, apply_dv=apply_dv, pt=st["trB"]):
                ctx.w(te, "dv", apply_dv)
                ctx.w(te, "dv", pt)
                ctx.w(te, "po", cst.ready)
                for ko in range(KO):
                    ins = te.transpose(B_ps[:, ko, :], xn_tm[:, par, tt, ko * 128:(ko + 1) * 128],
                                       cst.ident_bf[:])
                ctx.inc(ins, "pe")
            tr_pe = ctx.cnt["pe"]

            @ctx.V
            def _(v, par=par, tt=tt, tr_pe=tr_pe):
                ctx.w(v, "pe", tr_pe)
                ctx.inc(v.tensor_copy(xn_fm[:, par, :, tt * 128:(tt + 1) * 128], B_ps[:]), "dv")
            st["trB"] = ctx.cnt["dv"]
        st["xnfm"][par] = ctx.cnt["dv"]

        for tt in range(TPG):
            @ctx.TE
            def _(te, par=par, tt=tt, xnfm_dv=st["xnfm"][par], rfree=st["rfree"],
                  rfa=st["rfree_ac"]):
                ctx.w(te, "dv", xnfm_dv)
                ctx.w(te, "dg", base_loads[0])
                ctx.w(te, "dv", rfree)
                if rfa:
                    te.wait_ge(ctx.sem["ac"], rfa)
                for ko in range(KO):
                    ins = te.matmul(R_ps[:, 0:8], xn_fm[:, par, ko, tt * 128:(tt + 1) * 128],
                                    gate_sb[:, ko, :], start=(ko == 0), stop=(ko == KO - 1))
                ctx.inc(ins, "pe")
            sc_pe = ctx.cnt["pe"]

            @ctx.V
            def _(v, sc_pe=sc_pe):
                ctx.w(v, "pe", sc_pe)
                ctx.inc(v.tensor_copy(sc_sb[:], R_ps[:, 0:8]), "dv")
                st["rfree"] = max(st["rfree"], ctx.cnt["dv"])
                ctx.w(v, "dv")
                ctx.inc(v.max(mx8[:], sc_sb[:]), "dv")
                ctx.w(v, "dv")
                ctx.inc(v.tensor_tensor(wcol[:, 0:1], mx8[:, 1:2], mx8[:, 0:1], ALU.subtract), "dv")
            d_dv = ctx.cnt["dv"]

            @ctx.A
            def _(a, d_dv=d_dv):
                ctx.w(a, "dv", d_dv)
                ctx.inc(a.activation(wcol[:, 1:2], wcol[:, 0:1], AF.Exp), "ac")
            e_ac = ctx.cnt["ac"]

            @ctx.V
            def _(v, par=par, tt=tt, e_ac=e_ac):
                ctx.w(v, "ac", e_ac)
                ctx.inc(v.tensor_single_scalar(wcol[:, 2:3], wcol[:, 1:2], 1.0, ALU.add), "dv")
                ctx.w(v, "dv")
                ctx.inc(v.reciprocal(wcol[:, 3:4], wcol[:, 2:3]), "dv")
                ctx.w(v, "dv")
                v.tensor_single_scalar(rt1[:], sc_sb[:], mx8[:, 0:1], ALU.is_equal)
                ctx.inc(v.tensor_single_scalar(rt2[:], sc_sb[:], mx8[:, 1:2], ALU.is_equal), "dv")
                ctx.w(v, "dv")
                v.tensor_scalar_mul(rt1[:], rt1[:], wcol[:, 3:4])
                ctx.inc(v.tensor_scalar(rt2[:], rt2[:], wcol[:, 3:4], wcol[:, 1:2],
                                        ALU.mult, ALU.mult), "dv")
                ctx.w(v, "dv")
                ctx.inc(v.tensor_add(wE_sb[:, par, tt, :], rt1[:], rt2[:]), "dv")
            wE_dv = ctx.cnt["dv"]

            @ctx.TE
            def _(te, par=par, tt=tt, wE_dv=wE_dv, rfree=st["rfree"],
                  rfa=st["rfree_ac"]):
                ctx.w(te, "dv", wE_dv)
                ctx.w(te, "dv", rfree)
                if rfa:
                    te.wait_ge(ctx.sem["ac"], rfa)
                ctx.inc(te.transpose(R_ps[0:8, 0:128], wE_sb[:, par, tt, :], cst.ident_f[:]), "pe")
            wt_pe = ctx.cnt["pe"]

            @ctx.V
            def _(v, par=par, tt=tt, wt_pe=wt_pe):
                ctx.w(v, "pe", wt_pe)
                ctx.inc(v.tensor_copy(wE8_sb[:, par, tt * 128:(tt + 1) * 128],
                                      R_ps[0:8, 0:128]), "dv")
                st["rfree"] = max(st["rfree"], ctx.cnt["dv"])
        st["wE"][par] = ctx.cnt["dv"]

        for tt in range(TPG):
            for n in range(2):
                @ctx.TE
                def _(te, par=par, tt=tt, n=n, wEd=st["wE"][par], rfree=st["rfree"],
                      rfa=st["rfree_ac"]):
                    ctx.w(te, "dv", wEd)
                    ctx.w(te, "dv", rfree)
                    if rfa:
                        te.wait_ge(ctx.sem["ac"], rfa)
                    ctx.w(te, "dg", base_loads[0])
                    ctx.inc(te.matmul(R_ps[:], wE8_sb[:, par, tt * 128:(tt + 1) * 128],
                                      lnb_sb[:, n * 512:(n + 1) * 512], start=True, stop=True), "pe")
                bt_pe = ctx.cnt["pe"]

                @ctx.V
                def _(v, par=par, tt=tt, n=n, bt_pe=bt_pe, af=st["accfree"][par]):
                    ctx.w(v, "pe", bt_pe)
                    ctx.w(v, "dv", af)
                    ctx.inc(v.tensor_copy(acc[:, par, tt, n * 512:(n + 1) * 512], R_ps[:]), "dv")
                    st["rfree"] = max(st["rfree"], ctx.cnt["dv"])
        st["accinit"][par] = ctx.cnt["dv"]

    def emit_eload(e):
        eslot = e % 2

        @ctx.S
        def _(s, e=e, eslot=eslot, w2f=st["w2free"][eslot], smf=st["smfree"][eslot]):
            if w2f:
                s.wait_ge(ctx.sem["pe"], w2f)
            if smf:
                s.wait_ge(ctx.sem["dv"], smf)
            s.dma_start(out=b1t_sb[:, eslot, :],
                        in_=W["b1"][e:e + 1, :].rearrange("a (b c) -> (a b) c", c=128)).then_inc(ctx.sem["dl"], 16)
            ctx.cnt["dl"] += 16
            for fr in range(NF):
                s.dma_start(out=w2_sb[:, eslot, fr, :],
                            in_=W["w2b"][e * 2048 + fr * 128:e * 2048 + (fr + 1) * 128, :]).then_inc(ctx.sem["dl"], 16)
                ctx.cnt["dl"] += 16

        @ctx.G
        def _(gg, e=e, eslot=eslot, w2f=st["w2free"][eslot]):
            if w2f:
                gg.wait_ge(ctx.sem["pe"], w2f)
            gg.dma_start(out=rows4[:, eslot, :], in_=W["b2"][e:e + 1, :]).then_inc(ctx.sem["dg"], 16)
            gg.dma_start(out=rows4[:, 2 + eslot, :], in_=W["ln_g"][e:e + 1, :]).then_inc(ctx.sem["dg"], 16)
            ctx.cnt["dg"] += 32
        return (ctx.cnt["dl"], ctx.cnt["dg"])

    def emit_expert(g, e, ld_marks):
        par = g % 2
        eslot = e % 2
        ld_dl, ld_dg = ld_marks

        @ctx.TE
        def _(te, eslot=eslot, ld_dl=ld_dl, ld_dg=ld_dg, rfree=st["rfree"],
              yf=st["yfree"][0], rfa=st["rfree_ac"]):
            ctx.w(te, "dl", ld_dl)
            ctx.w(te, "dg", ld_dg)
            ctx.w(te, "dv", rfree)
            if rfa:
                te.wait_ge(ctx.sem["ac"], rfa)
            ctx.w(te, "dv", yf)
            te.matmul(Y[:, 0, 0, :], cst.ones_row_bf[:], rows4[0:1, 2 + eslot, 0:512],
                      start=True, stop=True)
            te.matmul(Y[:, 0, 1, :], cst.ones_row_bf[:], rows4[0:1, 2 + eslot, 512:1024],
                      start=True, stop=True)
            ctx.inc(te.transpose(R_ps[:, 0:16], b1t_sb[:, eslot, :],
                                 cst.ident_f[0:16, 0:16]), "pe")
        gb_pe = ctx.cnt["pe"]

        @ctx.V
        def _(v, eslot=eslot, gb_pe=gb_pe):
            ctx.w(v, "pe", gb_pe)
            v.tensor_copy(gb_sb[:, eslot, 0:512], Y[:, 0, 0, :])
            ctx.inc(v.tensor_copy(gb_sb[:, eslot, 512:1024], Y[:, 0, 1, :]), "dv")
        gb_dv = ctx.cnt["dv"]
        st["yfree"][0] = max(st["yfree"][0], gb_dv)
        b1_pe = gb_pe

        ac0 = ctx.cnt["ac"]
        gelu_marks = []
        NPAIR = NF // 2
        gpair = g * NE * NPAIR + e * NPAIR
        for f in range(NF):
            pair, sub = f // 2, f % 2
            slot = (gpair + pair) % 4
            if sub == 0:
                gp_next = gpair + pair + 3
                tot = NG * NE * NPAIR
                if gp_next < tot:
                    ne, np_ = (gp_next // NPAIR) % NE, gp_next % NPAIR
                    nslot = gp_next % 4

                    @ctx.G
                    def _(s, ne=ne, np_=np_, nslot=nslot, used=st["w1free"][nslot]):
                        if used:
                            s.wait_ge(ctx.sem["pe"], used)
                        blk = W["w1b"].rearrange("(e p) c -> e (p c)", p=1024)
                        s.dma_start(
                            out=w1_sb[:, nslot, :, :],
                            in_=blk[ne:ne + 1, np_ * 262144:(np_ + 1) * 262144].rearrange(
                                "a (p k f) -> (a p) k f", p=128, k=8, f=256)).then_inc(ctx.sem["da"], 16)
                        ctx.cnt["da"] += 16
                    st["w1da"][nslot] = ctx.cnt["da"]

            hb = f % 2
            hfree = (ac0 + f - 1) if f >= 2 else st["hfree"][hb]

            @ctx.TE
            def _(te, par=par, f=f, slot=slot, sub=sub, hb=hb, gb_dv=gb_dv,
                  hfree=hfree, w1f=st["w1da"][slot], xnfm_dv=st["xnfm"][par]):
                ctx.w(te, "da", w1f)
                ctx.w(te, "dv", xnfm_dv)
                if hfree:
                    te.wait_ge(ctx.sem["ac"], hfree)
                for ko in range(KO):
                    ins = te.matmul(H0[:, hb, :],
                                    w1_sb[:, slot, ko, sub * 128:(sub + 1) * 128],
                                    xn_fm[:, par, ko, :],
                                    start=(ko == 0), stop=(ko == KO - 1))
                ctx.inc(ins, "pe")
            h_pe = ctx.cnt["pe"]
            gelu_marks.append((f, hb, h_pe))
            if sub == 1:
                st["w1free"][slot] = h_pe

        @ctx.A
        def _(a, eslot=eslot, b1_pe=b1_pe, marks=tuple(gelu_marks)):
            a.wait_ge(ctx.sem["pe"], b1_pe)
            ctx.inc(a.activation(b1c_sb[:, eslot, :], R_ps[:, 0:16], AF.Copy), "po")
            ctx.w(a, "po")
            for f, hb, h_pe in marks:
                a.wait_ge(ctx.sem["pe"], h_pe)
                ctx.inc(a.activation(h_sb[:, f, :], H0[:, hb, :], AF.Gelu,
                                     bias=b1c_sb[:, eslot, f:f + 1]), "ac")
        assert ctx.cnt["ac"] == ac0 + NF
        st["rfree_ac"] = ac0 + 1
        flush_d2b()
        st["hfree"] = [ac0 + NF - 1, ac0 + NF]
        h_ac = ctx.cnt["ac"]

        aggr_marks = []
        for tt in range(TPG):
            slot = tt % 2

            @ctx.TE
            def _(te, par=par, tt=tt, slot=slot, eslot=eslot, h_ac=h_ac, ld_dl=ld_dl,
                  yfree=st["yfree"][slot]):
                ctx.w(te, "ac", h_ac)
                ctx.w(te, "dl", ld_dl)
                ctx.w(te, "dg")
                ctx.w(te, "dv", yfree)
                for n in range(2):
                    for f in range(NF):
                        te.matmul(Y[:, slot, n, :], h_sb[:, f, tt * 128:(tt + 1) * 128],
                                  w2_sb[:, eslot, f, n * 512:(n + 1) * 512],
                                  start=(f == 0), stop=False)
                    ins = te.matmul(Y[:, slot, n, :], cst.ones_row_bf[:],
                                    rows4[0:1, eslot, n * 512:(n + 1) * 512],
                                    start=False, stop=True)
                ctx.inc(ins, "pe")
            y_pe = ctx.cnt["pe"]

            @ctx.V
            def _(v, par=par, tt=tt, slot=slot, y_pe=y_pe):
                ctx.w(v, "pe", y_pe)
                ctx.inc(v.tensor_add(u4_sb[:, tt, :], Y[:, slot].rearrange("p a b -> p (a b)"),
                                     xn_tm[:, par, tt, :]), "dv")
                ctx.w(v, "dv")
                v.bn_stats(bn12[:, 0, 0:6], u4_sb[:, tt, 0:512])
                ctx.inc(v.bn_stats(bn12[:, 0, 6:12], u4_sb[:, tt, 512:1024]), "dv")
                ctx.w(v, "dv")
                ctx.inc(v.bn_aggr(mrd[:, tt, 0:2], bn12[:, 0, :]), "dv")
            st["yfree"][slot] = ctx.cnt["dv"]
            aggr_marks.append(ctx.cnt["dv"])
        st["w2free"][eslot] = ctx.cnt["pe"]
        st["pend_d2b"] = (g, e, tuple(aggr_marks))

    def flush_d2b():
        if st["pend_d2b"] is None:
            return
        pg, pe_, marks = st["pend_d2b"]
        st["pend_d2b"] = None
        ppar = pg % 2
        peslot = pe_ % 2

        @ctx.A
        def _(a, marks=marks):
            for tt in range(TPG):
                a.wait_ge(ctx.sem["dv"], marks[tt])
                ctx.inc(a.activation(mrd[:, tt, 3:4], mrd[:, tt, 1:2], AF.Ln,
                                     bias=cst.eps6_col[:]), "ac")
                ctx.w(a, "ac")
                ctx.inc(a.activation(mrd[:, tt, 2:3], mrd[:, tt, 3:4], AF.Exp,
                                     scale=-0.5), "ac")
        rstd_ac = ctx.cnt["ac"]

        @ctx.V
        def _(v, ppar=ppar, pe_=pe_, peslot=peslot, rstd_ac=rstd_ac,
              accinit=st["accinit"][ppar]):
            ctx.w(v, "ac", rstd_ac)
            ctx.w(v, "dv", accinit)
            for tt in range(TPG):
                ctx.inc(v.tensor_tensor(mrd[:, tt, 3:4], mrd[:, tt, 2:3],
                                        wE_sb[:, ppar, tt, pe_:pe_ + 1], ALU.mult), "dv")
            ctx.w(v, "dv")
            for tt in range(TPG):
                ctx.inc(v.scalar_tensor_tensor(t3_sb[:], u4_sb[:, tt, :], mrd[:, tt, 0:1],
                                               gb_sb[:, peslot, :],
                                               ALU.subtract, ALU.mult), "dv")
                ctx.w(v, "dv")
                ctx.inc(v.scalar_tensor_tensor(acc[:, ppar, tt, :], t3_sb[:], mrd[:, tt, 3:4],
                                               acc[:, ppar, tt, :], ALU.mult, ALU.add), "dv")
        st["smfree"][peslot] = ctx.cnt["dv"]

    def emit_E(g):
        flush_d2b()
        par = g % 2
        for tt in range(TPG):
            t = g * TPG + tt
            sl = tt % 2

            @ctx.S
            def _(s, t=t, po=st["xo"][0]):
                ctx.w(s, "dv", po)
                ctx.inc(s.dma_start(out=xo[:, 0, :], in_=xin[t * 128:(t + 1) * 128, :]), "dl", 16)
            xo_dl = ctx.cnt["dl"]

            @ctx.V
            def _(v, xo_dl=xo_dl):
                ctx.w(v, "dl", xo_dl)
                ctx.w(v, "dv")
            ln_stats(ctx, bn12[:, sl, :], mr[:, sl, :], acc[:, par, tt, 0:512],
                     acc[:, par, tt, 512:1024], cst.eps5_col[:])
            ac_rstd = ctx.cnt["ac"]

            @ctx.V
            def _(v, par=par, tt=tt, sl=sl, ac_rstd=ac_rstd, pstore=st["store"][0]):
                ctx.w(v, "ac", ac_rstd)
                ctx.inc(v.tensor_scalar(acc[:, par, tt, :], acc[:, par, tt, :], mr[:, sl, 0:1],
                                        mr[:, sl, 2:3], ALU.subtract, ALU.mult), "dv")
                ctx.w(v, "dv")
                ctx.w(v, "ds", pstore)
                ctx.inc(v.tensor_add(xnew[:, 0, :], acc[:, par, tt, :], xo[:, 0, :]), "dv")
            st["xo"][0] = ctx.cnt["dv"]
            res_dv = ctx.cnt["dv"]

            @ctx.S
            def _(s, t=t, res_dv=res_dv):
                ctx.w(s, "dv", res_dv)
                ctx.inc(s.dma_start(out=xout[t * 128:(t + 1) * 128, :], in_=xnew[:, 0, :]), "ds", 16)
            st["store"][0] = ctx.cnt["ds"]
        st["accfree"][par] = ctx.cnt["dv"]
        st["xnfree"][par] = ctx.cnt["dv"]

    emit_ABC(0)
    ld_marks = emit_eload(0)
    for g in range(NG):
        for e in range(NE):
            cur_marks = ld_marks
            emit_expert(g, e, cur_marks)
            if e == 0 and g > 0:
                emit_E(g - 1)
            if e + 1 < NE:
                ld_marks = emit_eload(e + 1)
            elif g + 1 < NG:
                ld_marks = emit_eload(0)
            if e == 3 and g + 1 < NG:
                emit_ABC(g + 1)
    emit_E(NG - 1)

    return ps


def emit_l1_attn(ctx, cst, xin, xout, W):
    nc = ctx.nc
    NH, DH, CH, NC_ = 16, 64, 256, 16
    EPS_R = 1e-6
    S_GLA = float(DH ** -0.5)
    ps = ExitStack()
    wsb = {}
    for nm in ("Wq", "Wk", "Wv", "Wg", "Wo"):
        wsb[nm] = ps.enter_context(nc.sbuf_tensor("g_" + nm, [128, KO, 1024], BF16))
    wsb["Wgt"] = ps.enter_context(nc.sbuf_tensor("g_Wgt", [128, KO, 16], BF16))
    xt = ps.enter_context(nc.sbuf_tensor("g_xt", [128, 2, 2, 1024], F32))
    xnew = ps.enter_context(nc.sbuf_tensor("g_xnew", [128, 2, 1024], F32))
    bn12 = ps.enter_context(nc.sbuf_tensor("g_bn12", [128, 12], F32))
    mr = ps.enter_context(nc.sbuf_tensor("g_mr", [128, 4], F32))
    xn_tm = ps.enter_context(nc.sbuf_tensor("g_xntm", [128, 2, 1024], BF16))
    xn_fm = ps.enter_context(nc.sbuf_tensor("g_xnfm", [128, KO, 256], BF16))
    q_tm = ps.enter_context(nc.sbuf_tensor("g_qtm", [128, 2, 1024], BF16))
    k_tm = ps.enter_context(nc.sbuf_tensor("g_ktm", [128, 2, 1024], BF16))
    v_tm = ps.enter_context(nc.sbuf_tensor("g_vtm", [128, 2, 1024], BF16))
    sg_tm = ps.enter_context(nc.sbuf_tensor("g_sgtm", [128, 2, 1024], BF16))
    kp_tm = ps.enter_context(nc.sbuf_tensor("g_kptm", [128, 2, 1024], BF16))
    vp_tm = ps.enter_context(nc.sbuf_tensor("g_vptm", [128, 2, 1024], BF16))
    o_tm = ps.enter_context(nc.sbuf_tensor("g_otm", [128, 2, 1024], BF16))
    on_tm = ps.enter_context(nc.sbuf_tensor("g_ontm", [128, 2, 1024], BF16))
    gated_tm = ps.enter_context(nc.sbuf_tensor("g_gated", [128, 2, 1024], BF16))
    gated_fm = ps.enter_context(nc.sbuf_tensor("g_gfm", [128, KO, 256], BF16))
    gt_sb = ps.enter_context(nc.sbuf_tensor("g_gt", [128, 2, 16], F32))
    lgA = ps.enter_context(nc.sbuf_tensor("g_lgA", [128, 2, 16], F32))
    lgB = ps.enter_context(nc.sbuf_tensor("g_lgB", [128, 2, 16], F32))
    EXN = ps.enter_context(nc.sbuf_tensor("g_EXN", [128, 2, 16], F32))
    EXPc = ps.enter_context(nc.sbuf_tensor("g_EXP", [128, 2, 16], F32))
    EK = ps.enter_context(nc.sbuf_tensor("g_EK", [128, 16], F32))
    al_sb = ps.enter_context(nc.sbuf_tensor("g_al", [128, 2, 16], F32))
    expB_sb = ps.enter_context(nc.sbuf_tensor("g_expB", [1, 16], F32))
    ebh_sb = ps.enter_context(nc.sbuf_tensor("g_ebh", [64, 2, 16], F32))
    cols = ps.enter_context(nc.sbuf_tensor("g_cols", [128, 2, 8], F32))
    sqs = ps.enter_context(nc.sbuf_tensor("g_sqs", [128, 1024], BF16))
    q_fm = ps.enter_context(nc.sbuf_tensor("g_qfm", [64, NH, 256], BF16))
    kp_fm = ps.enter_context(nc.sbuf_tensor("g_kpfm", [64, NH, 256], BF16))
    A_sb = ps.enter_context(nc.sbuf_tensor("g_A", [128, 2, 2, 256], BF16))
    oT_sb = ps.enter_context(nc.sbuf_tensor("g_oT", [64, 2, 256], BF16))
    kv_sb = ps.enter_context(nc.sbuf_tensor("g_kv", [64, 2, 64], F32))
    S_sb = ps.enter_context(nc.sbuf_tensor("g_S", [64, NH, 64], F32))
    S_bf = ps.enter_context(nc.sbuf_tensor("g_Sbf", [64, NH, 64], BF16))
    scr64 = ps.enter_context(nc.sbuf_tensor("g_scr64", [64, 64], F32))
    on2 = ps.enter_context(nc.sbuf_tensor("g_on2", [128, 16], F32))
    ost = ps.enter_context(nc.sbuf_tensor("g_ost", [128, 16], F32))
    u_sb = ps.enter_context(nc.sbuf_tensor("g_u", [128, 1024], F32))
    P0 = ps.enter_context(nc.psum_tensor("g_P0", [128, 2, 512], F32))
    AT_ps = ps.enter_context(nc.psum_tensor("g_AT", [128, 2, 2, 256], F32))
    OT_ps = ps.enter_context(nc.psum_tensor("g_OT", [64, 2, 512], F32))
    KV_ps = ps.enter_context(nc.psum_tensor("g_KV", [64, 64], F32))
    B_ps = ps.enter_context(nc.psum_tensor("g_B", [128, 8, 128], BF16))

    @ctx.G
    def _(g):
        for nm in ("Wq", "Wk", "Wv", "Wg", "Wo"):
            for ko in range(KO):
                g.dma_start(out=wsb[nm][:, ko, :],
                            in_=W[nm][ko * 128:(ko + 1) * 128, :]).then_inc(ctx.sem["dg"], 16)
                ctx.cnt["dg"] += 16
        for ko in range(KO):
            g.dma_start(out=wsb["Wgt"][:, ko, :],
                        in_=W["Wgt"][ko * 128:(ko + 1) * 128, :]).then_inc(ctx.sem["dg"], 16)
            ctx.cnt["dg"] += 16
    wts_dg = ctx.cnt["dg"]

    @ctx.V
    def _(v):
        v.memset(S_sb[:], 0.0)
        ctx.inc(v.memset(S_bf[:], 0.0), "dv")

    prev_resid = [[0, 0], [0, 0]]
    prev_store = [0, 0]
    trB = [0]
    prev_projfree = 0
    at_u_guard = [0]
    marks = {}
    xnfm_marks = {}
    gt_marks = {}
    front_marks = {}

    def s1_tile(n, i):
        t = 2 * n + i
        par = n % 2

        @ctx.S
        def _(s, t=t, par=par, i=i, pr=prev_resid[par][i]):
            ctx.w(s, "dv", pr)
            ctx.inc(s.dma_start(out=xt[:, par, i, :], in_=xin[t * 128:(t + 1) * 128, :]), "dl", 16)
        dl_here = ctx.cnt["dl"]

        @ctx.V
        def _(v, dl_here=dl_here):
            ctx.w(v, "dl", dl_here)
        ln_stats(ctx, bn12[:], mr[:], xt[:, par, i, 0:512], xt[:, par, i, 512:1024],
                 cst.eps5_col[:])
        ac_rstd = ctx.cnt["ac"]

        @ctx.V
        def _(v, par=par, i=i, ac_rstd=ac_rstd):
            ctx.w(v, "ac", ac_rstd)
            ctx.inc(v.tensor_scalar(xn_tm[:, i, :], xt[:, par, i, :], mr[:, 0:1], mr[:, 2:3],
                                    ALU.subtract, ALU.mult), "dv")
        apply_dv = ctx.cnt["dv"]

        @ctx.TE
        def _(te, i=i, apply_dv=apply_dv, pt=trB[0]):
            ctx.w(te, "dv", apply_dv)
            ctx.w(te, "dv", pt)
            ctx.w(te, "po", cst.ready)
            for ko in range(KO):
                ins = te.transpose(B_ps[:, ko, :], xn_tm[:, i, ko * 128:(ko + 1) * 128],
                                   cst.ident_bf[:])
            ctx.inc(ins, "pe")
        tr_pe = ctx.cnt["pe"]

        @ctx.V
        def _(v, i=i, tr_pe=tr_pe):
            ctx.w(v, "pe", tr_pe)
            ctx.inc(v.tensor_copy(xn_fm[:, :, i * 128:(i + 1) * 128], B_ps[:]), "dv")
        trB[0] = ctx.cnt["dv"]
        xnfm_marks[n] = ctx.cnt["dv"]

    def zt_tile(n, i):
        @ctx.TE
        def _(te, i=i, xnfm_dv=xnfm_marks[n]):
            ctx.w(te, "dv", xnfm_dv)
            ctx.w(te, "dg", wts_dg)
            ctx.w(te, "ac")
            for ko in range(KO):
                ins = te.matmul(P0[:, 0, i * 16:(i + 1) * 16], xn_fm[:, ko, i * 128:(i + 1) * 128],
                                wsb["Wgt"][:, ko, :], start=(ko == 0), stop=(ko == KO - 1))
            ctx.inc(ins, "pe")
        zt_pe = ctx.cnt["pe"]

        @ctx.V
        def _(v, i=i, zt_pe=zt_pe):
            ctx.w(v, "pe", zt_pe)
            ctx.inc(v.tensor_single_scalar(lgA[:, i, :], P0[:, 0, i * 16:(i + 1) * 16],
                                           0.0, ALU.min), "dv")
        mn_dv = ctx.cnt["dv"]

        @ctx.A
        def _(a, i=i, zt_pe=zt_pe, mn_dv=mn_dv):
            ctx.w(a, "pe", zt_pe)
            ctx.w(a, "dv", mn_dv)
            ctx.inc(a.activation(lgB[:, i, :], P0[:, 0, i * 16:(i + 1) * 16], AF.Abs), "ac")
            ctx.w(a, "ac")
            ctx.inc(a.activation(lgB[:, i, :], lgB[:, i, :], AF.Exp, scale=-1.0), "ac")
            ctx.w(a, "ac")
            ctx.inc(a.activation(lgB[:, i, :], lgB[:, i, :], AF.Ln, bias=1.0), "ac")
        lg_ac = ctx.cnt["ac"]

        @ctx.V
        def _(v, i=i, lg_ac=lg_ac):
            ctx.w(v, "ac", lg_ac)
            ctx.inc(v.tensor_tensor(gt_sb[:, i, :], lgA[:, i, :], lgB[:, i, :],
                                    ALU.subtract), "dv")
            ctx.w(v, "dv")
            ctx.inc(v.tensor_scalar_mul(gt_sb[:, i, :], gt_sb[:, i, :], 1.0 / 16.0), "dv")
        gt_marks[n] = ctx.cnt["dv"]

    def bc_chunk(n):
        par = n % 2

        @ctx.TE
        def _(te, gt_dv=gt_marks[n]):
            ctx.w(te, "dv", gt_dv)
            ctx.w(te, "ac")
            te.matmul(P0[:, 0, 32:48], cst.triu_f[:], gt_sb[:, 0, :], start=True, stop=True)
            te.matmul(P0[:, 0, 48:64], cst.ones128_f[:], gt_sb[:, 0, :], start=True, stop=False)
            te.matmul(P0[:, 0, 48:64], cst.triu_f[:], gt_sb[:, 1, :], start=False, stop=True)
            te.matmul(P0[0:1, 0, 64:80], cst.ones_col_f[:], gt_sb[:, 0, :], start=True, stop=False)
            ctx.inc(te.matmul(P0[0:1, 0, 64:80], cst.ones_col_f[:], gt_sb[:, 1, :],
                              start=False, stop=True), "pe")
        bc_pe = ctx.cnt["pe"]

        @ctx.A
        def _(a, bc_pe=bc_pe):
            ctx.w(a, "pe", bc_pe)
            for i in range(2):
                a.activation(EXN[:, i, :], P0[:, 0, (32 + 16 * i):(48 + 16 * i)], AF.Exp, scale=-1.0)
                a.activation(EXPc[:, i, :], P0[:, 0, (32 + 16 * i):(48 + 16 * i)], AF.Exp)
            ctx.inc(a.activation(expB_sb[:], P0[0:1, 0, 64:80], AF.Exp), "ac")
        exf_ac = ctx.cnt["ac"]

        @ctx.TE
        def _(te, exf_ac=exf_ac):
            ctx.w(te, "ac", exf_ac)
            ctx.inc(te.matmul(P0[0:64, 0, 80:96], cst.ones_row_f[0:1, 0:64], expB_sb[:],
                              start=True, stop=True), "pe")
        eb_pe = ctx.cnt["pe"]

        @ctx.V
        def _(v, par=par, eb_pe=eb_pe):
            ctx.w(v, "pe", eb_pe)
            ctx.inc(v.tensor_copy(ebh_sb[:, par, :], P0[0:64, 0, 80:96]), "dv")
        front_marks[n] = (exf_ac, ctx.cnt["dv"])

    def front_all(n):
        for i in range(2):
            s1_tile(n, i)
        for i in range(2):
            zt_tile(n, i)
        bc_chunk(n)

    front_all(0)

    for n in range(NC_):
        par = n % 2
        xnfm_dv = xnfm_marks[n]
        exf_ac, ebh_dv = front_marks[n]

        projfree = [prev_projfree, 0]
        for i in range(2):
            for pj, (nm, dst) in enumerate((("Wq", q_tm), ("Wk", k_tm), ("Wv", v_tm),
                                            ("Wg", sg_tm))):
                g8 = i * 4 + pj
                reg = g8 % 2

                @ctx.TE
                def _(te, i=i, pj=pj, nm=nm, g8=g8, reg=reg, pfree=projfree[reg],
                      xnfm_dv=xnfm_dv, exf=exf_ac, ebh=ebh_dv, ug=at_u_guard[0]):
                    ctx.w(te, "dv", xnfm_dv)
                    ctx.w(te, "ac", pfree)
                    if g8 == 0:
                        ctx.w(te, "ac", exf)
                        ctx.w(te, "dv", ebh)
                    if g8 == 1:
                        ctx.w(te, "dv", ug)
                    for ko in range(KO):
                        for nn in range(2):
                            out = P0[:, nn, :] if reg == 0 else \
                                AT_ps[:, nn].rearrange("p a b -> p (a b)")
                            ins = te.matmul(out, xn_fm[:, ko, i * 128:(i + 1) * 128],
                                            wsb[nm][:, ko, nn * 512:(nn + 1) * 512],
                                            start=(ko == 0), stop=(ko == KO - 1))
                    ctx.inc(ins, "pe")
                pj_pe = ctx.cnt["pe"]

                @ctx.A
                def _(a, i=i, pj=pj, reg=reg, dst=dst, pj_pe=pj_pe):
                    ctx.w(a, "pe", pj_pe)
                    ctx.w(a, "dv")
                    src = P0[:].rearrange("p a b -> p (a b)") if reg == 0 else \
                        AT_ps[:].rearrange("p a b c -> p (a b c)")
                    if pj == 3:
                        ins = a.activation(dst[:, i, :], src, AF.Silu)
                    else:
                        a.activation(dst[:, i, :], src, AF.Copy)
                        ins = a.activation(sqs[:], src, AF.Square,
                                           accum_out=cols[:, i, pj:pj + 1])
                    ctx.inc(ins, "ac")
                projfree[reg] = ctx.cnt["ac"]
                prev_projfree = ctx.cnt["ac"]
                ev_ac = ctx.cnt["ac"]

                if pj < 3:
                    @ctx.V
                    def _(v, i=i, pj=pj, ev_ac=ev_ac):
                        ctx.w(v, "ac", ev_ac)
                        ctx.inc(v.tensor_scalar(cols[:, i, 4 + pj:5 + pj], cols[:, i, pj:pj + 1],
                                                1.0 / D, float(EPS_R), ALU.mult, ALU.add), "dv")
                    rc_dv = ctx.cnt["dv"]

                    @ctx.A
                    def _(a, i=i, pj=pj, rc_dv=rc_dv):
                        ctx.w(a, "dv", rc_dv)
                        ctx.inc(a.activation(cols[:, i, 4 + pj:5 + pj],
                                             cols[:, i, 4 + pj:5 + pj], AF.Ln), "ac")
                        ctx.w(a, "ac")
                        ctx.inc(a.activation(cols[:, i, 4 + pj:5 + pj],
                                             cols[:, i, 4 + pj:5 + pj], AF.Exp, scale=-0.5), "ac")
        rqkv_ac = ctx.cnt["ac"]

        for i in range(2):
            @ctx.V
            def _(v, i=i, rqkv_ac=rqkv_ac, ebh_dv=ebh_dv):
                ctx.w(v, "ac", rqkv_ac)
                ctx.w(v, "dv", ebh_dv)
                ctx.inc(v.tensor_scalar_mul(EK[:], EXN[:, i, :], cols[:, i, 5:6]), "dv")
                ctx.w(v, "dv")
                ctx.inc(v.tensor_tensor(
                    kp_tm[:, i, :].rearrange("p (h e) -> p h e", e=64),
                    k_tm[:, i, :].rearrange("p (h e) -> p h e", e=64),
                    EK[:].unsqueeze(2).broadcast_to([128, 16, 64]), ALU.mult), "dv")
                ctx.inc(v.tensor_scalar_mul(vp_tm[:, i, :], v_tm[:, i, :], cols[:, i, 6:7]), "dv")
                ctx.inc(v.tensor_scalar(al_sb[:, i, :], EXPc[:, i, :], cols[:, i, 4:5],
                                        S_GLA, ALU.mult, ALU.mult), "dv")
        kpv_dv = ctx.cnt["dv"]

        for i in range(2):
            for grp in range(2):
                for src_tm, dst_fm in ((q_tm, q_fm), (kp_tm, kp_fm)):
                    @ctx.TE
                    def _(te, i=i, grp=grp, src_tm=src_tm, kpv_dv=kpv_dv, pt=trB[0]):
                        ctx.w(te, "dv", kpv_dv)
                        ctx.w(te, "dv", pt)
                        for hh in range(8):
                            h = grp * 8 + hh
                            ins = te.transpose(B_ps[0:64, hh, :],
                                               src_tm[:, i, h * 64:(h + 1) * 64],
                                               cst.ident_bf[:])
                        ctx.inc(ins, "pe")
                    tq_pe = ctx.cnt["pe"]

                    @ctx.V
                    def _(v, i=i, grp=grp, dst_fm=dst_fm, tq_pe=tq_pe):
                        ctx.w(v, "pe", tq_pe)
                        ctx.inc(v.tensor_copy(
                            dst_fm[:, grp * 8:(grp + 1) * 8, i * 128:(i + 1) * 128],
                            B_ps[0:64, :, :]), "dv")
                    trB[0] = ctx.cnt["dv"]
        qk_fm_dv = ctx.cnt["dv"]

        hg0 = n * NH

        def stage_at(h):
            sl = h % 2

            @ctx.TE
            def _(te, h=h, sl=sl, qk_fm_dv=qk_fm_dv,
                  atfree=marks.get(("aev", hg0 + h - 2), 0), uguard=at_u_guard[0]):
                ctx.w(te, "dv", qk_fm_dv)
                if h < 2:
                    ctx.w(te, "dv", uguard)
                if atfree:
                    te.wait_ge(ctx.sem["dv"], atfree)
                for jt in range(2):
                    ins = te.matmul(AT_ps[:, sl, jt, :], kp_fm[:, h, jt * 128:(jt + 1) * 128],
                                    q_fm[:, h, :], start=True, stop=True)
                ctx.inc(ins, "pe")
            marks[("at", hg0 + h)] = ctx.cnt["pe"]

        def stage_aev(h):
            sl = h % 2

            @ctx.V
            def _(v, h=h, sl=sl, at_pe=marks[("at", hg0 + h)],
                  afree=marks.get(("ot", hg0 + h - 2), 0)):
                ctx.w(v, "pe", at_pe)
                if afree:
                    v.wait_ge(ctx.sem["pe"], afree)
                v.tensor_tensor(A_sb[:, sl, 0, 0:128], AT_ps[:, sl, 0, 0:128],
                                cst.triu_bf[:], ALU.mult)
                v.tensor_copy(A_sb[:, sl, 0, 128:256], AT_ps[:, sl, 0, 128:256])
                v.memset(A_sb[:, sl, 1, 0:128], 0.0)
                ctx.inc(v.tensor_tensor(A_sb[:, sl, 1, 128:256], AT_ps[:, sl, 1, 128:256],
                                        cst.triu_bf[:], ALU.mult), "dv")
            marks[("aev", hg0 + h)] = ctx.cnt["dv"]

        def stage_ot(h):
            sl = h % 2

            @ctx.TE
            def _(te, h=h, sl=sl, n=n, a_dv=marks[("aev", hg0 + h)],
                  otfree=marks.get(("otev", hg0 + h - 2), 0),
                  kvfree=marks.get(("kvev", hg0 + h - 1), 0)):
                ctx.w(te, "dv", a_dv)
                if otfree:
                    te.wait_ge(ctx.sem["dv"], otfree)
                for jt in range(2):
                    te.matmul(OT_ps[:, sl, 0:256], vp_tm[:, jt, h * 64:(h + 1) * 64],
                              A_sb[:, sl, jt, :], start=(jt == 0), stop=(jt == 1 and n == 0))
                if n > 0:
                    te.matmul(OT_ps[:, sl, 0:256], S_bf[:, h, :], q_fm[:, h, :],
                              start=False, stop=True)
                if kvfree:
                    te.wait_ge(ctx.sem["dv"], kvfree)
                for jt in range(2):
                    ins = te.matmul(KV_ps[:], kp_tm[:, jt, h * 64:(h + 1) * 64],
                                    vp_tm[:, jt, h * 64:(h + 1) * 64],
                                    start=(jt == 0), stop=(jt == 1))
                ctx.inc(ins, "pe")
            marks[("ot", hg0 + h)] = ctx.cnt["pe"]

        def stage_ev(h):
            sl = h % 2

            @ctx.V
            def _(a, h=h, sl=sl, ot_pe=marks[("ot", hg0 + h)],
                  otr=marks.get(("otr", hg0 + h - 2), 0), sup=marks.get(("sup", hg0 + h - 2), 0)):
                ctx.w(a, "pe", ot_pe)
                if sup:
                    a.wait_ge(ctx.sem["dv"], sup)
                ctx.inc(a.tensor_copy(kv_sb[:, sl, :], KV_ps[:]), "dv")
                if otr:
                    a.wait_ge(ctx.sem["pe"], otr)
                ctx.inc(a.tensor_copy(oT_sb[:, sl, :], OT_ps[:, sl, 0:256]), "dv")
            marks[("kvev", hg0 + h)] = ctx.cnt["dv"] - 1
            marks[("otev", hg0 + h)] = ctx.cnt["dv"]

        def stage_otr(h):
            sl = h % 2

            @ctx.TE
            def _(te, h=h, sl=sl, otev=marks[("otev", hg0 + h)], pt=trB[0]):
                te.wait_ge(ctx.sem["dv"], otev)
                ctx.w(te, "dv", pt)
                te.transpose(B_ps[0:128, 0, 0:64], oT_sb[:, sl, 0:128],
                             cst.ident_bf[0:64, 0:64])
                ctx.inc(te.transpose(B_ps[0:128, 1, 0:64], oT_sb[:, sl, 128:256],
                                     cst.ident_bf[0:64, 0:64]), "pe")
            marks[("otr", hg0 + h)] = ctx.cnt["pe"]

            @ctx.V
            def _(v, h=h, otr_pe=marks[("otr", hg0 + h)]):
                ctx.w(v, "pe", otr_pe)
                for i in range(2):
                    ins = v.tensor_scalar_mul(o_tm[:, i, h * 64:(h + 1) * 64],
                                              B_ps[:, i, 0:64], al_sb[:, i, h:h + 1])
                ctx.inc(ins, "dv")
            trB[0] = ctx.cnt["dv"]

            @ctx.V
            def _(v, h=h, sl=sl, par=par, kvev=marks[("kvev", hg0 + h)]):
                v.wait_ge(ctx.sem["dv"], kvev)
                ctx.inc(v.tensor_add(scr64[:], S_sb[:, h, :], kv_sb[:, sl, :]), "dv")
                ctx.w(v, "dv")
                ctx.inc(v.tensor_scalar_mul(S_sb[:, h, :], scr64[:],
                                            ebh_sb[:, par, h:h + 1]), "dv")
                ctx.w(v, "dv")
                ctx.inc(v.tensor_copy(S_bf[:, h, :], S_sb[:, h, :]), "dv")
            marks[("sup", hg0 + h)] = ctx.cnt["dv"]

        stage_at(0)
        for h in range(NH):
            stage_aev(h)
            if h + 1 < NH:
                stage_at(h + 1)
            stage_ot(h)
            stage_ev(h)
            stage_otr(h)
            if n + 1 < NC_:
                if h == 3:
                    s1_tile(n + 1, 0)
                elif h == 5:
                    s1_tile(n + 1, 1)
                elif h == 7:
                    zt_tile(n + 1, 0)
                elif h == 9:
                    zt_tile(n + 1, 1)
                elif h == 11:
                    bc_chunk(n + 1)
        o_dv = ctx.cnt["dv"]

        for i in range(2):
            t = 2 * n + i

            @ctx.A
            def _(a, i=i, o_dv=o_dv):
                ctx.w(a, "dv", o_dv)
                ctx.w(a, "dv")
                ctx.inc(a.activation(sqs[:], o_tm[:, i, :], AF.Square), "ac")
            sq_ac = ctx.cnt["ac"]

            @ctx.V
            def _(v, i=i, sq_ac=sq_ac):
                ctx.w(v, "ac", sq_ac)
                ctx.inc(v.tensor_reduce(on2[:], sqs[:].rearrange("p (h e) -> p h e", e=64),
                                        AX.X, ALU.add), "dv")
                ctx.w(v, "dv")
                ctx.inc(v.tensor_scalar(ost[:], on2[:], 1.0 / 64.0, float(EPS_R),
                                        ALU.mult, ALU.add), "dv")
            rc_dv = ctx.cnt["dv"]

            @ctx.A
            def _(a, rc_dv=rc_dv):
                ctx.w(a, "dv", rc_dv)
                ctx.inc(a.activation(ost[:], ost[:], AF.Ln), "ac")
                ctx.w(a, "ac")
                ctx.inc(a.activation(ost[:], ost[:], AF.Exp, scale=-0.5), "ac")
            rs_ac = ctx.cnt["ac"]

            @ctx.V
            def _(v, i=i, rs_ac=rs_ac):
                ctx.w(v, "ac", rs_ac)
                ctx.inc(v.tensor_tensor(
                    on_tm[:, i, :].rearrange("p (h e) -> p h e", e=64),
                    o_tm[:, i, :].rearrange("p (h e) -> p h e", e=64),
                    ost[:].unsqueeze(2).broadcast_to([128, 16, 64]), ALU.mult), "dv")
                ctx.w(v, "dv")
                ctx.inc(v.tensor_tensor(gated_tm[:, i, :], on_tm[:, i, :], sg_tm[:, i, :],
                                        ALU.mult), "dv")
            gated_dv = ctx.cnt["dv"]

            @ctx.TE
            def _(te, i=i, gated_dv=gated_dv, pt=trB[0]):
                ctx.w(te, "dv", gated_dv)
                ctx.w(te, "dv", pt)
                for ko in range(KO):
                    ins = te.transpose(B_ps[:, ko, :], gated_tm[:, i, ko * 128:(ko + 1) * 128],
                                       cst.ident_bf[:])
                ctx.inc(ins, "pe")
            gtr_pe = ctx.cnt["pe"]

            @ctx.V
            def _(v, i=i, gtr_pe=gtr_pe):
                ctx.w(v, "pe", gtr_pe)
                ctx.inc(v.tensor_copy(gated_fm[:, :, i * 128:(i + 1) * 128], B_ps[:]), "dv")
            trB[0] = ctx.cnt["dv"]
            gfm_dv = ctx.cnt["dv"]

            @ctx.TE
            def _(te, i=i, gfm_dv=gfm_dv,
                  aev_last=marks[("aev", hg0 + NH - 1)], ug=at_u_guard[0]):
                ctx.w(te, "dv", gfm_dv)
                if i == 0:
                    ctx.w(te, "dv", aev_last)
                else:
                    ctx.w(te, "dv", ug)
                for nn in range(2):
                    for ko in range(KO):
                        ins = te.matmul(AT_ps[:, nn].rearrange("p a b -> p (a b)"),
                                        gated_fm[:, ko, i * 128:(i + 1) * 128],
                                        wsb["Wo"][:, ko, nn * 512:(nn + 1) * 512],
                                        start=(ko == 0), stop=(ko == KO - 1))
                ctx.inc(ins, "pe")
            wo_pe = ctx.cnt["pe"]

            @ctx.V
            def _(v, wo_pe=wo_pe):
                ctx.w(v, "pe", wo_pe)
            ln_stats(ctx, bn12[:], mr[:],
                     AT_ps[:, 0].rearrange("p a b -> p (a b)"),
                     AT_ps[:, 1].rearrange("p a b -> p (a b)"), cst.eps5_col[:])
            ac_rstd = ctx.cnt["ac"]

            @ctx.V
            def _(v, i=i, par=par, ac_rstd=ac_rstd, pstore=prev_store[i]):
                ctx.w(v, "ac", ac_rstd)
                ctx.inc(v.tensor_scalar(u_sb[:], AT_ps[:].rearrange("p a b c -> p (a b c)"),
                                        mr[:, 0:1], mr[:, 2:3], ALU.subtract, ALU.mult), "dv")
                ctx.w(v, "dv")
                ctx.w(v, "ds", pstore)
                ctx.inc(v.tensor_add(xnew[:, i, :], u_sb[:], xt[:, par, i, :]), "dv")
            at_u_guard[0] = ctx.cnt["dv"] - 1
            prev_resid[par][i] = ctx.cnt["dv"]
            res_dv = ctx.cnt["dv"]

            @ctx.S
            def _(s, t=t, i=i, res_dv=res_dv):
                ctx.w(s, "dv", res_dv)
                ctx.inc(s.dma_start(out=xout[t * 128:(t + 1) * 128, :], in_=xnew[:, i, :]),
                        "ds", 16)
            prev_store[i] = ctx.cnt["ds"]

    return ps


def emit_l1_ffn(ctx, cst, xin, xout, W):
    nc = ctx.nc
    NG, GT, TPG, NF, NE = 8, 512, 4, 16, 8
    ps = ExitStack()
    rt_w = ps.enter_context(nc.sbuf_tensor("m_rt", [128, KO, 8], BF16))
    w13_sb = ps.enter_context(nc.sbuf_tensor("m_w13", [128, 4, KO, 512], BF16))
    w2_sb = ps.enter_context(nc.sbuf_tensor("m_w2", [128, 2, NF, 1024], BF16))
    h_sb = ps.enter_context(nc.sbuf_tensor("m_h", [128, NF, GT], BF16))
    s1_sb = ps.enter_context(nc.sbuf_tensor("m_s1", [128, GT], BF16))
    xt = ps.enter_context(nc.sbuf_tensor("m_xt", [128, 2, 1024], F32))
    xo = ps.enter_context(nc.sbuf_tensor("m_xo", [128, 1, 1024], F32))
    xnew = ps.enter_context(nc.sbuf_tensor("m_xnew", [128, 1, 1024], F32))
    xn_tm = ps.enter_context(nc.sbuf_tensor("m_xntm", [128, 2, TPG, 1024], BF16))
    xn_fm = ps.enter_context(nc.sbuf_tensor("m_xnfm", [128, 2, KO, GT], BF16))
    acc = ps.enter_context(nc.sbuf_tensor("m_acc", [128, 2, TPG, 1024], F32))
    bn12 = ps.enter_context(nc.sbuf_tensor("m_bn12", [128, 2, 12], F32))
    mr = ps.enter_context(nc.sbuf_tensor("m_mr", [128, 2, 4], F32))
    sc_sb = ps.enter_context(nc.sbuf_tensor("m_sc", [128, 8], F32))
    mx8 = ps.enter_context(nc.sbuf_tensor("m_mx8", [128, 8], F32))
    ex8 = ps.enter_context(nc.sbuf_tensor("m_ex8", [128, 8], F32))
    rt1 = ps.enter_context(nc.sbuf_tensor("m_rt1", [128, 8], F32))
    rt2 = ps.enter_context(nc.sbuf_tensor("m_rt2", [128, 8], F32))
    wcol = ps.enter_context(nc.sbuf_tensor("m_wcol", [128, 4], F32))
    wE_sb = ps.enter_context(nc.sbuf_tensor("m_wE", [128, 2, TPG, 8], F32))
    Z = ps.enter_context(nc.psum_tensor("m_Z", [128, 2, 512], F32))
    Y = ps.enter_context(nc.psum_tensor("m_Y", [128, 2, 2, 512], F32))
    B_ps = ps.enter_context(nc.psum_tensor("m_B", [128, 8, 128], BF16))
    R_ps = ps.enter_context(nc.psum_tensor("m_R", [128, 512], F32))

    @ctx.G
    def _(g):
        for ko in range(KO):
            g.dma_start(out=rt_w[:, ko, :],
                        in_=W["w_router"][ko * 128:(ko + 1) * 128, :]).then_inc(ctx.sem["dg"], 16)
            ctx.cnt["dg"] += 16
    rt_dg = ctx.cnt["dg"]

    st = {
        "xt": [0, 0], "xo": [0], "store": [0], "trB": 0,
        "w13free": [0, 0, 0, 0], "w13da": [(0, 0), (0, 0), (0, 0), (0, 0)],
        "w2free": [0, 0], "zfree": (0, 0), "yfree": [0, 0],
        "xnfm": [0, 0], "wE": [0, 0], "accinit": [0, 0],
        "accfree": [0, 0], "xnfree": [0, 0], "rfree": 0,
    }

    NPAIR = NF // 2
    NEX = NE + 1

    def w13_block(e):
        if e < 0:
            return (W["ws1b"].rearrange("(e p) c -> e (p c)", p=1024),
                    W["ws3b"].rearrange("(e p) c -> e (p c)", p=1024), 0)
        return (W["w1b"].rearrange("(e p) c -> e (p c)", p=1024),
                W["w3b"].rearrange("(e p) c -> e (p c)", p=1024), e)

    def emit_w13_load(gp):
        ex = (gp // NPAIR) % NEX - 1
        np_ = gp % NPAIR
        nslot = gp % 4
        b1a, b3a, eidx = w13_block(ex)

        @ctx.G
        def _(s, b1a=b1a, eidx=eidx, np_=np_, nslot=nslot,
              used=st["w13free"][nslot]):
            if used:
                s.wait_ge(ctx.sem["pe"], used)
            s.dma_start(
                out=w13_sb[:, nslot, :, 0:256],
                in_=b1a[eidx:eidx + 1, np_ * 262144:(np_ + 1) * 262144].rearrange(
                    "a (p k f) -> (a p) k f", p=128, k=8, f=256)).then_inc(ctx.sem["da"], 16)
            ctx.cnt["da"] += 16

        @ctx.S
        def _(s, b3a=b3a, eidx=eidx, np_=np_, nslot=nslot,
              used=st["w13free"][nslot]):
            if used:
                s.wait_ge(ctx.sem["pe"], used)
            s.dma_start(
                out=w13_sb[:, nslot, :, 256:512],
                in_=b3a[eidx:eidx + 1, np_ * 262144:(np_ + 1) * 262144].rearrange(
                    "a (p k f) -> (a p) k f", p=128, k=8, f=256)).then_inc(ctx.sem["dl"], 16)
            ctx.cnt["dl"] += 16
        st["w13da"][nslot] = (ctx.cnt["da"], ctx.cnt["dl"])

    @ctx.G
    def _(g):
        pass
    for gp0 in range(3):
        emit_w13_load(gp0)

    def emit_AB(g):
        par = g % 2
        for tt in range(TPG):
            t = g * TPG + tt
            sl = tt % 2

            @ctx.S
            def _(s, t=t, sl=sl, pa=st["xt"][sl]):
                ctx.w(s, "dv", pa)
                ctx.inc(s.dma_start(out=xt[:, sl, :], in_=xin[t * 128:(t + 1) * 128, :]), "dl", 16)
            dl_here = ctx.cnt["dl"]

            @ctx.V
            def _(v, dl_here=dl_here):
                ctx.w(v, "dl", dl_here)
            ln_stats(ctx, bn12[:, sl, :], mr[:, sl, :], xt[:, sl, 0:512],
                     xt[:, sl, 512:1024], cst.eps5_col[:])
            ac_rstd = ctx.cnt["ac"]

            @ctx.V
            def _(v, par=par, tt=tt, sl=sl, ac_rstd=ac_rstd, xf=st["xnfree"][par]):
                ctx.w(v, "ac", ac_rstd)
                ctx.w(v, "dv", xf)
                ctx.inc(v.tensor_scalar(xn_tm[:, par, tt, :], xt[:, sl, :], mr[:, sl, 0:1],
                                        mr[:, sl, 2:3], ALU.subtract, ALU.mult), "dv")
            st["xt"][sl] = ctx.cnt["dv"]
            apply_dv = ctx.cnt["dv"]

            @ctx.TE
            def _(te, par=par, tt=tt, apply_dv=apply_dv, pt=st["trB"]):
                ctx.w(te, "dv", apply_dv)
                ctx.w(te, "dv", pt)
                ctx.w(te, "po", cst.ready)
                for ko in range(KO):
                    ins = te.transpose(B_ps[:, ko, :], xn_tm[:, par, tt, ko * 128:(ko + 1) * 128],
                                       cst.ident_bf[:])
                ctx.inc(ins, "pe")
            tr_pe = ctx.cnt["pe"]

            @ctx.V
            def _(v, par=par, tt=tt, tr_pe=tr_pe):
                ctx.w(v, "pe", tr_pe)
                ctx.inc(v.tensor_copy(xn_fm[:, par, :, tt * 128:(tt + 1) * 128], B_ps[:]), "dv")
            st["trB"] = ctx.cnt["dv"]
        st["xnfm"][par] = ctx.cnt["dv"]

        for tt in range(TPG):
            @ctx.TE
            def _(te, par=par, tt=tt, xnfm_dv=st["xnfm"][par], rfree=st["rfree"]):
                ctx.w(te, "dv", xnfm_dv)
                ctx.w(te, "dg", rt_dg)
                ctx.w(te, "dv", rfree)
                for ko in range(KO):
                    ins = te.matmul(R_ps[:, 0:8], xn_fm[:, par, ko, tt * 128:(tt + 1) * 128],
                                    rt_w[:, ko, :], start=(ko == 0), stop=(ko == KO - 1))
                ctx.inc(ins, "pe")
            sc_pe = ctx.cnt["pe"]

            @ctx.V
            def _(v, sc_pe=sc_pe):
                ctx.w(v, "pe", sc_pe)
                ctx.inc(v.tensor_copy(sc_sb[:], R_ps[:, 0:8]), "dv")
                st["rfree"] = max(st["rfree"], ctx.cnt["dv"])
                ctx.w(v, "dv")
                ctx.inc(v.max(mx8[:], sc_sb[:]), "dv")
                ctx.w(v, "dv")
                ctx.inc(v.tensor_scalar_mul(wcol[:, 0:1], mx8[:, 0:1], -1.0), "dv")
            nm_dv = ctx.cnt["dv"]

            @ctx.A
            def _(a, nm_dv=nm_dv):
                ctx.w(a, "dv", nm_dv)
                a.activation(ex8[:], sc_sb[:], AF.Exp, bias=wcol[:, 0:1])
                ctx.inc(a.activation(wcol[:, 3:4], mx8[:, 1:2], AF.Exp, bias=wcol[:, 0:1]), "ac")
            ex_ac = ctx.cnt["ac"]

            @ctx.V
            def _(v, par=par, tt=tt, ex_ac=ex_ac):
                ctx.w(v, "ac", ex_ac)
                ctx.inc(v.reduce_sum(wcol[:, 1:2], ex8[:], axis=AX.X), "dv")
                ctx.w(v, "dv")
                ctx.inc(v.reciprocal(wcol[:, 1:2], wcol[:, 1:2]), "dv")
                ctx.w(v, "dv")
                ctx.inc(v.tensor_scalar_mul(wcol[:, 2:3], wcol[:, 1:2], 2.0 / 3.0), "dv")
                ctx.w(v, "dv")
                v.tensor_single_scalar(rt1[:], sc_sb[:], mx8[:, 0:1], ALU.is_equal)
                ctx.inc(v.tensor_single_scalar(rt2[:], sc_sb[:], mx8[:, 1:2], ALU.is_equal), "dv")
                ctx.w(v, "dv")
                v.tensor_scalar_mul(rt1[:], rt1[:], wcol[:, 2:3])
                ctx.inc(v.tensor_scalar(rt2[:], rt2[:], wcol[:, 2:3], wcol[:, 3:4],
                                        ALU.mult, ALU.mult), "dv")
                ctx.w(v, "dv")
                ctx.inc(v.tensor_add(wE_sb[:, par, tt, :], rt1[:], rt2[:]), "dv")
        st["wE"][par] = ctx.cnt["dv"]

    def emit_expert(g, e, ld_marks):
        par = g % 2
        eslot = (e + 1) % 2
        ld_sem, ld_dl = ld_marks
        gpair = (g * NEX + (e + 1)) * NPAIR

        ac0 = ctx.cnt["ac"]
        silu_marks = []
        zfree = st["zfree"]
        for f in range(NF):
            pair, sub = f // 2, f % 2
            slot = (gpair + pair) % 4
            if sub == 0:
                gp_next = gpair + pair + 3
                if gp_next < NG * NEX * NPAIR:
                    emit_w13_load(gp_next)

            z1free = (ac0 + f) if f >= 1 else zfree[0]
            z3free = st["zfree"][1] if f == 0 else mult_dv

            @ctx.TE
            def _(te, par=par, f=f, slot=slot, sub=sub, z1free=z1free, z3free=z3free,
                  w13f=st["w13da"][slot], xnfm_dv=st["xnfm"][par]):
                ctx.w(te, "da", w13f[0])
                ctx.w(te, "dl", w13f[1])
                ctx.w(te, "dv", xnfm_dv)
                if z1free:
                    te.wait_ge(ctx.sem["ac"], z1free)
                for ko in range(KO):
                    ins = te.matmul(Z[:, 0, :],
                                    w13_sb[:, slot, ko, sub * 128:(sub + 1) * 128],
                                    xn_fm[:, par, ko, :],
                                    start=(ko == 0), stop=(ko == KO - 1))
                ctx.inc(ins, "pe")
                if z3free:
                    ctx.w(te, "dv", z3free)
                for ko in range(KO):
                    ins = te.matmul(Z[:, 1, :],
                                    w13_sb[:, slot, ko, 256 + sub * 128:256 + (sub + 1) * 128],
                                    xn_fm[:, par, ko, :],
                                    start=(ko == 0), stop=(ko == KO - 1))
                ctx.inc(ins, "pe")
            z_pe = ctx.cnt["pe"]
            silu_marks.append((f, z_pe - 1, 0 if f == 0 else mult_dv))
            if sub == 1:
                st["w13free"][slot] = z_pe

            @ctx.V
            def _(v, f=f, z_pe=z_pe, sac=ac0 + f + 1):
                v.wait_ge(ctx.sem["ac"], sac)
                ctx.w(v, "pe", z_pe)
                ctx.inc(v.tensor_tensor(h_sb[:, f, :], Z[:, 1, :], s1_sb[:], ALU.mult), "dv")
            mult_dv = ctx.cnt["dv"]

        @ctx.A
        def _(a, marks=tuple(silu_marks), s1f=st["zfree"][1]):
            for f, z1pe, mprev in marks:
                a.wait_ge(ctx.sem["pe"], z1pe)
                mwait = mprev if f > 0 else s1f
                if mwait:
                    a.wait_ge(ctx.sem["dv"], mwait)
                ctx.inc(a.activation(s1_sb[:], Z[:, 0, :], AF.Silu), "ac")
        assert ctx.cnt["ac"] == ac0 + NF
        st["zfree"] = (ac0 + NF, mult_dv)
        h_dv = mult_dv

        for tt in range(TPG):
            slot = tt % 2

            @ctx.TE
            def _(te, par=par, tt=tt, slot=slot, eslot=eslot, h_dv=h_dv,
                  ld_sem=ld_sem, ld_dl=ld_dl,
                  yfree=st["yfree"][slot], accinit=st["accinit"][par]):
                ctx.w(te, "dv", h_dv)
                ctx.w(te, ld_sem, ld_dl)
                ctx.w(te, "dv", yfree)
                for n in range(2):
                    for f in range(NF):
                        ins = te.matmul(Y[:, slot, n, :], h_sb[:, f, tt * 128:(tt + 1) * 128],
                                        w2_sb[:, eslot, f, n * 512:(n + 1) * 512],
                                        start=(f == 0), stop=(f == NF - 1))
                ctx.inc(ins, "pe")
            y_pe = ctx.cnt["pe"]

            @ctx.V
            def _(v, par=par, tt=tt, e=e, slot=slot, y_pe=y_pe,
                  accinit=st["accinit"][par], accfree=st["accfree"][par]):
                ctx.w(v, "pe", y_pe)
                yflat = Y[:, slot].rearrange("p a b -> p (a b)")
                if e < 0:
                    ctx.w(v, "dv", accfree)
                    ctx.inc(v.tensor_scalar_mul(acc[:, par, tt, :], yflat, 1.0 / 3.0), "dv")
                else:
                    ctx.w(v, "dv", accinit)
                    ctx.inc(v.scalar_tensor_tensor(acc[:, par, tt, :], yflat,
                                                   wE_sb[:, par, tt, e:e + 1], acc[:, par, tt, :],
                                                   ALU.mult, ALU.add), "dv")
            st["yfree"][slot] = ctx.cnt["dv"]
        if e < 0:
            st["accinit"][par] = ctx.cnt["dv"]
        st["w2free"][eslot] = ctx.cnt["pe"]

    def emit_eload(e):
        eslot = (e + 1) % 2
        sec, sem = (ctx.S, "dl") if eslot == 0 else (ctx.G, "dg")

        @sec
        def _(s, e=e, eslot=eslot, sem=sem, w2f=st["w2free"][eslot]):
            if w2f:
                s.wait_ge(ctx.sem["pe"], w2f)
            for fr in range(NF):
                src = W["ws2b"][fr * 128:(fr + 1) * 128, :] if e < 0 else \
                    W["w2b"][e * 2048 + fr * 128:e * 2048 + (fr + 1) * 128, :]
                s.dma_start(out=w2_sb[:, eslot, fr, :], in_=src).then_inc(ctx.sem[sem], 16)
                ctx.cnt[sem] += 16
        return (sem, ctx.cnt[sem])

    def emit_E(g):
        par = g % 2
        for tt in range(TPG):
            t = g * TPG + tt
            sl = tt % 2

            @ctx.S
            def _(s, t=t, po=st["xo"][0]):
                ctx.w(s, "dv", po)
                ctx.inc(s.dma_start(out=xo[:, 0, :], in_=xin[t * 128:(t + 1) * 128, :]), "dl", 16)
            xo_dl = ctx.cnt["dl"]

            @ctx.V
            def _(v, xo_dl=xo_dl):
                ctx.w(v, "dl", xo_dl)
                ctx.w(v, "dv")
            ln_stats(ctx, bn12[:, sl, :], mr[:, sl, :], acc[:, par, tt, 0:512],
                     acc[:, par, tt, 512:1024], cst.eps5_col[:])
            ac_rstd = ctx.cnt["ac"]

            @ctx.V
            def _(v, par=par, tt=tt, sl=sl, ac_rstd=ac_rstd, pstore=st["store"][0]):
                ctx.w(v, "ac", ac_rstd)
                ctx.inc(v.tensor_scalar(acc[:, par, tt, :], acc[:, par, tt, :], mr[:, sl, 0:1],
                                        mr[:, sl, 2:3], ALU.subtract, ALU.mult), "dv")
                ctx.w(v, "dv")
                ctx.w(v, "ds", pstore)
                ctx.inc(v.tensor_add(xnew[:, 0, :], acc[:, par, tt, :], xo[:, 0, :]), "dv")
            st["xo"][0] = ctx.cnt["dv"]
            res_dv = ctx.cnt["dv"]

            @ctx.S
            def _(s, t=t, res_dv=res_dv):
                ctx.w(s, "dv", res_dv)
                ctx.inc(s.dma_start(out=xout[t * 128:(t + 1) * 128, :], in_=xnew[:, 0, :]), "ds", 16)
            st["store"][0] = ctx.cnt["ds"]
        st["accfree"][par] = ctx.cnt["dv"]
        st["xnfree"][par] = ctx.cnt["dv"]

    emit_AB(0)
    ld_marks = emit_eload(-1)
    for g in range(NG):
        for e in range(-1, NE):
            cur_marks = ld_marks
            emit_expert(g, e, cur_marks)
            if e == -1 and g > 0:
                emit_E(g - 1)
            if e + 1 < NE:
                ld_marks = emit_eload(e + 1)
            elif g + 1 < NG:
                ld_marks = emit_eload(-1)
            if e == 3 and g + 1 < NG:
                emit_AB(g + 1)
    emit_E(NG - 1)

    return ps


def finish(ctx):
    @ctx.S
    def _(s):
        ctx.w(s, "ds")


def build_l0_attn_program():
    nc = bass.Bass("TRN2", target_bir_lowering=False, debug=False)
    xin = nc.dram_tensor("x_in", [T, D], F32, kind="ExternalInput").ap()
    xout = nc.dram_tensor("x_out", [T, D], F32, kind="ExternalOutput").ap()
    W = {
        "wq": nc.dram_tensor("wq", [1024, 1024], F32, kind="ExternalInput").ap(),
        "wk_down": nc.dram_tensor("wk_down", [1024, 64], F32, kind="ExternalInput").ap(),
        "wv_down": nc.dram_tensor("wv_down", [1024, 64], F32, kind="ExternalInput").ap(),
        "wk_up": nc.dram_tensor("wk_up", [32, 128], F32, kind="ExternalInput").ap(),
        "wv_up": nc.dram_tensor("wv_up", [32, 128], F32, kind="ExternalInput").ap(),
        "wo": nc.dram_tensor("wo", [1024, 1024], F32, kind="ExternalInput").ap(),
    }
    with ExitStack() as es:
        ctx = Ctx(nc, es)
        cst = Consts(ctx)
        ph = emit_l0_attn(ctx, cst, xin, xout, W)
        finish(ctx)
        ph.close()
    return nc


def build_l0_ffn_program():
    nc = bass.Bass("TRN2", target_bir_lowering=False, debug=False)
    xin = nc.dram_tensor("x_in", [T, D], F32, kind="ExternalInput").ap()
    xout = nc.dram_tensor("x_out", [T, D], F32, kind="ExternalOutput").ap()
    W = {
        "gate_w": nc.dram_tensor("gate_w", [1024, 8], F32, kind="ExternalInput").ap(),
        "w1": nc.dram_tensor("w1", [8, 1024, 2048], F32, kind="ExternalInput").ap(),
        "b1": nc.dram_tensor("b1", [8, 2048], F32, kind="ExternalInput").ap(),
        "w2": nc.dram_tensor("w2", [8, 2048, 1024], F32, kind="ExternalInput").ap(),
        "b2": nc.dram_tensor("b2", [8, 1024], F32, kind="ExternalInput").ap(),
        "ln_g": nc.dram_tensor("ln_g", [8, 1024], F32, kind="ExternalInput").ap(),
        "ln_b": nc.dram_tensor("ln_b", [8, 1024], F32, kind="ExternalInput").ap(),
    }
    W["w1b"] = nc.dram_tensor("w1b", [8 * 1024, 2048], BF16, kind="Internal").ap()
    W["w2b"] = nc.dram_tensor("w2b", [8 * 2048, 1024], BF16, kind="Internal").ap()
    with ExitStack() as es:
        ctx = Ctx(nc, es)
        cst = Consts(ctx)
        pc, _pce, _pct = emit_precast(ctx, [(W["w1"].rearrange("e k f -> (e k) f"), W["w1b"], "pairtile"),
                                (W["w2"].rearrange("e k f -> (e k) f"), W["w2b"], "flat")])
        _pce(_pct)
        emit_barrier(ctx)
        pc.close()
        ph = emit_l0_ffn(ctx, cst, xin, xout, W)
        finish(ctx)
        ph.close()
    return nc


def build_l1_attn_program():
    nc = bass.Bass("TRN2", target_bir_lowering=False, debug=False)
    xin = nc.dram_tensor("x_in", [T, D], F32, kind="ExternalInput").ap()
    xout = nc.dram_tensor("x_out", [T, D], F32, kind="ExternalOutput").ap()
    W = {}
    for nm in ("Wq", "Wk", "Wv", "Wg", "Wo"):
        W[nm] = nc.dram_tensor(nm, [1024, 1024], F32, kind="ExternalInput").ap()
    W["Wgt"] = nc.dram_tensor("Wgt", [1024, 16], F32, kind="ExternalInput").ap()
    with ExitStack() as es:
        ctx = Ctx(nc, es)
        cst = Consts(ctx)
        ph = emit_l1_attn(ctx, cst, xin, xout, W)
        finish(ctx)
        ph.close()
    return nc


def build_l1_ffn_program():
    nc = bass.Bass("TRN2", target_bir_lowering=False, debug=False)
    xin = nc.dram_tensor("x_in", [T, D], F32, kind="ExternalInput").ap()
    xout = nc.dram_tensor("x_out", [T, D], F32, kind="ExternalOutput").ap()
    W = {
        "w_router": nc.dram_tensor("w_router", [1024, 8], F32, kind="ExternalInput").ap(),
        "w1": nc.dram_tensor("w1", [8, 1024, 2048], F32, kind="ExternalInput").ap(),
        "w3": nc.dram_tensor("w3", [8, 1024, 2048], F32, kind="ExternalInput").ap(),
        "w2": nc.dram_tensor("w2", [8, 2048, 1024], F32, kind="ExternalInput").ap(),
        "ws1": nc.dram_tensor("ws1", [1024, 2048], F32, kind="ExternalInput").ap(),
        "ws3": nc.dram_tensor("ws3", [1024, 2048], F32, kind="ExternalInput").ap(),
        "ws2": nc.dram_tensor("ws2", [2048, 1024], F32, kind="ExternalInput").ap(),
    }
    W["w1b"] = nc.dram_tensor("m_w1b", [8 * 1024, 2048], BF16, kind="Internal").ap()
    W["w3b"] = nc.dram_tensor("m_w3b", [8 * 1024, 2048], BF16, kind="Internal").ap()
    W["w2b"] = nc.dram_tensor("m_w2b", [8 * 2048, 1024], BF16, kind="Internal").ap()
    W["ws1b"] = nc.dram_tensor("m_ws1b", [1024, 2048], BF16, kind="Internal").ap()
    W["ws3b"] = nc.dram_tensor("m_ws3b", [1024, 2048], BF16, kind="Internal").ap()
    W["ws2b"] = nc.dram_tensor("m_ws2b", [2048, 1024], BF16, kind="Internal").ap()
    with ExitStack() as es:
        ctx = Ctx(nc, es)
        cst = Consts(ctx)
        pc, _pce, _pct = emit_precast(ctx, [
            (W["w1"].rearrange("e k f -> (e k) f"), W["w1b"], "pairtile"),
            (W["w3"].rearrange("e k f -> (e k) f"), W["w3b"], "pairtile"),
            (W["w2"].rearrange("e k f -> (e k) f"), W["w2b"], "flat"),
            (W["ws1"], W["ws1b"], "pairtile"),
            (W["ws3"], W["ws3b"], "pairtile"),
            (W["ws2"], W["ws2b"], "flat")])
        _pce(_pct)
        emit_barrier(ctx)
        pc.close()
        ph = emit_l1_ffn(ctx, cst, xin, xout, W)
        finish(ctx)
        ph.close()
    return nc


def emit_barrier(ctx):
    marks = [(n, ctx.cnt[n]) for n in ("dl", "dg", "ds", "da", "pe", "dv", "ac", "po")
             if ctx.cnt[n] > 0]
    for sec in (ctx.S, ctx.G, ctx.TE, ctx.V, ctx.A):
        @sec
        def _(e, marks=marks):
            for n, v in marks:
                e.wait_ge(ctx.sem[n], v)


def build_full_program():
    nc = bass.Bass("TRN2", target_bir_lowering=False, debug=False)
    x_in = nc.dram_tensor("L0_attn_x", [T, D], F32, kind="ExternalInput").ap()
    x_out = nc.dram_tensor("x_out", [T, D], F32, kind="ExternalOutput").ap()
    xs1 = nc.dram_tensor("xs1", [T, D], F32, kind="Internal").ap()
    xs2 = nc.dram_tensor("xs2", [T, D], F32, kind="Internal").ap()

    def din(name, shape):
        return nc.dram_tensor(name, shape, F32, kind="ExternalInput").ap()

    W0a = {"wq": din("L0_attn_wq", [1024, 1024]),
           "wk_down": din("L0_attn_wk_down", [1024, 64]),
           "wv_down": din("L0_attn_wv_down", [1024, 64]),
           "wk_up": din("L0_attn_wk_up", [32, 128]),
           "wv_up": din("L0_attn_wv_up", [32, 128]),
           "wo": din("L0_attn_wo", [1024, 1024])}
    W0f = {"gate_w": din("L0_ffn_gate_w", [1024, 8]),
           "w1": din("L0_ffn_w1", [8, 1024, 2048]),
           "b1": din("L0_ffn_b1", [8, 2048]),
           "w2": din("L0_ffn_w2", [8, 2048, 1024]),
           "b2": din("L0_ffn_b2", [8, 1024]),
           "ln_g": din("L0_ffn_ln_g", [8, 1024]),
           "ln_b": din("L0_ffn_ln_b", [8, 1024])}
    W1a = {nm: din("L1_attn_" + nm, [1024, 1024]) for nm in ("Wq", "Wk", "Wv", "Wg", "Wo")}
    W1a["Wgt"] = din("L1_attn_Wgt", [1024, 16])
    W1f = {"w_router": din("L1_ffn_w_router", [1024, 8]),
           "w1": din("L1_ffn_w1", [8, 1024, 2048]),
           "w3": din("L1_ffn_w3", [8, 1024, 2048]),
           "w2": din("L1_ffn_w2", [8, 2048, 1024]),
           "ws1": din("L1_ffn_ws1", [1024, 2048]),
           "ws3": din("L1_ffn_ws3", [1024, 2048]),
           "ws2": din("L1_ffn_ws2", [2048, 1024])}

    W0f["w1b"] = nc.dram_tensor("f0_w1b", [8 * 1024, 2048], BF16, kind="Internal").ap()
    W0f["w2b"] = nc.dram_tensor("f0_w2b", [8 * 2048, 1024], BF16, kind="Internal").ap()
    W1f["w1b"] = nc.dram_tensor("m_w1b", [8 * 1024, 2048], BF16, kind="Internal").ap()
    W1f["w3b"] = nc.dram_tensor("m_w3b", [8 * 1024, 2048], BF16, kind="Internal").ap()
    W1f["w2b"] = nc.dram_tensor("m_w2b", [8 * 2048, 1024], BF16, kind="Internal").ap()
    W1f["ws1b"] = nc.dram_tensor("m_ws1b", [1024, 2048], BF16, kind="Internal").ap()
    W1f["ws3b"] = nc.dram_tensor("m_ws3b", [1024, 2048], BF16, kind="Internal").ap()
    W1f["ws2b"] = nc.dram_tensor("m_ws2b", [2048, 1024], BF16, kind="Internal").ap()
    with ExitStack() as es:
        ctx = Ctx(nc, es)
        cst = Consts(ctx)
        pc, pc_emit, pc_total = emit_precast(ctx, [
            (W0f["w1"].rearrange("e k f -> (e k) f"), W0f["w1b"], "pairtile"),
            (W0f["w2"].rearrange("e k f -> (e k) f"), W0f["w2b"], "flat"),
            (W1f["w1"].rearrange("e k f -> (e k) f"), W1f["w1b"], "pairtile"),
            (W1f["w3"].rearrange("e k f -> (e k) f"), W1f["w3b"], "pairtile"),
            (W1f["w2"].rearrange("e k f -> (e k) f"), W1f["w2b"], "flat"),
            (W1f["ws1"], W1f["ws1b"], "pairtile"),
            (W1f["ws3"], W1f["ws3b"], "pairtile"),
            (W1f["ws2"], W1f["ws2b"], "flat")])
        per_win = -(-pc_total // (NW - 1))
        ph = emit_l0_attn(ctx, cst, x_in, xs1, W0a,
                          between=lambda c: pc_emit(per_win))
        pc_emit(pc_total)
        pc_ds = ctx.cnt["ds"]
        emit_barrier(ctx)
        ph.close()
        pc.close()
        for sec in (ctx.S, ctx.G, ctx.A):
            @sec
            def _(e, pc_ds=pc_ds):
                e.wait_ge(ctx.sem["ds"], pc_ds)
        ph = emit_l0_ffn(ctx, cst, xs1, xs2, W0f)
        emit_barrier(ctx)
        ph.close()
        ph = emit_l1_attn(ctx, cst, xs2, xs1, W1a)
        emit_barrier(ctx)
        ph.close()
        ph = emit_l1_ffn(ctx, cst, xs1, x_out, W1f)
        finish(ctx)
        ph.close()
    return nc


_PHASE_SPECS = [
    ("l0_attn", build_l0_attn_program,
     {"wq": "L0_attn_wq", "wk_down": "L0_attn_wk_down", "wv_down": "L0_attn_wv_down",
      "wk_up": "L0_attn_wk_up", "wv_up": "L0_attn_wv_up", "wo": "L0_attn_wo"}),
    ("l0_ffn", build_l0_ffn_program,
     {"gate_w": "L0_ffn_gate_w", "w1": "L0_ffn_w1", "b1": "L0_ffn_b1",
      "w2": "L0_ffn_w2", "b2": "L0_ffn_b2", "ln_g": "L0_ffn_ln_g", "ln_b": "L0_ffn_ln_b"}),
    ("l1_attn", build_l1_attn_program,
     {"Wq": "L1_attn_Wq", "Wk": "L1_attn_Wk", "Wv": "L1_attn_Wv",
      "Wg": "L1_attn_Wg", "Wgt": "L1_attn_Wgt", "Wo": "L1_attn_Wo"}),
    ("l1_ffn", build_l1_ffn_program,
     {"w_router": "L1_ffn_w_router", "w1": "L1_ffn_w1", "w3": "L1_ffn_w3",
      "w2": "L1_ffn_w2", "ws1": "L1_ffn_ws1", "ws3": "L1_ffn_ws3", "ws2": "L1_ffn_ws2"}),
]


def kernel(**inputs):
    x = np.ascontiguousarray(np.asarray(inputs["L0_attn_x"], dtype=np.float32))
    B = x.shape[0]
    nc = build_full_program()
    wvals = {k: np.ascontiguousarray(np.asarray(v, dtype=np.float32))
             for k, v in inputs.items() if k != "L0_attn_x"}
    maps = []
    for c in range(B):
        m = {"L0_attn_x": np.ascontiguousarray(x[c])}
        m.update(wvals)
        maps.append(m)
    res = run_bass_kernel_spmd(nc, maps, core_ids=list(range(B)))
    return np.stack([r["x_out"] for r in res.results])
